# version 16
# baseline (speedup 1.0000x reference)
import numpy as np
from contextlib import ExitStack

import concourse.bass as bass
import concourse.mybir as mybir
from concourse.bass_utils import run_bass_kernel_spmd

F32 = mybir.dt.float32
BF16 = mybir.dt.bfloat16
F16 = mybir.dt.float16
I16 = mybir.dt.int16
U16 = mybir.dt.uint16
I32 = mybir.dt.int32
U8 = mybir.dt.uint8
A = mybir.AluOpType
AF = mybir.ActivationFunctionType

D = 2048
T = 2048
HALO = 3
TH = T + HALO
DIN = 4096
NH = 64
HD = 64
DS = 128
CONVD = 4352
ZC = 4096
DTC = 64
KD = 16
NT = 16
NTE = 17
CH = 128
NCH = 16
GRP = 512
NGRP = 4
CPG = 4
MXBC = 34
KDIN = 32
I5 = 5632
KI = 44
MGU = 88
NMO = 16
ZBLK = 1024
NZB = 4
LN_EPS = 1e-5
RMS_EPS = 1.1920929e-07
FFN_EPS = 1e-6
GU_SH = 1408
GU_MT = 11
DN_SH = 256
DN_MT = 2
WO_MT = 2


class Sem:
    def __init__(self, h):
        self.h = h
        self.n = 0


class Prog:
    ENGS = ("sync", "tensor", "vector", "scalar", "gpsimd")

    def __init__(self):
        self.q = {e: [] for e in self.ENGS}

    def dma(self, eng, sem, out, in_, **kw):
        if sem.n >= 16:
            self.q[eng].append(lambda e, s=sem, vv=sem.n: e.wait_ge(s.h, vv))
        sem.n += 16
        self.q[eng].append(lambda e, o=out, i=in_, s=sem, kw=kw: e.dma_start(out=o, in_=i, **kw).then_inc(s.h, 16))
        return sem.n

    def op(self, eng, fn, inc=None, drain=True):
        if inc is not None:
            inc.n += 1
            g = lambda e, f=fn, s=inc: f(e).then_inc(s.h, 1)
            g._has_inc = True
            g._is_op = True
            self.q[eng].append(g)
        else:
            fn._is_op = True
            self.q[eng].append(fn)
        if drain and eng in ("vector", "scalar", "gpsimd"):
            d = lambda e: e.drain()
            d._is_drain = True
            self.q[eng].append(d)
        return inc.n if inc is not None else None

    def wait(self, eng, sem, val=None):
        v = sem.n if val is None else val
        if v > 0:
            self.q[eng].append(lambda e, s=sem, vv=v: e.wait_ge(s.h, vv))
        return v

    def inc(self, eng, sem, n=1):
        sem.n += n
        last = self.q[eng][-1] if self.q[eng] else None
        if (last is not None and getattr(last, "_is_op", False)
                and not getattr(last, "_has_inc", False)):
            prev = self.q[eng].pop()
            fn = lambda e, p=prev, s=sem, nn=n: p(e).then_inc(s.h, nn)
            fn._has_inc = True
            fn._is_op = True
            self.q[eng].append(fn)
        else:
            if eng in ("vector", "scalar", "gpsimd") and last is not None \
                    and not getattr(last, "_is_drain", False):
                d = lambda e: e.drain()
                d._is_drain = True
                self.q[eng].append(d)
            self.q[eng].append(lambda e, s=sem, nn=n: e.nop().then_inc(s.h, nn))
        return sem.n


def _view(pool, off, shape, dtype):
    esz = mybir.dt.size(dtype)
    n = int(np.prod(shape)) * esz
    assert off % 4 == 0
    ap = pool.ap()[:, off:off + n].bitcast(dtype)
    if len(shape) == 1:
        return ap
    spec = " ".join(f"a{i}" for i in range(len(shape)))
    kw = {f"a{i}": int(shape[i]) for i in range(len(shape))}
    return ap.rearrange(f"p ({spec}) -> p {spec}", **kw)


def build(debug=0, stop=99, xbc_m=36, dis=()):
    nc = bass.Bass("TRN2", target_bir_lowering=False)
    P = Prog()
    es = ExitStack()

    def dr(name, shape, dt, kind=None):
        if kind:
            return nc.dram_tensor(name, shape, dt, kind=kind).ap()
        return nc.dram_tensor(name, shape, dt).ap()

    u_fm = dr("u_fm", [D, TH], F32, "ExternalInput")
    u_tm = dr("u_tm", [TH, D], F32, "ExternalInput")
    w_z = dr("w_z", [KD, 128, ZC + DTC], F32, "ExternalInput")
    w_xbc = dr("w_xbc", [MXBC, KD, 128, 128], F32, "ExternalInput")
    w_out_sh = dr("w_out_sh", [WO_MT, KDIN, 128, 128], F32, "ExternalInput")
    gu_w16 = dr("gu_w16", [KD, 32, GU_SH], I16, "ExternalInput")
    gu_scl = dr("gu_scl", [KD, GU_SH], F16, "ExternalInput")
    dn_w16 = dr("dn_w16", [KI, 32, DN_SH], I16, "ExternalInput")
    dn_scl = dr("dn_scl", [KI, DN_SH], F16, "ExternalInput")
    conv_w_i = dr("conv_w", [128, MXBC, 4], F32, "ExternalInput")
    conv_b_i = dr("conv_b", [128, MXBC], F32, "ExternalInput")
    a_log_i = dr("a_log", [NH, 1], F32, "ExternalInput")
    dt_bias_i = dr("dt_bias", [NH, 1], F32, "ExternalInput")
    d_col_i = dr("d_col_i", [NH, 1], F32, "ExternalInput")
    nw_rep_i = dr("nw_rep_i", [128, DIN], F32, "ExternalInput")
    d_row_i = dr("d_row_i", [1, NH], F32, "ExternalInput")
    is_b_i = dr("is_b", [1, 1], F32, "ExternalInput")
    out_t = dr("out", [T, D], F32, "ExternalOutput")

    dbg = "ExternalOutput" if debug else None
    z_dram = dr("z_dram", [T, ZC], BF16, dbg)
    x_dram = dr("x_dram", [NCH, CH, DIN], BF16, dbg)
    st_dram = dr("st_dram", [NCH, DS, DIN], BF16, dbg)
    sin_dram = dr("sin_dram", [NCH, DS, DIN], BF16, dbg)
    stats_dram2 = dr("stats_dram2", [2 * NTE, 128], BF16)
    vecs_dram = dr("vecs_dram", [2, NCH, NH], F32)
    acs_dram = dr("acs_dram", [NCH, NH, CH], F32)
    acs2_dram = dr("acs2_dram", [NCH, CH, NH], F32)
    sclb_dram = dr("sclb_dram", [KD + KI, GU_SH], BF16)
    stateag_in = dr("stateag_in", [DS, DIN], F32)
    stateag_out = dr("stateag_out", [2, DS, DIN], F32)
    gu_ag_in = dr("gu_ag_in", [GU_MT, 128, KD, 128], BF16)
    gu_ag_out = dr("gu_ag_out", [8, GU_MT, 128, KD, 128], BF16)
    dn_ag_in = dr("dn_ag_in", [DN_MT, 128, KI, 128], BF16)
    dn_ag_out = dr("dn_ag_out", [8, DN_MT, 128, KI, 128], BF16)
    wo_ag_in = dr("wo_ag_in", [WO_MT, 128, KDIN, 128], BF16)
    wo_ag_out = dr("wo_ag_out", [8, WO_MT, 128, KDIN, 128], BF16)
    if debug:
        mix_dram = dr("mix_dram", [D, T], F32, "ExternalOutput")
        dbg_dram = dr("dbg_dram", [6, 128, NCH * NH], F32, "ExternalOutput")

    sb = lambda n, s, d: es.enter_context(nc.sbuf_tensor(n, s, d))
    poolA = sb("poolA", [128, 65664], U8)
    poolB = sb("poolB", [128, 33536], U8)
    poolC = sb("poolC", [128, 26624], U8)
    poolD = sb("poolD", [128, 25600], U8)
    poolF = sb("poolF", [128, 17536], U8)

    Bfm = sb("Bfm", [128, T], BF16)
    Cfm = sb("Cfm", [128, T], BF16)
    ds_lh = sb("ds_lh", [128, NCH, NH], BF16)
    e0_lh = sb("e0_lh", [128, NCH, NH], BF16)
    dt_lh = sb("dt_lh", [128, NCH, NH], BF16)
    D_rep = sb("D_rep", [128, NH], F32)
    acs_lc = sb("acs_lc", [128, NH], F32)
    pg_rep = sb("pg_rep", [128, NCH, NH], BF16)
    S_ext = sb("S_ext", [DS, DIN], BF16)
    nw_rep = sb("nw_rep", [128, DIN], BF16)
    cw = sb("cw", [128, MXBC, 4], F32)
    cb = sb("cb", [128, MXBC], F32)
    ident = sb("ident", [128, 128], BF16)
    ident32 = sb("ident32", [128, 128], F32)
    triu = sb("triu", [128, 128], BF16)
    ones_col = sb("ones_col", [128, 1], BF16)
    ones_row32 = sb("ones_row32", [1, 128], F32)
    A_col = sb("A_col", [NH, 1], F32)
    dtb_col = sb("dtb_col", [NH, 1], F32)
    d_col = sb("d_col", [NH, 1], F32)
    isb_col = sb("isb_col", [128, 1], F32)
    amt16 = sb("amt16", [128, 1], I16)
    amt32 = sb("amt32", [128, 1], I32)
    cs_h = sb("cs_h", [NH, NCH], F32)
    pgl_h = sb("pgl_h", [NH, NCH], F32)
    acs63 = sb("acs63", [NH, 1], F32)
    csT = sb("csT", [NCH, NH], F32)
    pgT = sb("pgT", [NCH, NH], F32)
    sttT = sb("sttT", [2 * NTE, 128], BF16)
    stat_s = sb("stat_s", [128, NTE, 1], F32)
    stat_q = sb("stat_q", [128, NTE, 1], F32)
    stat_t = sb("stat_t", [128, NTE, 1], F32)
    stat_sb = sb("stat_sb", [128, NTE, 2], BF16)
    rms_c = sb("rms_c", [128, 1], F32)
    rms8 = sb("rms8", [128, 8, 1], F32)
    gsc_f = _view(poolF, 0, [GU_SH], F16)[0:KD, :]
    gsc_b = _view(poolF, 2816, [GU_SH], BF16)[0:KD, :]
    dsc_f = _view(poolF, 5632, [DN_SH], F16)[0:KI, :]
    dsc_b = _view(poolF, 6144, [DN_SH], BF16)[0:KI, :]

    ps = lambda n, s, d: es.enter_context(nc.psum_tensor(n, s, d))
    Btm = _view(poolB, 18944, [NCH, 128], BF16)
    psA = ps("psA", [128, 512], F32)
    psB = ps("psB", [128, 512], F32)
    psT = ps("psT", [128, 1024], BF16)
    psT2 = ps("psT2", [128, 1024], BF16)
    psG = ps("psG", [128, 512], F32)
    psY0 = ps("psY0", [128, 512], F32)
    psY1 = ps("psY1", [128, 512], F32)
    psSt = ps("psSt", [128, 512], F32)

    mk = lambda n: Sem(es.enter_context(nc.semaphore(n)))
    syW = mk("syW"); syA = mk("syA"); scD = mk("scD"); gD = mk("gD"); dqD = mk("dqD")
    tS = mk("tS"); vS = mk("vS"); sS = mk("sS"); gS = mk("gS"); ccS = mk("ccS")
    vF = mk("vF"); sF = mk("sF"); tF = mk("tF"); gF = mk("gF")

    blk = es.enter_context(nc.Block())

    def _finish():
        P.wait("scalar", scD, scD.n)
        P.wait("gpsimd", gD, gD.n)
        P.wait("sync", syA, syA.n)
        P.wait("sync", syW, syW.n)
        P.wait("vector", vS, 0)

        @blk.sync
        def _(e):
            for f in P.q["sync"]:
                f(e)

        @blk.tensor
        def _(e):
            for f in P.q["tensor"]:
                f(e)

        @blk.vector
        def _(e):
            for f in P.q["vector"]:
                f(e)

        @blk.scalar
        def _(e):
            for f in P.q["scalar"]:
                f(e)

        @blk.gpsimd
        def _(e):
            for f in P.q["gpsimd"]:
                f(e)

        es.close()
        return nc

    P.op("gpsimd", lambda e: e.memset(ident[:], 0.0))
    P.op("gpsimd", lambda e: e.affine_select(out=ident[:], in_=ident[:], compare_op=A.not_equal,
                                             fill=1.0, base=0, pattern=[[-1, 128]], channel_multiplier=1))
    P.op("gpsimd", lambda e: e.memset(ident32[:], 0.0))
    P.op("gpsimd", lambda e: e.affine_select(out=ident32[:], in_=ident32[:], compare_op=A.not_equal,
                                             fill=1.0, base=0, pattern=[[-1, 128]], channel_multiplier=1))
    P.op("gpsimd", lambda e: e.memset(triu[:], 0.0))
    P.op("gpsimd", lambda e: e.affine_select(out=triu[:], in_=triu[:], compare_op=A.is_gt,
                                             fill=1.0, base=0, pattern=[[-1, 128]], channel_multiplier=1))
    P.op("gpsimd", lambda e: e.memset(ones_col[:], 1.0))
    P.op("gpsimd", lambda e: e.memset(ones_row32[:], 1.0))
    P.op("gpsimd", lambda e: e.memset(stat_s[:], 0.0))
    P.op("gpsimd", lambda e: e.memset(stat_q[:], 0.0))
    P.op("gpsimd", lambda e: e.iota(amt32[:], pattern=[[0, 1]], base=0, channel_multiplier=1))
    CONST_G = P.inc("gpsimd", gS)
    P.wait("vector", gS, CONST_G)
    P.op("vector", lambda e: e.tensor_scalar(amt32[:], amt32[:], 3, None, A.bitwise_and))
    P.op("vector", lambda e: e.tensor_scalar(amt32[:], amt32[:], 4, None, A.mult))
    P.op("vector", lambda e: e.tensor_copy(amt16[:], amt32[:]))
    AMT_V = P.inc("vector", vS)

    P.dma("sync", syA, cw[:], conv_w_i)
    P.dma("sync", syA, cb[:], conv_b_i)
    P.dma("sync", syA, A_col[:], a_log_i)
    P.dma("sync", syA, dtb_col[:], dt_bias_i)
    P.dma("sync", syA, d_col[:], d_col_i)
    P.dma("sync", syA, isb_col[:], is_b_i.broadcast_to([128, 1]))
    P.dma("sync", syA, D_rep[:], d_row_i.broadcast_to([128, NH]))
    P.dma("sync", syA, gsc_f, gu_scl)
    P.dma("sync", syA, dsc_f, dn_scl)
    TINY_LD = syA.n
    nwst = _view(poolD, 0, [2, ZBLK], F32)
    P.wait("sync", syA, TINY_LD)
    nwmark = [0, 0]
    for i in range(4):
        if i >= 2:
            P.wait("sync", vF, nwmark[i % 2])
        P.dma("sync", syA, nwst[:, i % 2, :], nw_rep_i[:, i * ZBLK:(i + 1) * ZBLK])
        P.wait("vector", syA)
        P.op("vector", lambda e, i=i: e.tensor_copy(nw_rep[:, i * ZBLK:(i + 1) * ZBLK], nwst[:, i % 2, :]))
        nwmark[i % 2] = P.inc("vector", vF)
    P.wait("scalar", syA, TINY_LD)
    P.op("scalar", lambda e: e.activation(A_col[:], A_col[:], AF.Exp))
    P.op("scalar", lambda e: e.mul(A_col[:], A_col[:], -1.0))
    A_S = P.inc("scalar", sS)
    P.wait("vector", syA, TINY_LD)
    P.op("vector", lambda e: e.tensor_scalar(gsc_b, gsc_f, 0.25, None, A.mult))
    P.op("vector", lambda e: e.tensor_scalar(dsc_b, dsc_f, 0.25, None, A.mult))
    SCL_V = P.inc("vector", vS)
    P.wait("gpsimd", vS, SCL_V)
    P.dma("gpsimd", gD, sclb_dram[0:KD, :], gsc_b)
    P.dma("gpsimd", gD, sclb_dram[KD:KD + KI, 0:DN_SH], dsc_b)
    P.wait("gpsimd", gD, gD.n)


    dq_wrep = _view(poolF, 0, [2, 704], I16)
    dq_srep = _view(poolF, 2816, [2, 704], BF16)
    DQH = 704
    dq_a = _view(poolF, 5632, [DQH], I16)
    dq_b = _view(poolF, 5632 + 1408, [DQH], I16)
    dq_c = _view(poolF, 5632 + 2816, [DQH], I16)
    dq_d = _view(poolF, 5632 + 4224, [DQH], I16)
    dq_v = _view(poolF, 5632 + 5632, [DQH], BF16)
    dq_o = _view(poolF, 5632 + 7040, [GU_SH], BF16)
    dq_state = {"mark": [0, 0], "si": 0, "first": True, "sclb_gate": 0}

    def dq_steps():
        amt_b = amt16.ap().broadcast_to([128, DQH])

        def decode(sl, wd, ow):
            if dq_state["first"]:
                P.wait("vector", gD, dq_state["sclb_gate"])
                dq_state["first"] = False
            P.op("vector", lambda e: e.tensor_tensor(
                dq_a[:, 0:wd], dq_wrep[:, sl, 0:wd], amt_b[:, 0:wd], A.logical_shift_right))
            P.op("vector", lambda e: e.tensor_scalar(dq_a[:, 0:wd], dq_a[:, 0:wd], 15, None, A.bitwise_and))
            P.op("vector", lambda e: e.tensor_scalar(dq_b[:, 0:wd], dq_a[:, 0:wd], 7, None, A.bitwise_and))
            P.op("vector", lambda e: e.tensor_scalar(dq_c[:, 0:wd], dq_b[:, 0:wd], 1, None, A.logical_shift_right))
            P.op("vector", lambda e: e.tensor_scalar(dq_d[:, 0:wd], dq_b[:, 0:wd], 1, None, A.bitwise_and))
            P.op("vector", lambda e: e.tensor_scalar(dq_d[:, 0:wd], dq_d[:, 0:wd], 2, None, A.add))
            P.op("vector", lambda e: e.tensor_tensor(dq_d[:, 0:wd], dq_d[:, 0:wd], dq_c[:, 0:wd], A.logical_shift_left))
            P.op("vector", lambda e: e.tensor_scalar(dq_c[:, 0:wd], dq_b[:, 0:wd], -1, 2, A.mult, A.add))
            P.op("vector", lambda e: e.tensor_scalar(dq_c[:, 0:wd], dq_c[:, 0:wd], 0, None, A.max))
            P.op("vector", lambda e: e.tensor_tensor(dq_d[:, 0:wd], dq_d[:, 0:wd], dq_c[:, 0:wd], A.subtract))
            P.op("vector", lambda e: e.tensor_scalar(dq_c[:, 0:wd], dq_a[:, 0:wd], 8, None, A.bitwise_and))
            P.op("vector", lambda e: e.tensor_scalar(dq_c[:, 0:wd], dq_c[:, 0:wd], 2, None, A.logical_shift_right))
            P.op("vector", lambda e: e.tensor_scalar(dq_c[:, 0:wd], dq_c[:, 0:wd], -1, 1, A.mult, A.add))
            P.op("vector", lambda e: e.tensor_tensor(dq_d[:, 0:wd], dq_d[:, 0:wd], dq_c[:, 0:wd], A.mult))
            P.op("vector", lambda e: e.tensor_copy(dq_v[:, 0:wd], dq_d[:, 0:wd]))
            P.op("vector", lambda e: e.tensor_tensor(ow, dq_v[:, 0:wd], dq_srep[:, sl, 0:wd], A.mult))
            dq_state["mark"][sl] = P.inc("vector", vS)

        for k in range(KD):
            def step(k=k):
                for hh in range(2):
                    sl = dq_state["si"] % 2
                    dq_state["si"] += 1
                    c0 = hh * DQH
                    P.wait("sync", vS, dq_state["mark"][sl])
                    P.dma("sync", dqD, dq_wrep[:, sl, :],
                          gu_w16[k][:, c0:c0 + DQH].unsqueeze(1).broadcast_to([32, 4, DQH]))
                    P.dma("sync", dqD, dq_srep[:, sl, :],
                          sclb_dram[k:k + 1, c0:c0 + DQH].broadcast_to([128, DQH]))
                    P.wait("vector", dqD, dqD.n)
                    decode(sl, DQH, dq_o[:, c0:c0 + DQH])
                P.wait("sync", vS, dq_state["mark"][(dq_state["si"] - 1) % 2])
                for j in range(GU_SH // 128):
                    P.dma("sync", dqD, gu_ag_in[j, :, k, :], dq_o[:, j * 128:(j + 1) * 128])
            yield step
        for k in range(0, KI, 2):
            def step(k=k):
                sl = dq_state["si"] % 2
                dq_state["si"] += 1
                P.wait("sync", vS, dq_state["mark"][sl])
                for q in range(2):
                    P.dma("sync", dqD, dq_wrep[:, sl, q * DN_SH:(q + 1) * DN_SH],
                          dn_w16[k + q].unsqueeze(1).broadcast_to([32, 4, DN_SH]))
                    P.dma("sync", dqD, dq_srep[:, sl, q * DN_SH:(q + 1) * DN_SH],
                          sclb_dram[KD + k + q:KD + k + q + 1, 0:DN_SH].broadcast_to([128, DN_SH]))
                P.wait("vector", dqD, dqD.n)
                decode(sl, 2 * DN_SH, dq_o[:, 0:2 * DN_SH])
                P.wait("sync", vS, dq_state["mark"][sl])
                for q in range(2):
                    for j in range(DN_SH // 128):
                        P.dma("sync", dqD, dn_ag_in[j, :, k + q, :],
                              dq_o[:, q * DN_SH + j * 128: q * DN_SH + (j + 1) * 128])
            yield step
        for j in range(WO_MT):
            def step2(j=j):
                for kq in range(KDIN // 8):
                    P.dma("gpsimd", gD, wo_ag_in[j, :, kq * 8:(kq + 1) * 8, :],
                          w_out_sh[j, kq * 8:(kq + 1) * 8].transpose([1, 0, 2]))
            yield step2

    dq_state["sclb_gate"] = gD.n
    _dq_iter = dq_steps()

    def dq_pump(n=1):
        for _ in range(n):
            s = next(_dq_iter, None)
            if s is not None:
                s()

    def dq_finish():
        while True:
            s = next(_dq_iter, None)
            if s is None:
                return

    if stop <= 0:
        return _finish()
    ubar = _view(poolA, 0, [KD, TH], BF16)
    u_stage = _view(poolD, 0, [2, TH], F32)
    m_bc = _view(poolD, 16408, [NTE * 128], BF16)
    r_bc = _view(poolD, 16408 + 2 * NTE * 128, [NTE * 128], BF16)
    sqd = _view(poolB, 0, [D], BF16)

    P.wait("sync", vF, vF.n)
    P.wait("gpsimd", vF, vF.n)
    P.wait("scalar", gS, CONST_G)
    smark = [0, 0]
    for j in range(NTE):
        rows = 128 if j < NT else HALO
        pb = j % 2
        eng, sem = ("sync", syA) if pb == 0 else ("gpsimd", gD)
        if j >= 2:
            P.wait(eng, sF, smark[pb])
        P.dma(eng, sem, u_stage[0:rows, pb, 0:D], u_tm[j * 128:j * 128 + rows, :])
        P.wait("scalar", sem)
        P.op("scalar", lambda e, j=j, rows=rows, pb=pb: e.activation(
            sqd[0:rows, :], u_stage[0:rows, pb, 0:D], AF.Copy, accum_out=stat_s[0:rows, j, :]))
        P.op("scalar", lambda e, j=j, rows=rows, pb=pb: e.activation(
            sqd[0:rows, :], u_stage[0:rows, pb, 0:D], AF.Square, accum_out=stat_q[0:rows, j, :]))
        smark[pb] = P.inc("scalar", sF)
    STATS_S = P.inc("scalar", sS)
    P.wait("vector", sS, STATS_S)
    P.op("vector", lambda e: e.tensor_scalar(stat_s[:], stat_s[:], 1.0 / D, None, A.mult))
    P.op("vector", lambda e: e.tensor_scalar(stat_q[:], stat_q[:], 1.0 / D, None, A.mult))
    P.op("vector", lambda e: e.tensor_tensor(stat_t[:], stat_s[:], stat_s[:], A.mult))
    P.op("vector", lambda e: e.tensor_tensor(stat_q[:], stat_q[:], stat_t[:], A.subtract))
    P.op("vector", lambda e: e.tensor_scalar(stat_q[:], stat_q[:], 1.0, LN_EPS, A.mult, A.add))
    vm = P.inc("vector", vS)
    P.wait("scalar", vS, vm)
    P.op("scalar", lambda e: e.activation(stat_q[:], stat_q[:], AF.Sqrt))
    sm = P.inc("scalar", sS)
    P.wait("vector", sS, sm)
    P.op("vector", lambda e: e.reciprocal(stat_q[:], stat_q[:]))
    P.op("vector", lambda e: e.tensor_copy(stat_sb[:, :, 0:1], stat_s[:]))
    P.op("vector", lambda e: e.tensor_copy(stat_sb[:, :, 1:2], stat_q[:]))
    STB_V = P.inc("vector", vS)
    P.wait("tensor", vS, STB_V)
    P.op("tensor", lambda e: e.transpose(psT[0:2 * NTE, 0:128],
                                         stat_sb.ap().rearrange("p a b -> p (a b)"), ident[:]), drain=False)
    tm = P.inc("tensor", tS)
    P.wait("vector", tS, tm)
    P.op("vector", lambda e: e.tensor_copy(sttT[:], psT[0:2 * NTE, 0:128]))
    vm = P.inc("vector", vS)
    P.wait("gpsimd", vS, vm)
    P.dma("gpsimd", gD, stats_dram2.ap() if False else stats_dram2, sttT[:])
    P.wait("gpsimd", gD, gD.n)
    msrc = stats_dram2[0:2 * NTE:2, :].unsqueeze(0).broadcast_to([128, NTE, 128])
    rsrc = stats_dram2[1:2 * NTE:2, :].unsqueeze(0).broadcast_to([128, NTE, 128])
    P.dma("gpsimd", gD, m_bc.rearrange("p (a b) -> p a b", a=NTE), msrc)
    P.dma("gpsimd", gD, r_bc.rearrange("p (a b) -> p a b", a=NTE), rsrc)
    STATS_G = gD.n

    vmark = [0, 0]
    P.wait("sync", sF, sF.n)
    P.wait("gpsimd", sF, sF.n)
    for k in range(KD):
        pb = k % 2
        eng, sem = ("sync", syA) if pb == 0 else ("gpsimd", gD)
        if k >= 2:
            P.wait(eng, vF, vmark[pb])
        P.dma(eng, sem, u_stage[:, pb, :], u_fm[k * 128:(k + 1) * 128, :])
        P.wait("vector", sem)
        if k == 0:
            P.wait("vector", gD, STATS_G)
        P.op("vector", lambda e, k=k, pb=pb: e.tensor_tensor(ubar[:, k, :], u_stage[:, pb, :], m_bc[:, 0:TH], A.subtract))
        P.op("vector", lambda e, k=k: e.tensor_tensor(ubar[:, k, :], ubar[:, k, :], r_bc[:, 0:TH], A.mult), drain=False)
        vmark[pb] = P.inc("vector", vF)
    UBAR_V = P.inc("vector", vS)

    if stop <= 1:
        return _finish()
    wzb = _view(poolB, 0, [KD, 2, 512], BF16)
    zev = _view(poolC, 8192, [2, 512], BF16)
    dtraw = _view(poolC, 8192 + 2048, [NCH, NH], F32)

    P.wait("tensor", vS, UBAR_V)
    bank_mark = {0: 0, 1: 0}
    bank_markv = {0: 0, 1: 0}
    wz_sl_tmark = [0, 0]
    zev_mark = [0, 0]
    NCB = ZC // 512 + 1
    for zcb in range(NCB):
        width = 512 if zcb < NCB - 1 else DTC
        sl = zcb % 2
        P.wait("gpsimd", tS, wz_sl_tmark[sl])
        for k in range(KD):
            P.dma("gpsimd", gD, wzb[:, k, sl, 0:width], w_z[k, :, zcb * 512: zcb * 512 + width])
        WZG = gD.n
        dq_pump(2)
        P.wait("tensor", gD, WZG)
        for j in range(NT):
            bi = j % 2
            bank = psA if bi == 0 else psB
            P.wait("tensor", sF, bank_mark[bi])
            P.wait("tensor", vF, bank_markv[bi])
            for k in range(KD):
                P.op("tensor", lambda e, j=j, k=k, sl=sl, width=width, bank=bank: e.matmul(
                    bank[:, 0:width], ubar[:, k, HALO + j * 128: HALO + (j + 1) * 128],
                    wzb[:, k, sl, 0:width], start=(k == 0), stop=(k == KD - 1)), drain=False)
            tm = P.inc("tensor", tS)
            if zcb < NCB - 1:
                pe = j % 2
                P.wait("scalar", tS, tm)
                P.wait("scalar", scD, zev_mark[pe])
                P.op("scalar", lambda e, pe=pe, bank=bank: e.activation(
                    zev[:, pe, 0:512], bank[:, 0:512], AF.Silu))
                bank_mark[bi] = P.inc("scalar", sF)
                P.dma("scalar", scD, z_dram[j * 128:(j + 1) * 128, zcb * 512: zcb * 512 + 512],
                      zev[:, pe, 0:512])
                zev_mark[pe] = scD.n
            else:
                P.wait("vector", tS, tm)
                P.op("vector", lambda e, j=j, bank=bank: e.tensor_copy(dtraw[:, j, :], bank[:, 0:DTC]), drain=False)
                bank_markv[bi] = P.inc("vector", vF)
        wz_sl_tmark[sl] = tS.n
    DT_V = P.inc("vector", vS)

    if stop <= 2:
        return _finish()
    wk_a = _view(poolC, 8192 + 2048 + 4096, [CH], F32)[0:NH, :]
    wk_b = _view(poolC, 8192 + 2048 + 4096 + 512, [CH], F32)[0:NH, :]
    wk_c = _view(poolC, 8192 + 2048 + 4096 + 1024, [CH], F32)[0:NH, :]
    wk_d = _view(poolC, 8192 + 2048 + 4096 + 1536, [CH], F32)[0:NH, :]
    wk_e = _view(poolC, 8192 + 2048 + 4096 + 2048, [CH], F32)[0:NH, :]

    P.wait("tensor", vS, DT_V)
    P.wait("scalar", sS, A_S)
    ps3_mark = {"G": 0, "Y0": 0, "Y1": 0}
    acs2_mark = 0
    acs1_mark = 0
    for c in range(NCH):
        dq_pump(1)
        P.wait("vector", gD, acs1_mark)
        P.wait("scalar", gD, acs1_mark)
        P.wait("tensor", vF, ps3_mark["G"])
        P.op("tensor", lambda e, c=c: e.transpose(psG[0:NH, 0:128], dtraw[:, c, :], ident32[:]), drain=False)
        tm = P.inc("tensor", tS)
        P.wait("vector", tS, tm)
        P.op("vector", lambda e: e.tensor_scalar(wk_a[:], psG[0:NH, 0:128], 1.0, dtb_col[:], A.mult, A.add))
        ps3_mark["G"] = P.inc("vector", vF)
        vm = P.inc("vector", vS)
        P.wait("scalar", vS, vm)
        P.op("scalar", lambda e: e.activation(wk_b[:], wk_a[:], AF.Abs))
        P.op("scalar", lambda e: e.activation(wk_b[:], wk_b[:], AF.Exp, scale=-1.0))
        P.op("scalar", lambda e: e.activation(wk_b[:], wk_b[:], AF.Ln, bias=1.0))
        P.op("scalar", lambda e: e.activation(wk_c[:], wk_a[:], AF.Relu))
        sm = P.inc("scalar", sS)
        P.wait("vector", sS, sm)
        P.op("vector", lambda e: e.tensor_add(wk_b[:], wk_b[:], wk_c[:]))
        P.op("vector", lambda e: e.tensor_scalar(wk_a[:], wk_b[:], A_col[:], None, A.mult))
        P.op("vector", lambda e: e.tensor_tensor_scan(wk_c[:], wk_a[:], wk_a[:], 0.0, A.add, A.bypass))
        P.op("vector", lambda e, c=c: e.tensor_copy(cs_h[:, c:c + 1], wk_c[:, 127:128]))
        P.op("vector", lambda e: e.tensor_copy(acs63[:], wk_c[:, 63:64]))
        P.op("vector", lambda e: e.tensor_scalar(wk_d[:], wk_c[:], -1.0, wk_c[:, 127:128], A.mult, A.add))
        vm = P.inc("vector", vS)
        P.wait("scalar", vS, vm)
        P.op("scalar", lambda e: e.activation(wk_d[:], wk_d[:], AF.Exp))
        sm = P.inc("scalar", sS)
        P.wait("vector", sS, sm)
        P.op("vector", lambda e: e.tensor_tensor(wk_d[:], wk_d[:], wk_b[:], A.mult))
        TDS_V = P.inc("vector", vS)
        P.wait("tensor", vS, TDS_V)
        P.wait("tensor", vF, ps3_mark["Y0"])
        P.op("tensor", lambda e: e.transpose(psY0[:, 0:64], wk_d[:], ident32[0:64, 0:64]), drain=False)
        P.op("tensor", lambda e: e.transpose(psY0[:, 64:128], wk_b[:], ident32[0:64, 0:64]), drain=False)
        tm = P.inc("tensor", tS)
        P.wait("vector", tS, tm)
        P.op("vector", lambda e, c=c: e.tensor_copy(ds_lh[:, c, :], psY0[:, 0:64]))
        P.op("vector", lambda e, c=c: e.tensor_copy(dt_lh[:, c, :], psY0[:, 64:128]))
        ps3_mark["Y0"] = P.inc("vector", vF)
        amark = P.inc("vector", vS)
        P.wait("gpsimd", vS, amark)
        acs1_mark = P.dma("gpsimd", gD, acs_dram[c], wk_c[:])
        P.op("vector", lambda e: e.tensor_copy(wk_e[:], wk_c[:]))
        vm = P.inc("vector", vS)
        P.wait("scalar", vS, vm)
        P.op("scalar", lambda e: e.activation(wk_e[:], wk_e[:], AF.Exp))
        sm = P.inc("scalar", sS)
        P.wait("tensor", sS, sm)
        P.wait("tensor", vF, ps3_mark["Y1"])
        P.op("tensor", lambda e: e.transpose(psY1[:, 0:64], wk_e[:], ident32[0:64, 0:64]), drain=False)
        P.op("tensor", lambda e: e.transpose(psY1[:, 64:128], wk_c[:], ident32[0:64, 0:64]), drain=False)
        tm = P.inc("tensor", tS)
        P.wait("vector", tS, tm)
        P.op("vector", lambda e, c=c: e.tensor_copy(e0_lh[:, c, :], psY1[:, 0:64]))
        P.wait("vector", gD, acs2_mark)
        P.op("vector", lambda e: e.tensor_copy(acs_lc[:], psY1[:, 64:128]))
        ps3_mark["Y1"] = P.inc("vector", vF)
        vm = P.inc("vector", vS)
        P.wait("gpsimd", vS, vm)
        acs2_mark = P.dma("gpsimd", gD, acs2_dram[c], acs_lc[:])
    P.op("vector", lambda e: e.tensor_tensor_scan(pgl_h[:], cs_h[:], cs_h[:], 0.0, A.add, A.bypass))
    P.op("vector", lambda e: e.tensor_tensor(pgl_h[:], pgl_h[:], cs_h[:], A.subtract))
    vm = P.inc("vector", vS)
    P.wait("scalar", vS, vm)
    P.op("scalar", lambda e: e.activation(cs_h[:], cs_h[:], AF.Exp))
    P.op("scalar", lambda e: e.activation(pgl_h[:], pgl_h[:], AF.Exp))
    sm = P.inc("scalar", sS)
    P.wait("tensor", sS, sm)
    P.wait("tensor", vF, ps3_mark["Y0"])
    P.op("tensor", lambda e: e.transpose(psY0[:, 0:64][0:16, :] if False else psY0[0:16, 0:64], cs_h[:], ident32[0:64, 0:64]), drain=False)
    P.op("tensor", lambda e: e.transpose(psY0[0:16, 64:128], pgl_h[:], ident32[0:64, 0:64]), drain=False)
    tm = P.inc("tensor", tS)
    P.wait("vector", tS, tm)
    P.op("vector", lambda e: e.tensor_copy(csT[:], psY0[0:16, 0:64]))
    P.op("vector", lambda e: e.tensor_copy(pgT[:], psY0[0:16, 64:128]))
    vm = P.inc("vector", vS)
    P.wait("gpsimd", vS, vm)
    P.dma("gpsimd", gD, vecs_dram[0], csT[:])
    P.dma("gpsimd", gD, vecs_dram[1], pgT[:])
    P.wait("gpsimd", gD, gD.n)
    P.dma("gpsimd", gD, pg_rep.ap().rearrange("p a b -> p (a b)"),
          vecs_dram[1].rearrange("a b -> (a b)").unsqueeze(0).broadcast_to([128, NCH * NH]))
    PG_G = gD.n
    DVEC_V = P.inc("vector", vS)

    if stop <= 3:
        return _finish()
    wxb = _view(poolC, 0, [2, KD, 128], BF16)
    xbcraw = _view(poolC, 8192, [2, TH], BF16)
    xacc = _view(poolC, 8192 + 8208, [2, TH], BF16)
    XsBUF = _view(poolB, 0, [NCH, 512], BF16)
    xtm_st = _view(poolB, NCH * 1024, [2, 128], BF16)
    st_st = _view(poolB, NCH * 1024 + 512, [2, 512], BF16)

    wxmark = [0, 0]
    xacc_mark = [0, 0]
    xbc_mark = [0, 0]
    psT_mark = [0, 0]
    psT_smark = [0, 0]
    xst_mark = [0, 0]
    stst_mark = [0, 0]
    psSt_mark = 0
    P.wait("sync", vF, vF.n)
    P.wait("scalar", scD, scD.n)
    P.wait("scalar", gD, gD.n)
    order = ([32, 33] + list(range(32)))[:xbc_m]
    pending = []
    xacc_tmark = [0, 0]
    wxb_tmark = [0, 0]
    for mi, m in enumerate(order):
        mb = mi % 2
        P.wait("gpsimd", tS, wxb_tmark[mb])
        WXG = P.dma("gpsimd", gD, wxb[:, mb, :, :], w_xbc[m].transpose([1, 0, 2]))
        P.wait("tensor", gD, WXG)
        if m < 8:
            dq_pump(1)
        ngs = [(0, 512), (512, 512), (1024, 512), (1536, 512), (2048, HALO)]
        for gi, (n0, nn) in enumerate(ngs):
            bi = gi % 2
            bank = psA if bi == 0 else psB
            P.wait("tensor", sF, bank_mark[bi])
            for k in range(KD):
                P.op("tensor", lambda e, k=k, mb=mb, n0=n0, nn=nn, bank=bank: e.matmul(
                    bank[:, 0:nn], wxb[:, mb, k, :], ubar[:, k, n0:n0 + nn],
                    start=(k == 0), stop=(k == KD - 1)), drain=False)
            tm = P.inc("tensor", tS)
            P.wait("scalar", tS, tm)
            if gi == 0:
                P.wait("scalar", vS, xacc_mark[mb])
            P.op("scalar", lambda e, mb=mb, n0=n0, nn=nn, bank=bank: e.copy(
                xbcraw[:, mb, n0:n0 + nn], bank[:, 0:nn]), drain=False)
            bank_mark[bi] = P.inc("scalar", sF)
        wxb_tmark[mb] = tS.n
        XB_S = P.inc("scalar", sS)
        P.wait("vector", sS, XB_S)
        P.wait("vector", tS, xacc_tmark[mb])
        cbb = cb.ap()[:, m:m + 1].broadcast_to([128, T])
        P.op("vector", lambda e, m=m, mb=mb, cbb=cbb: e.scalar_tensor_tensor(
            xacc[:, mb, HALO:TH], xbcraw[:, mb, 0:T], cw.ap()[:, m, 0:1], cbb, A.mult, A.add))
        for tap in range(1, 4):
            P.op("vector", lambda e, m=m, mb=mb, tap=tap: e.scalar_tensor_tensor(
                xacc[:, mb, HALO:TH], xbcraw[:, mb, tap:T + tap], cw.ap()[:, m, tap:tap + 1],
                xacc[:, mb, HALO:TH], A.mult, A.add))
        CV_V = P.inc("vector", vS)
        xacc_mark[mb] = CV_V
        P.wait("scalar", vS, CV_V)
        P.op("scalar", lambda e, mb=mb: e.activation(xacc[:, mb, HALO:TH], xacc[:, mb, HALO:TH], AF.Silu), drain=False)
        SIL_S = P.inc("scalar", sS)
        if m == 32:
            P.wait("vector", sS, SIL_S)
            P.op("vector", lambda e, mb=mb: e.tensor_copy(Bfm[:], xacc[:, mb, HALO:TH]))
            BM_V = P.inc("vector", vS)
            P.wait("tensor", vS, BM_V)
            for c in range(NCH):
                pt = c % 2
                P.wait("tensor", vS, psT_mark[pt])
                P.op("tensor", lambda e, c=c, pt=pt: e.transpose(
                    (psT if pt == 0 else psT2)[:, 0:128], Bfm[:, c * 128:(c + 1) * 128], ident[:]), drain=False)
                tm = P.inc("tensor", tS)
                P.wait("vector", tS, tm)
                P.op("vector", lambda e, c=c, pt=pt: e.tensor_copy(Btm[:, c, :], (psT if pt == 0 else psT2)[:, 0:128]), drain=False)
                psT_mark[pt] = P.inc("vector", vS)
        elif m == 33:
            P.wait("vector", sS, SIL_S)
            P.op("vector", lambda e, mb=mb: e.tensor_copy(Cfm[:], xacc[:, mb, HALO:TH]))
            P.inc("vector", vS)
        else:
            def make_post(m=m, mb=mb, SIL_S=SIL_S):
                def post():
                    nonlocal psSt_mark
                    blk_i = m % 4
                    P.wait("tensor", sS, SIL_S)
                    for c in range(NCH):
                        pt = c % 2
                        P.wait("tensor", vS, psT_mark[pt])
                        P.wait("tensor", sS, psT_smark[pt])
                        P.op("tensor", lambda e, c=c, mb=mb, pt=pt: e.transpose(
                            (psT if pt == 0 else psT2)[:, 0:128],
                            xacc[:, mb, HALO + c * 128: HALO + (c + 1) * 128], ident[:]), drain=False)
                        tm = P.inc("tensor", tS)
                        P.wait("vector", tS, tm)
                        dsap = ds_lh.ap()[:, c, 2 * m: 2 * m + 2].unsqueeze(2).broadcast_to([128, 2, HD])
                        P.op("vector", lambda e, c=c, blk_i=blk_i, pt=pt, dsap=dsap: e.tensor_tensor(
                            XsBUF[:, c, blk_i * 128:(blk_i + 1) * 128].rearrange("p (a b) -> p a b", a=2),
                            (psT if pt == 0 else psT2)[:, 0:128].rearrange("p (a b) -> p a b", a=2), dsap, A.mult), drain=False)
                        psT_mark[pt] = P.inc("vector", vS)
                        P.wait("scalar", tS, tm)
                        P.wait("scalar", vS, psT_mark[pt])
                        P.wait("scalar", gD, xst_mark[pt])
                        P.op("scalar", lambda e, pt=pt: e.copy(xtm_st[:, pt, :], (psT if pt == 0 else psT2)[:, 0:128]), drain=False)
                        psT_smark[pt] = P.inc("scalar", sS)
                        P.wait("gpsimd", sS, psT_smark[pt])
                        P.dma("gpsimd", gD, x_dram[c, :, m * 128:(m + 1) * 128], xtm_st[:, pt, :])
                        xst_mark[pt] = gD.n
                    XS_V = P.inc("vector", vS)
                    xacc_tmark[mb] = tS.n
                    if (m + 1) % 4 == 0:
                        blk0 = (m - 3) * 128
                        P.wait("tensor", vS, XS_V)
                        for c in range(NCH):
                            pt = c % 2
                            P.wait("tensor", vS, psSt_mark)
                            P.op("tensor", lambda e, c=c: e.matmul(
                                psSt[:, :], Btm[:, c, :], XsBUF[:, c, :], start=True, stop=True), drain=False)
                            tm = P.inc("tensor", tS)
                            P.wait("vector", tS, tm)
                            P.wait("vector", gD, stst_mark[pt])
                            P.op("vector", lambda e, pt=pt: e.tensor_copy(st_st[:, pt, :], psSt[:, :]), drain=False)
                            psSt_mark = P.inc("vector", vS)
                            P.wait("gpsimd", vS, psSt_mark)
                            P.dma("gpsimd", gD, st_dram[c, :, blk0:blk0 + 512], st_st[:, pt, :])
                            stst_mark[pt] = gD.n
                return post
            pending.append(make_post())
            if len(pending) > 1:
                pending.pop(0)()
    while pending:
        pending.pop(0)()
    XDONE_V = P.inc("vector", vS)
    if debug:
        P.wait("gpsimd", vS, XDONE_V)
        for i_, tt_ in enumerate([ds_lh, dt_lh, e0_lh]):
            P.dma("gpsimd", gD, dbg_dram[i_], tt_.ap().rearrange("p a b -> p (a b)"))
        P.dma("gpsimd", gD, dbg_dram[5], dtraw.rearrange("p a b -> p (a b)"))

    dq_finish()

    if stop <= 4:
        return _finish()
    S_run = _view(poolA, 0, [DIN], F32)
    st_ld = _view(poolA, DIN * 4, [2, DIN], BF16)
    sin_st = _view(poolA, DIN * 4 + DIN * 4, [2, DIN], BF16)
    etot_rep = _view(poolA, DIN * 4 + DIN * 8, [NCH, NH], BF16)
    etot_flat = _view(poolA, DIN * 4 + DIN * 8, [NCH * NH], BF16)

    P.wait("gpsimd", vS, XDONE_V)
    P.wait("gpsimd", gD, gD.n)
    P.dma("gpsimd", gD, etot_flat,
          vecs_dram[0].rearrange("a b -> (a b)").unsqueeze(0).broadcast_to([128, NCH * NH]))
    ETOT_G = gD.n
    stld_mark = [0, 0]
    P.wait("sync", gD, gD.n)
    P.wait("vector", vS, XDONE_V)
    P.wait("vector", gD, ETOT_G)
    P.op("vector", lambda e: e.memset(S_run, 0.0))
    srun_vmark = P.inc("vector", vF)
    sinst_mark = [0, 0]
    for c in range(NCH):
        pb = c % 2
        if c >= 2:
            P.wait("sync", vF, stld_mark[pb])
        P.dma("sync", syA, st_ld[:, pb, :], st_dram[c])
        P.wait("scalar", vF, srun_vmark)
        P.wait("scalar", scD, sinst_mark[pb])
        P.op("scalar", lambda e, pb=pb: e.copy(sin_st[:, pb, :], S_run))
        sm = P.inc("scalar", sS)
        sinst_mark[pb] = P.dma("scalar", scD, sin_dram[c], sin_st[:, pb, :])
        eap = etot_rep[:, c, :].unsqueeze(2).broadcast_to([128, NH, HD])
        P.wait("vector", sS, sm)
        P.op("vector", lambda e, eap=eap: e.tensor_tensor(
            S_run.rearrange("p (a b) -> p a b", a=NH), S_run.rearrange("p (a b) -> p a b", a=NH), eap, A.mult))
        P.wait("vector", syA)
        P.op("vector", lambda e, pb=pb: e.tensor_tensor(S_run, S_run, st_ld[:, pb, :], A.add))
        stld_mark[pb] = P.inc("vector", vF)
        srun_vmark = stld_mark[pb]
    SRUN_V = P.inc("vector", vS)
    P.wait("gpsimd", vS, SRUN_V)
    P.wait("gpsimd", gD, gD.n)
    P.dma("gpsimd", gD, stateag_in.opt(), S_run)
    P.wait("gpsimd", gD, gD.n)
    P.op("gpsimd", lambda e: e.collective_compute(
        "AllGather", A.bypass, replica_groups=[[0, 1], [2, 3], [4, 5], [6, 7]],
        ins=[stateag_in.opt()], outs=[stateag_out.opt()]), inc=ccS, drain=False)
    CC_ST = ccS.n
    P.wait("gpsimd", dqD, dqD.n)
    P.op("gpsimd", lambda e: e.collective_compute(
        "AllGather", A.bypass, replica_groups=[[0, 1, 2, 3, 4, 5, 6, 7]],
        ins=[gu_ag_in.opt()], outs=[gu_ag_out.opt()]), inc=ccS, drain=False)
    P.op("gpsimd", lambda e: e.collective_compute(
        "AllGather", A.bypass, replica_groups=[[0, 1, 2, 3, 4, 5, 6, 7]],
        ins=[dn_ag_in.opt()], outs=[dn_ag_out.opt()]), inc=ccS, drain=False)
    P.op("gpsimd", lambda e: e.collective_compute(
        "AllGather", A.bypass, replica_groups=[[0, 1, 2, 3, 4, 5, 6, 7]],
        ins=[wo_ag_in.opt()], outs=[wo_ag_out.opt()]), inc=ccS, drain=False)
    CC_W = ccS.n
    P.wait("gpsimd", ccS, CC_ST)
    P.dma("gpsimd", gD, S_ext[:], stateag_out[0])
    P.wait("vector", gD, gD.n)
    P.op("vector", lambda e: e.tensor_scalar(S_ext[:], S_ext[:], isb_col[:], None, A.mult))
    SEXT_V = P.inc("vector", vS)

    if stop <= 5:
        return _finish()
    h_fm = _view(poolA, 0, [KI, GRP], BF16)
    x2_fm = _view(poolA, KI * 1024, [KD, GRP], BF16)
    ygfm = _view(poolB, 0, [KDIN, GRP], BF16)
    y_ffn = _view(poolB, 0, [KD, GRP], BF16)
    zch = _view(poolC, 0, [DIN], BF16)
    xtmch = _view(poolC, DIN * 2, [DIN], BF16)
    outst = _view(poolC, DIN * 4, [2, 1024], F32)
    outst_dbg = _view(poolC, DIN * 4, [D], F32)
    sinch = _view(poolD, 0, [DIN], BF16)
    x1_fm = _view(poolD, 0, [KD, GRP], BF16)
    gext_t = _view(poolD, DIN * 2 * 2, [DIN], BF16)
    u_grp = _view(poolD, DIN * 2 * 2, [2, GRP], F32)
    y_tm = _view(poolF, 0, [DIN], BF16)
    xdblk = _view(poolF, 8192, [GRP], BF16)
    tB_f = _view(poolF, 8192 + 1024, [GRP], F32)
    gm = _view(poolF, 8192 + 1024 + 2048, [128], BF16)
    mh4 = _view(poolF, 8192 + 1024 + 2048 + 256, [2, GRP], BF16)
    ghb = _view(poolF, 8192 + 1024 + 2048 + 256 + 2048, [2, GRP], BF16)
    acs_sv = _view(poolF, 8192 + 1024 + 2048 + 256 + 4096, [NH], F32)
    xdt = _view(poolF, 8192 + 1024 + 2048 + 256 + 4096 + 256, [GRP], BF16)
    acs_b = _view(poolD, 8192, [2, GRP], F32)
    argb = _view(poolD, 12288, [2, GRP], F32)
    wt_s = [_view(poolF, 0, [22, 128], BF16),
            _view(poolF, 5632, [22, 128], BF16)]
    gsil = _view(poolF, KI * 256, [GRP], BF16)
    sq2 = _view(poolF, KI * 256 + 1024, [GRP], BF16)
    srow = _view(poolF, KI * 256 + 2048, [2, GRP], F32)

    wt_slot_mark = [0, 0]
    out_mark = {0: 0, 1: 0}
    sc_g_mark = 0
    P.wait("tensor", vS, SEXT_V)
    P.wait("sync", gD, gD.n)
    P.wait("sync", scD, scD.n)
    P.wait("gpsimd", vS, SEXT_V)
    zch_mark = 0
    xtm_mark = 0
    sin_mark = 0
    acs_s_mark = 0
    accb_mark = [0, 0]
    argb_mark = [0, 0]
    ghb_mark = [0, 0]
    psY_mark = 0
    ytm_mark = 0
    ug_mark = [0, 0]
    wt_mark = 0
    x1_mark = 0
    for g in range(NGRP):
        for ci in range(CPG):
            c = g * CPG + ci
            P.wait("sync", vS, xtm_mark)
            P.dma("sync", syA, xtmch, x_dram[c])
            P.wait("scalar", tS, sin_mark)
            P.wait("scalar", vS, x1_mark)
            P.dma("scalar", scD, sinch, sin_dram[c])
            SRELD = scD.n
            P.wait("scalar", vS, zch_mark)
            P.dma("scalar", scD, zch, z_dram[c * 128:(c + 1) * 128, :])
            ZRELD = scD.n
            P.wait("gpsimd", scD, SRELD)
            P.wait("gpsimd", vS, x1_mark)
            pgap = pg_rep.ap()[:, c, :].unsqueeze(2).broadcast_to([128, NH, HD])
            P.op("gpsimd", lambda e, pgap=pgap: e.tensor_tensor(
                gext_t.rearrange("p (a b) -> p a b", a=NH),
                S_ext.ap().rearrange("p (a b) -> p a b", a=NH), pgap, A.mult))
            P.op("gpsimd", lambda e: e.tensor_tensor(sinch, sinch, gext_t, A.add))
            SC_G = P.inc("gpsimd", gS)
            sc_g_mark = SC_G
            P.wait("tensor", gS, SC_G)
            P.wait("tensor", vF, bank_markv[0])
            P.op("tensor", lambda e, c=c: e.matmul(
                psG[:, 0:128], Bfm[:, c * 128:(c + 1) * 128], Cfm[:, c * 128:(c + 1) * 128],
                start=True, stop=True), drain=False)
            tm = P.inc("tensor", tS)
            P.wait("vector", tS, tm)
            P.op("vector", lambda e: e.tensor_tensor(gm, psG[:, 0:128], triu[:], A.mult), drain=False)
            bank_markv[0] = P.inc("vector", vF)
            GM_V = P.inc("vector", vS)
            P.wait("tensor", vS, GM_V)
            P.wait("sync", vS, acs_s_mark)
            P.dma("sync", syA, acs_sv, acs2_dram[c])
            ACSS = syA.n
            P.wait("vector", syA, ACSS)
            for bb in range(DIN // 512):
                h0 = bb * 8
                dDap = D_rep.ap()[:, h0:h0 + 8].unsqueeze(2).broadcast_to([128, 8, HD])
                dtap = dt_lh.ap()[:, c, h0:h0 + 8].unsqueeze(2).broadcast_to([128, 8, HD])
                xsl = xtmch[:, bb * 512:(bb + 1) * 512].rearrange("p (a b) -> p a b", a=8)
                P.op("vector", lambda e, dDap=dDap, xsl=xsl: e.tensor_tensor(
                    xdblk.rearrange("p (a b) -> p a b", a=8), xsl, dDap, A.mult), drain=False)
                P.op("vector", lambda e, dtap=dtap, xsl=xsl: e.tensor_tensor(
                    xdt.rearrange("p (a b) -> p a b", a=8), xsl, dtap, A.mult), drain=False)
                XY_V = P.inc("vector", vS)
                P.wait("tensor", vS, XY_V)
                P.wait("tensor", vS, psY_mark)
                P.op("tensor", lambda e, c=c, bb=bb: e.matmul(
                    psY1[:, :], Cfm[:, c * 128:(c + 1) * 128], sinch[:, bb * 512:(bb + 1) * 512],
                    start=True, stop=True), drain=False)
                P.op("tensor", lambda e: e.matmul(
                    psY0[:, :], ident[:, :], xdblk[:, :], start=True, stop=False), drain=False)
                P.inc("tensor", tS)
                for sub in range(2):
                    hb = 2 * bb + sub
                    slot = hb % 2
                    P.wait("gpsimd", vS, accb_mark[slot])
                    P.dma("gpsimd", gD, acs_b[:, slot, :],
                          acs_dram[c, 4 * hb:4 * hb + 4].rearrange("a b -> (a b)").unsqueeze(0)
                          .broadcast_to([128, GRP]))
                    gdn = gD.n
                    P.wait("vector", gD, gdn)
                    P.wait("vector", sS, argb_mark[slot])
                    a4 = acs_sv[:, 4 * hb:4 * hb + 4].unsqueeze(2).broadcast_to([128, 4, 128])
                    P.op("vector", lambda e, slot=slot, a4=a4: e.tensor_tensor(
                        argb[:, slot, :].rearrange("p (a b) -> p a b", a=4),
                        acs_b[:, slot, :].rearrange("p (a b) -> p a b", a=4), a4, A.subtract))
                    P.op("vector", lambda e, slot=slot: e.tensor_scalar(
                        argb[:, slot, :], argb[:, slot, :], 0.0, None, A.min), drain=False)
                    accb_mark[slot] = P.inc("vector", vF) * 0 + vS.n
                    vm = P.inc("vector", vS)
                    accb_mark[slot] = vm
                    P.wait("scalar", vS, vm)
                    P.op("scalar", lambda e, slot=slot: e.activation(mh4[:, slot, :], argb[:, slot, :], AF.Exp))
                    argb_mark[slot] = P.inc("scalar", sS)
                    P.wait("vector", sS, argb_mark[slot])
                    P.wait("vector", tS, ghb_mark[slot])
                    gmap = gm.unsqueeze(1).broadcast_to([128, 4, 128])
                    P.op("vector", lambda e, slot=slot, gmap=gmap: e.tensor_tensor(
                        ghb[:, slot, :].rearrange("p (a b) -> p a b", a=4),
                        mh4[:, slot, :].rearrange("p (a b) -> p a b", a=4), gmap, A.mult), drain=False)
                    ghv = P.inc("vector", vS)
                    P.wait("tensor", vS, ghv)
                    for hh in range(4):
                        col = (4 * sub + hh) * HD
                        last = (sub == 1 and hh == 3)
                        P.op("tensor", lambda e, slot=slot, hh=hh, col=col, last=last: e.matmul(
                            psY0[:, col:col + HD], ghb[:, slot, hh * 128:(hh + 1) * 128],
                            xdt[:, col:col + HD], start=False, stop=last), drain=False)
                    ghb_mark[slot] = P.inc("tensor", tS)
                tm = tS.n
                P.wait("vector", tS, tm)
                e0ap = e0_lh.ap()[:, c, h0:h0 + 8].unsqueeze(2).broadcast_to([128, 8, HD])
                P.op("vector", lambda e, e0ap=e0ap: e.tensor_tensor(
                    tB_f.rearrange("p (a b) -> p a b", a=8),
                    psY1.ap().rearrange("p (a b) -> p a b", a=8), e0ap, A.mult))
                P.op("vector", lambda e, bb=bb: e.tensor_tensor(
                    y_tm[:, bb * 512:(bb + 1) * 512], psY0.ap()[:, :], tB_f, A.add), drain=False)
                psY_mark = P.inc("vector", vS)
            YV = P.inc("vector", vS)
            xtm_mark = YV
            acs_s_mark = YV
            P.wait("scalar", vS, YV)
            for qq in range(DIN // GRP):
                P.op("scalar", lambda e, qq=qq: e.activation(
                    sq2, y_tm[:, qq * GRP:(qq + 1) * GRP], AF.Square, accum_out=rms8[:, qq, :]))
            SQ_S = P.inc("scalar", sS)
            P.wait("vector", sS, SQ_S)
            P.op("vector", lambda e: e.reduce_sum(rms_c[:], rms8.ap().rearrange("p a b -> p (a b)"),
                                                  axis=mybir.AxisListType.X))
            P.op("vector", lambda e: e.tensor_scalar(rms_c[:], rms_c[:], 1.0 / DIN, RMS_EPS, A.mult, A.add))
            vm = P.inc("vector", vS)
            P.wait("scalar", vS, vm)
            P.op("scalar", lambda e: e.activation(rms_c[:], rms_c[:], AF.Sqrt))
            sm = P.inc("scalar", sS)
            P.wait("vector", sS, sm)
            P.op("vector", lambda e: e.reciprocal(rms_c[:], rms_c[:]))
            P.op("vector", lambda e: e.scalar_tensor_tensor(y_tm[:, :], y_tm[:, :], rms_c[:], nw_rep[:], A.mult, A.mult))
            P.wait("vector", scD, ZRELD)
            P.op("vector", lambda e: e.tensor_tensor(y_tm[:, :], y_tm[:, :], zch[:, :], A.mult), drain=False)
            YG_V = P.inc("vector", vS)
            zch_mark = YG_V
            sin_mark = tS.n
            P.wait("tensor", vS, YG_V)
            for m4 in range(KDIN // 4):
                pt = m4 % 2
                bankT = psT if pt == 0 else psT2
                P.wait("tensor", vS, psT_mark[pt])
                P.wait("tensor", sS, psT_smark[pt])
                for q in range(4):
                    mm = m4 * 4 + q
                    P.op("tensor", lambda e, mm=mm, q=q, bankT=bankT: e.transpose(
                        bankT[:, q * 128:(q + 1) * 128], y_tm[:, mm * 128:(mm + 1) * 128], ident[:]),
                        drain=False)
                tm = P.inc("tensor", tS)
                P.wait("scalar", tS, tm)
                P.op("scalar", lambda e, m4=m4, ci=ci, bankT=bankT: e.copy(
                    ygfm[:, m4 * 4:(m4 + 1) * 4, ci * 128:(ci + 1) * 128],
                    bankT[:, 0:512].rearrange("p (a b) -> p a b", a=4)), drain=False)
                psT_smark[pt] = P.inc("scalar", sS)
                psT_mark[pt] = vS.n
        YGT_S = P.inc("scalar", sS)
        ygt_tmark = tS.n
        P.wait("tensor", sS, YGT_S)
        for mm in range(NMO):
            pu = mm % 2
            P.wait("sync", ccS, CC_W)
            P.wait("sync", vF, ug_mark[pu])
            P.wait("sync", gS, sc_g_mark)
            P.dma("sync", syA, u_grp[:, pu, :],
                  u_fm[mm * 128:(mm + 1) * 128, HALO + g * GRP: HALO + (g + 1) * GRP])
            bi = mm % 2
            bank = psA if bi == 0 else psB
            P.wait("tensor", vF, bank_markv[bi])
            for hf in range(2):
                sl = hf
                P.wait("sync", tS, max(wt_slot_mark[sl], ygt_tmark))
                P.wait("sync", vS, x1_mark)
                P.dma("sync", syW, wt_s[sl][:, 0:KDIN // 2, :],
                      wo_ag_out[mm // WO_MT, mm % WO_MT, :, hf * (KDIN // 2):(hf + 1) * (KDIN // 2), :])
                P.wait("tensor", syW)
                for k in range(KDIN // 2):
                    kk = hf * (KDIN // 2) + k
                    P.op("tensor", lambda e, k=k, kk=kk, sl=sl, bank=bank: e.matmul(
                        bank[:, :], wt_s[sl][:, k, :], ygfm[:, kk, :],
                        start=(kk == 0), stop=(kk == KDIN - 1)), drain=False)
                wt_slot_mark[sl] = P.inc("tensor", tS)
            tm = tS.n
            P.wait("vector", tS, tm)
            P.wait("vector", syA)
            P.op("vector", lambda e, mm=mm, pu=pu, bank=bank: e.tensor_tensor(
                x1_fm[:, mm, :], u_grp[:, pu, :], bank[:, :], A.add), drain=False)
            if debug:
                P.wait("vector", gD, gD.n)
                P.op("vector", lambda e, bank=bank: e.tensor_copy(outst_dbg[:, 0:GRP], bank[:, :]))
                vm = P.inc("vector", vS)
                P.wait("gpsimd", vS, vm)
                P.dma("gpsimd", gD, mix_dram[mm * 128:(mm + 1) * 128, g * GRP:(g + 1) * GRP], outst_dbg[:, 0:GRP])
            ug_mark[pu] = P.inc("vector", vF)
            bank_markv[bi] = ug_mark[pu]
        X1_V = P.inc("vector", vS)
        P.wait("tensor", vS, X1_V)
        P.wait("tensor", vS, psY_mark)
        for k in range(KD):
            P.op("tensor", lambda e, k=k: e.matmul(
                psSt[0:1, :], ones_col[:], x1_fm[:, k, :], start=(k == 0), stop=(k == KD - 1)), drain=False)
        tm = P.inc("tensor", tS)
        P.wait("scalar", vS, X1_V)
        for k in range(KD):
            if k > 0:
                P.wait("scalar", tS, sq_tmark)
            P.op("scalar", lambda e, k=k: e.activation(sq2, x1_fm[:, k, :], AF.Square))
            sm = P.inc("scalar", sS)
            P.wait("tensor", sS, sm)
            P.op("tensor", lambda e, k=k: e.matmul(
                psG[0:1, :], ones_col[:], sq2, start=(k == 0), stop=(k == KD - 1)), drain=False)
            sq_tmark = P.inc("tensor", tS)
        tm = sq_tmark
        P.wait("vector", tS, tm)
        P.op("vector", lambda e: e.tensor_scalar(srow[0:1, 0, :], psSt[0:1, :], 1.0 / D, None, A.mult))
        P.op("vector", lambda e: e.tensor_scalar(srow[0:1, 1, :], psG[0:1, :], 1.0 / D, None, A.mult))
        P.op("vector", lambda e: e.tensor_tensor(psY0.ap()[0:1, :], srow[0:1, 0, :], srow[0:1, 0, :], A.mult))
        P.op("vector", lambda e: e.tensor_tensor(srow[0:1, 1, :], srow[0:1, 1, :], psY0.ap()[0:1, :], A.subtract))
        P.op("vector", lambda e: e.tensor_scalar(srow[0:1, 1, :], srow[0:1, 1, :], 1.0, LN_EPS, A.mult, A.add))
        vm = P.inc("vector", vS)
        P.wait("scalar", vS, vm)
        P.op("scalar", lambda e: e.activation(srow[0:1, 1, :], srow[0:1, 1, :], AF.Sqrt))
        sm = P.inc("scalar", sS)
        P.wait("vector", sS, sm)
        P.op("vector", lambda e: e.reciprocal(srow[0:1, 1, :], srow[0:1, 1, :]))
        vm = P.inc("vector", vS)
        P.wait("tensor", vS, vm)
        P.op("tensor", lambda e: e.matmul(psY0[:, :], ones_row32[:], srow[0:1, 0, :], start=True, stop=True), drain=False)
        P.op("tensor", lambda e: e.matmul(psY1[:, :], ones_row32[:], srow[0:1, 1, :], start=True, stop=True), drain=False)
        tm = P.inc("tensor", tS)
        P.wait("vector", tS, tm)
        for k in range(KD):
            P.op("vector", lambda e, k=k: e.tensor_tensor(x2_fm[:, k, :], x1_fm[:, k, :], psY0[:, :], A.subtract))
            P.op("vector", lambda e, k=k: e.tensor_tensor(x2_fm[:, k, :], x2_fm[:, k, :], psY1[:, :], A.mult), drain=False)
        X2_V = P.inc("vector", vS)
        psY_mark = X2_V
        P.wait("tensor", vS, X2_V)
        for j in range(MGU // 2):
            P.wait("tensor", sF, bank_mark[0])
            P.wait("tensor", vF, bank_markv[1])
            for half, jj, bank in ((0, j, psA), (1, j + MGU // 2, psB)):
                sl = half
                P.wait("sync", tS, wt_slot_mark[sl])
                P.dma("sync", syW, wt_s[sl][:, 0:KD, :], gu_ag_out[jj // GU_MT, jj % GU_MT])
                P.wait("tensor", syW)
                for k in range(KD):
                    P.op("tensor", lambda e, k=k, sl=sl, bank=bank: e.matmul(
                        bank[:, :], wt_s[sl][:, k, :], x2_fm[:, k, :],
                        start=(k == 0), stop=(k == KD - 1)), drain=False)
                wt_slot_mark[sl] = P.inc("tensor", tS)
            tm = tS.n
            P.wait("scalar", tS, tm)
            P.op("scalar", lambda e: e.activation(gsil, psA[:, :], AF.Silu))
            bank_mark[0] = P.inc("scalar", sF)
            sm = P.inc("scalar", sS)
            P.wait("vector", sS, sm)
            P.op("vector", lambda e, j=j: e.tensor_tensor(h_fm[:, j, :], gsil, psB[:, :], A.mult), drain=False)
            bank_markv[1] = P.inc("vector", vF)
        H_V = P.inc("vector", vS)
        P.wait("tensor", vS, H_V)
        for mm in range(NMO):
            bi = mm % 2
            bank = psA if bi == 0 else psB
            P.wait("tensor", vF, bank_markv[bi])
            for hf in range(2):
                sl = hf
                k0 = hf * (KI // 2)
                kn = (KI // 2) if hf == 0 else (KI - KI // 2)
                P.wait("sync", tS, wt_slot_mark[sl])
                P.dma("sync", syW, wt_s[sl][:, 0:kn, :],
                      dn_ag_out[mm // DN_MT, mm % DN_MT, :, k0:k0 + kn, :])
                P.wait("tensor", syW)
                for k in range(kn):
                    kk = k0 + k
                    P.op("tensor", lambda e, k=k, kk=kk, sl=sl, bank=bank: e.matmul(
                        bank[:, :], wt_s[sl][:, k, :], h_fm[:, kk, :],
                        start=(kk == 0), stop=(kk == KI - 1)), drain=False)
                wt_slot_mark[sl] = P.inc("tensor", tS)
            tm = tS.n
            P.wait("vector", tS, tm)
            P.op("vector", lambda e, mm=mm, bank=bank: e.tensor_copy(y_ffn[:, mm, :], bank[:, :]), drain=False)
            bank_markv[bi] = P.inc("vector", vF)
        YF_V = P.inc("vector", vS)
        P.wait("scalar", vS, YF_V)
        for k in range(KD):
            if k > 0:
                P.wait("scalar", tS, sq_tmark)
            P.op("scalar", lambda e, k=k: e.activation(sq2, y_ffn[:, k, :], AF.Square))
            sm = P.inc("scalar", sS)
            P.wait("tensor", sS, sm)
            P.op("tensor", lambda e, k=k: e.matmul(
                psSt[0:1, :], ones_col[:], sq2, start=(k == 0), stop=(k == KD - 1)), drain=False)
            sq_tmark = P.inc("tensor", tS)
        tm = sq_tmark
        P.wait("vector", tS, tm)
        P.op("vector", lambda e: e.tensor_scalar(srow[0:1, 0, :], psSt[0:1, :], 1.0 / D, FFN_EPS, A.mult, A.add))
        vm = P.inc("vector", vS)
        P.wait("scalar", vS, vm)
        P.op("scalar", lambda e: e.activation(srow[0:1, 0, :], srow[0:1, 0, :], AF.Sqrt))
        sm = P.inc("scalar", sS)
        P.wait("vector", sS, sm)
        P.op("vector", lambda e: e.reciprocal(srow[0:1, 0, :], srow[0:1, 0, :]))
        vm = P.inc("vector", vS)
        P.wait("tensor", vS, vm)
        P.op("tensor", lambda e: e.matmul(psY0[:, :], ones_row32[:], srow[0:1, 0, :], start=True, stop=True), drain=False)
        tm = P.inc("tensor", tS)
        P.wait("vector", tS, tm)
        for k in range(KD):
            P.op("vector", lambda e, k=k: e.tensor_tensor(y_ffn[:, k, :], y_ffn[:, k, :], psY0[:, :], A.mult))
            P.op("vector", lambda e, k=k: e.tensor_tensor(y_ffn[:, k, :], y_ffn[:, k, :], x1_fm[:, k, :], A.add), drain=False)
        S_V = P.inc("vector", vS)
        psY_mark = S_V
        x1_mark = S_V
        sin_mark = S_V
        if debug:
            P.wait("vector", gD, gD.n)
        P.wait("tensor", vS, S_V)
        for tt in range(CPG):
            for half in range(2):
                o_eng, o_sem = ("scalar", scD) if half == 0 else ("sync", syA)
                P.wait("vector", o_sem, out_mark[half])
                for q in range(8):
                    mm = half * 8 + q
                    pt = mm % 2
                    P.wait("tensor", vS, psT_mark[pt])
                    P.op("tensor", lambda e, tt=tt, mm=mm, pt=pt: e.transpose(
                        (psT if pt == 0 else psT2)[:, 0:128], y_ffn[:, mm, tt * 128:(tt + 1) * 128], ident[:]),
                        drain=False)
                    tm = P.inc("tensor", tS)
                    P.wait("vector", tS, tm)
                    P.op("vector", lambda e, q=q, half=half, pt=pt: e.tensor_copy(
                        outst[:, half, q * 128:(q + 1) * 128], (psT if pt == 0 else psT2)[:, 0:128]), drain=False)
                    psT_mark[pt] = P.inc("vector", vS)
                OW_V = P.inc("vector", vS)
                P.wait(o_eng, vS, OW_V)
                out_mark[half] = P.dma(o_eng, o_sem,
                                       out_t[g * GRP + tt * 128: g * GRP + (tt + 1) * 128,
                                             half * 1024:(half + 1) * 1024],
                                       outst[:, half, :])

    return _finish()


def host_prepare(inputs):
    f32 = np.float32
    u = np.asarray(inputs["L0_attn_u"], f32)
    W_in = np.asarray(inputs["L0_attn_W_in"], f32)
    conv_w = np.asarray(inputs["L0_attn_conv_w"], f32)
    conv_b = np.asarray(inputs["L0_attn_conv_b"], f32)
    A_log = np.asarray(inputs["L0_attn_A_log"], f32)
    Dv = np.asarray(inputs["L0_attn_D"], f32)
    dt_bias = np.asarray(inputs["L0_attn_dt_bias"], f32)
    norm_w = np.asarray(inputs["L0_attn_norm_w"], f32)
    W_out = np.asarray(inputs["L0_attn_W_out"], f32)
    gup = np.asarray(inputs["L0_ffn_gate_up_packed"], np.int32)
    gus = np.asarray(inputs["L0_ffn_gate_up_scales"], np.float16)
    dnp = np.asarray(inputs["L0_ffn_down_packed"], np.int32)
    dns = np.asarray(inputs["L0_ffn_down_scales"], np.float16)

    WT = W_in.T
    w_z = np.ascontiguousarray(
        np.concatenate([WT[:, 0:ZC], WT[:, ZC + CONVD:]], axis=1)).reshape(KD, 128, ZC + DTC)
    w_xbc = np.ascontiguousarray(
        WT[:, ZC:ZC + CONVD].reshape(KD, 128, MXBC, 128).transpose(2, 0, 1, 3))
    wo_t = np.ascontiguousarray(W_out.T.reshape(KDIN, 128, NMO, 128).transpose(2, 0, 1, 3))
    gu16 = np.ascontiguousarray(gup.view(np.uint16).reshape(2 * I5, D // 4).T)
    gu16 = gu16.reshape(KD, 32, 2 * I5)
    dn16 = np.ascontiguousarray(dnp.view(np.uint16).reshape(D, I5 // 4).T)
    dn16 = dn16.reshape(KI, 32, D)
    cwp = np.ascontiguousarray(conv_w.reshape(MXBC, 128, 4).transpose(1, 0, 2))
    cbp = np.ascontiguousarray(conv_b.reshape(MXBC, 128).T)
    nwrep = np.ascontiguousarray(np.broadcast_to(norm_w[None, :], (128, DIN)))

    in_maps = []
    for c in range(8):
        b, h = c // 2, c % 2
        t0 = h * T
        if h == 0:
            u_ext = np.concatenate([np.zeros((HALO, D), f32), u[b, 0:T]], axis=0)
        else:
            u_ext = u[b, t0 - HALO:t0 + T]
        u_ext = np.ascontiguousarray(u_ext)
        in_maps.append({
            "u_fm": np.ascontiguousarray(u_ext.T),
            "u_tm": u_ext,
            "w_z": w_z,
            "w_xbc": w_xbc,
            "w_out_sh": np.ascontiguousarray(wo_t[2 * c:2 * c + 2]),
            "gu_w16": np.ascontiguousarray(gu16[:, :, c * GU_SH:(c + 1) * GU_SH]).view(np.int16),
            "gu_scl": np.ascontiguousarray(np.asarray(gus)[:, c * GU_SH:(c + 1) * GU_SH]),
            "dn_w16": np.ascontiguousarray(dn16[:, :, c * DN_SH:(c + 1) * DN_SH]).view(np.int16),
            "dn_scl": np.ascontiguousarray(np.asarray(dns)[:, c * DN_SH:(c + 1) * DN_SH]),
            "conv_w": cwp,
            "conv_b": cbp,
            "a_log": np.ascontiguousarray(A_log[:, None]),
            "dt_bias": np.ascontiguousarray(dt_bias[:, None]),
            "d_col_i": np.ascontiguousarray(Dv[:, None]),
            "d_row_i": np.ascontiguousarray(Dv[None, :]),
            "nw_rep_i": nwrep,
            "is_b": np.array([[float(h)]], f32),
        })
    return in_maps


_NC_CACHE = {}


def kernel(**inputs):
    in_maps = host_prepare(inputs)
    if "nc" not in _NC_CACHE:
        _NC_CACHE["nc"] = build(debug=0)
    nc = _NC_CACHE["nc"]
    res = run_bass_kernel_spmd(nc, in_maps, core_ids=list(range(8)))
    out = np.empty((4, 2 * T, D), np.float32)
    for c in range(8):
        b, h = c // 2, c % 2
        out[b, h * T:(h + 1) * T, :] = res.results[c]["out"]
    return out


# version 17
# speedup vs baseline: 1.0052x; 1.0052x over previous
import numpy as np
from contextlib import ExitStack

import concourse.bass as bass
import concourse.mybir as mybir
from concourse.bass_utils import run_bass_kernel_spmd

F32 = mybir.dt.float32
BF16 = mybir.dt.bfloat16
F16 = mybir.dt.float16
I16 = mybir.dt.int16
U16 = mybir.dt.uint16
I32 = mybir.dt.int32
U8 = mybir.dt.uint8
A = mybir.AluOpType
AF = mybir.ActivationFunctionType

D = 2048
T = 2048
HALO = 3
TH = T + HALO
DIN = 4096
NH = 64
HD = 64
DS = 128
CONVD = 4352
ZC = 4096
DTC = 64
KD = 16
NT = 16
NTE = 17
CH = 128
NCH = 16
GRP = 512
NGRP = 4
CPG = 4
MXBC = 34
KDIN = 32
I5 = 5632
KI = 44
MGU = 88
NMO = 16
ZBLK = 1024
NZB = 4
LN_EPS = 1e-5
RMS_EPS = 1.1920929e-07
FFN_EPS = 1e-6
GU_SH = 1408
GU_MT = 11
DN_SH = 256
DN_MT = 2
WO_MT = 2


class Sem:
    def __init__(self, h):
        self.h = h
        self.n = 0


class Prog:
    ENGS = ("sync", "tensor", "vector", "scalar", "gpsimd")

    def __init__(self):
        self.q = {e: [] for e in self.ENGS}

    def dma(self, eng, sem, out, in_, **kw):
        if sem.n >= 16:
            self.q[eng].append(lambda e, s=sem, vv=sem.n: e.wait_ge(s.h, vv))
        sem.n += 16
        self.q[eng].append(lambda e, o=out, i=in_, s=sem, kw=kw: e.dma_start(out=o, in_=i, **kw).then_inc(s.h, 16))
        return sem.n

    def op(self, eng, fn, inc=None, drain=True):
        if inc is not None:
            inc.n += 1
            g = lambda e, f=fn, s=inc: f(e).then_inc(s.h, 1)
            g._has_inc = True
            g._is_op = True
            self.q[eng].append(g)
        else:
            fn._is_op = True
            self.q[eng].append(fn)
        if drain and eng in ("vector", "scalar", "gpsimd"):
            d = lambda e: e.drain()
            d._is_drain = True
            self.q[eng].append(d)
        return inc.n if inc is not None else None

    def wait(self, eng, sem, val=None):
        v = sem.n if val is None else val
        if v > 0:
            self.q[eng].append(lambda e, s=sem, vv=v: e.wait_ge(s.h, vv))
        return v

    def inc(self, eng, sem, n=1):
        sem.n += n
        last = self.q[eng][-1] if self.q[eng] else None
        if (last is not None and getattr(last, "_is_op", False)
                and not getattr(last, "_has_inc", False)):
            prev = self.q[eng].pop()
            fn = lambda e, p=prev, s=sem, nn=n: p(e).then_inc(s.h, nn)
            fn._has_inc = True
            fn._is_op = True
            self.q[eng].append(fn)
        else:
            if eng in ("vector", "scalar", "gpsimd") and last is not None \
                    and not getattr(last, "_is_drain", False):
                d = lambda e: e.drain()
                d._is_drain = True
                self.q[eng].append(d)
            self.q[eng].append(lambda e, s=sem, nn=n: e.nop().then_inc(s.h, nn))
        return sem.n


def _view(pool, off, shape, dtype):
    esz = mybir.dt.size(dtype)
    n = int(np.prod(shape)) * esz
    assert off % 4 == 0
    ap = pool.ap()[:, off:off + n].bitcast(dtype)
    if len(shape) == 1:
        return ap
    spec = " ".join(f"a{i}" for i in range(len(shape)))
    kw = {f"a{i}": int(shape[i]) for i in range(len(shape))}
    return ap.rearrange(f"p ({spec}) -> p {spec}", **kw)


def build(debug=0, stop=99, xbc_m=36, dis=()):
    nc = bass.Bass("TRN2", target_bir_lowering=False)
    P = Prog()
    es = ExitStack()

    def dr(name, shape, dt, kind=None):
        if kind:
            return nc.dram_tensor(name, shape, dt, kind=kind).ap()
        return nc.dram_tensor(name, shape, dt).ap()

    u_fm = dr("u_fm", [D, TH], F32, "ExternalInput")
    u_tm = dr("u_tm", [TH, D], F32, "ExternalInput")
    w_z = dr("w_z", [KD, 128, ZC + DTC], F32, "ExternalInput")
    w_xbc = dr("w_xbc", [MXBC, KD, 128, 128], F32, "ExternalInput")
    w_out_sh = dr("w_out_sh", [WO_MT, KDIN, 128, 128], F32, "ExternalInput")
    gu_w16 = dr("gu_w16", [KD, 32, GU_SH], I16, "ExternalInput")
    gu_scl = dr("gu_scl", [KD, GU_SH], F16, "ExternalInput")
    dn_w16 = dr("dn_w16", [KI, 32, DN_SH], I16, "ExternalInput")
    dn_scl = dr("dn_scl", [KI, DN_SH], F16, "ExternalInput")
    conv_w_i = dr("conv_w", [128, MXBC, 4], F32, "ExternalInput")
    conv_b_i = dr("conv_b", [128, MXBC], F32, "ExternalInput")
    a_log_i = dr("a_log", [NH, 1], F32, "ExternalInput")
    dt_bias_i = dr("dt_bias", [NH, 1], F32, "ExternalInput")
    d_col_i = dr("d_col_i", [NH, 1], F32, "ExternalInput")
    nw_rep_i = dr("nw_rep_i", [128, DIN], F32, "ExternalInput")
    d_row_i = dr("d_row_i", [1, NH], F32, "ExternalInput")
    is_b_i = dr("is_b", [1, 1], F32, "ExternalInput")
    out_t = dr("out", [T, D], F32, "ExternalOutput")

    dbg = "ExternalOutput" if debug else None
    z_dram = dr("z_dram", [T, ZC], BF16, dbg)
    x_dram = dr("x_dram", [NCH, CH, DIN], BF16, dbg)
    st_dram = dr("st_dram", [NCH, DS, DIN], BF16, dbg)
    sin_dram = dr("sin_dram", [NCH, DS, DIN], BF16, dbg)
    stats_dram2 = dr("stats_dram2", [2 * NTE, 128], BF16)
    vecs_dram = dr("vecs_dram", [2, NCH, NH], F32)
    acs_dram = dr("acs_dram", [NCH, NH, CH], F32)
    acs2_dram = dr("acs2_dram", [NCH, CH, NH], F32)
    sclb_dram = dr("sclb_dram", [KD + KI, GU_SH], BF16)
    stateag_in = dr("stateag_in", [DS, DIN], F32)
    stateag_out = dr("stateag_out", [2, DS, DIN], F32)
    gu_ag_in = dr("gu_ag_in", [GU_MT, 128, KD, 128], BF16)
    gu_ag_out = dr("gu_ag_out", [8, GU_MT, 128, KD, 128], BF16)
    dn_ag_in = dr("dn_ag_in", [DN_MT, 128, KI, 128], BF16)
    dn_ag_out = dr("dn_ag_out", [8, DN_MT, 128, KI, 128], BF16)
    wo_ag_in = dr("wo_ag_in", [WO_MT, 128, KDIN, 128], BF16)
    wo_ag_out = dr("wo_ag_out", [8, WO_MT, 128, KDIN, 128], BF16)
    if debug:
        mix_dram = dr("mix_dram", [D, T], F32, "ExternalOutput")
        dbg_dram = dr("dbg_dram", [6, 128, NCH * NH], F32, "ExternalOutput")

    sb = lambda n, s, d: es.enter_context(nc.sbuf_tensor(n, s, d))
    poolA = sb("poolA", [128, 65664], U8)
    poolB = sb("poolB", [128, 33536], U8)
    poolC = sb("poolC", [128, 26624], U8)
    poolD = sb("poolD", [128, 25600], U8)
    poolF = sb("poolF", [128, 17536], U8)

    Bfm = sb("Bfm", [128, T], BF16)
    Cfm = sb("Cfm", [128, T], BF16)
    ds_lh = sb("ds_lh", [128, NCH, NH], BF16)
    e0_lh = sb("e0_lh", [128, NCH, NH], BF16)
    dt_lh = sb("dt_lh", [128, NCH, NH], BF16)
    D_rep = sb("D_rep", [128, NH], F32)
    acs_lc = sb("acs_lc", [128, NH], F32)
    pg_rep = sb("pg_rep", [128, NCH, NH], BF16)
    S_ext = sb("S_ext", [DS, DIN], BF16)
    nw_rep = sb("nw_rep", [128, DIN], BF16)
    cw = sb("cw", [128, MXBC, 4], F32)
    cb = sb("cb", [128, MXBC], F32)
    ident = sb("ident", [128, 128], BF16)
    ident32 = sb("ident32", [128, 128], F32)
    triu = sb("triu", [128, 128], BF16)
    ones_col = sb("ones_col", [128, 1], BF16)
    ones_row32 = sb("ones_row32", [1, 128], F32)
    A_col = sb("A_col", [NH, 1], F32)
    dtb_col = sb("dtb_col", [NH, 1], F32)
    d_col = sb("d_col", [NH, 1], F32)
    isb_col = sb("isb_col", [128, 1], F32)
    amt16 = sb("amt16", [128, 1], I16)
    amt32 = sb("amt32", [128, 1], I32)
    cs_h = sb("cs_h", [NH, NCH], F32)
    pgl_h = sb("pgl_h", [NH, NCH], F32)
    acs63 = sb("acs63", [NH, 1], F32)
    csT = sb("csT", [NCH, NH], F32)
    pgT = sb("pgT", [NCH, NH], F32)
    sttT = sb("sttT", [2 * NTE, 128], BF16)
    stat_s = sb("stat_s", [128, NTE, 1], F32)
    stat_q = sb("stat_q", [128, NTE, 1], F32)
    stat_t = sb("stat_t", [128, NTE, 1], F32)
    stat_sb = sb("stat_sb", [128, NTE, 2], BF16)
    rms_c = sb("rms_c", [128, 1], F32)
    rms8 = sb("rms8", [128, 8, 1], F32)
    gsc_f = _view(poolF, 0, [GU_SH], F16)[0:KD, :]
    gsc_b = _view(poolF, 2816, [GU_SH], BF16)[0:KD, :]
    dsc_f = _view(poolF, 5632, [DN_SH], F16)[0:KI, :]
    dsc_b = _view(poolF, 6144, [DN_SH], BF16)[0:KI, :]

    ps = lambda n, s, d: es.enter_context(nc.psum_tensor(n, s, d))
    Btm = _view(poolB, 18944, [NCH, 128], BF16)
    psA = ps("psA", [128, 512], F32)
    psB = ps("psB", [128, 512], F32)
    psT = ps("psT", [128, 1024], BF16)
    psT2 = ps("psT2", [128, 1024], BF16)
    psG = ps("psG", [128, 512], F32)
    psY0 = ps("psY0", [128, 512], F32)
    psY1 = ps("psY1", [128, 512], F32)
    psSt = ps("psSt", [128, 512], F32)

    mk = lambda n: Sem(es.enter_context(nc.semaphore(n)))
    syW = mk("syW"); syA = mk("syA"); scD = mk("scD"); gD = mk("gD"); dqD = mk("dqD")
    tS = mk("tS"); vS = mk("vS"); sS = mk("sS"); gS = mk("gS"); ccS = mk("ccS")
    vF = mk("vF"); sF = mk("sF"); tF = mk("tF"); gF = mk("gF")

    blk = es.enter_context(nc.Block())

    def _finish():
        P.wait("scalar", scD, scD.n)
        P.wait("gpsimd", gD, gD.n)
        P.wait("sync", syA, syA.n)
        P.wait("sync", syW, syW.n)
        P.wait("vector", vS, 0)

        @blk.sync
        def _(e):
            for f in P.q["sync"]:
                f(e)

        @blk.tensor
        def _(e):
            for f in P.q["tensor"]:
                f(e)

        @blk.vector
        def _(e):
            for f in P.q["vector"]:
                f(e)

        @blk.scalar
        def _(e):
            for f in P.q["scalar"]:
                f(e)

        @blk.gpsimd
        def _(e):
            for f in P.q["gpsimd"]:
                f(e)

        es.close()
        return nc

    P.op("gpsimd", lambda e: e.memset(ident[:], 0.0))
    P.op("gpsimd", lambda e: e.affine_select(out=ident[:], in_=ident[:], compare_op=A.not_equal,
                                             fill=1.0, base=0, pattern=[[-1, 128]], channel_multiplier=1))
    P.op("gpsimd", lambda e: e.memset(ident32[:], 0.0))
    P.op("gpsimd", lambda e: e.affine_select(out=ident32[:], in_=ident32[:], compare_op=A.not_equal,
                                             fill=1.0, base=0, pattern=[[-1, 128]], channel_multiplier=1))
    P.op("gpsimd", lambda e: e.memset(triu[:], 0.0))
    P.op("gpsimd", lambda e: e.affine_select(out=triu[:], in_=triu[:], compare_op=A.is_gt,
                                             fill=1.0, base=0, pattern=[[-1, 128]], channel_multiplier=1))
    P.op("gpsimd", lambda e: e.memset(ones_col[:], 1.0))
    P.op("gpsimd", lambda e: e.memset(ones_row32[:], 1.0))
    P.op("gpsimd", lambda e: e.memset(stat_s[:], 0.0))
    P.op("gpsimd", lambda e: e.memset(stat_q[:], 0.0))
    P.op("gpsimd", lambda e: e.iota(amt32[:], pattern=[[0, 1]], base=0, channel_multiplier=1))
    CONST_G = P.inc("gpsimd", gS)
    P.wait("vector", gS, CONST_G)
    P.op("vector", lambda e: e.tensor_scalar(amt32[:], amt32[:], 3, None, A.bitwise_and))
    P.op("vector", lambda e: e.tensor_scalar(amt32[:], amt32[:], 4, None, A.mult))
    P.op("vector", lambda e: e.tensor_copy(amt16[:], amt32[:]))
    AMT_V = P.inc("vector", vS)

    P.dma("sync", syA, cw[:], conv_w_i)
    P.dma("sync", syA, cb[:], conv_b_i)
    P.dma("sync", syA, A_col[:], a_log_i)
    P.dma("sync", syA, dtb_col[:], dt_bias_i)
    P.dma("sync", syA, d_col[:], d_col_i)
    P.dma("sync", syA, isb_col[:], is_b_i.broadcast_to([128, 1]))
    P.dma("sync", syA, D_rep[:], d_row_i.broadcast_to([128, NH]))
    P.dma("sync", syA, gsc_f, gu_scl)
    P.dma("sync", syA, dsc_f, dn_scl)
    TINY_LD = syA.n
    nwst = _view(poolD, 0, [2, ZBLK], F32)
    P.wait("sync", syA, TINY_LD)
    nwmark = [0, 0]
    for i in range(4):
        if i >= 2:
            P.wait("sync", vF, nwmark[i % 2])
        P.dma("sync", syA, nwst[:, i % 2, :], nw_rep_i[:, i * ZBLK:(i + 1) * ZBLK])
        P.wait("vector", syA)
        P.op("vector", lambda e, i=i: e.tensor_copy(nw_rep[:, i * ZBLK:(i + 1) * ZBLK], nwst[:, i % 2, :]))
        nwmark[i % 2] = P.inc("vector", vF)
    P.wait("scalar", syA, TINY_LD)
    P.op("scalar", lambda e: e.activation(A_col[:], A_col[:], AF.Exp))
    P.op("scalar", lambda e: e.mul(A_col[:], A_col[:], -1.0))
    A_S = P.inc("scalar", sS)
    P.wait("vector", syA, TINY_LD)
    P.op("vector", lambda e: e.tensor_scalar(gsc_b, gsc_f, 0.25, None, A.mult))
    P.op("vector", lambda e: e.tensor_scalar(dsc_b, dsc_f, 0.25, None, A.mult))
    SCL_V = P.inc("vector", vS)
    P.wait("gpsimd", vS, SCL_V)
    P.dma("gpsimd", gD, sclb_dram[0:KD, :], gsc_b)
    P.dma("gpsimd", gD, sclb_dram[KD:KD + KI, 0:DN_SH], dsc_b)
    P.wait("gpsimd", gD, gD.n)


    dq_wrep = _view(poolF, 0, [2, 704], I16)
    dq_srep = _view(poolF, 2816, [2, 704], BF16)
    DQH = 704
    dq_a = _view(poolF, 5632, [DQH], I16)
    dq_b = _view(poolF, 5632 + 1408, [DQH], I16)
    dq_c = _view(poolF, 5632 + 2816, [DQH], I16)
    dq_d = _view(poolF, 5632 + 4224, [DQH], I16)
    dq_v = _view(poolF, 5632 + 5632, [DQH], BF16)
    dq_o = _view(poolF, 5632 + 7040, [GU_SH], BF16)
    dq_state = {"mark": [0, 0], "si": 0, "first": True, "sclb_gate": 0}

    def dq_steps():
        amt_b = amt16.ap().broadcast_to([128, DQH])

        def decode(sl, wd, ow):
            if dq_state["first"]:
                P.wait("vector", gD, dq_state["sclb_gate"])
                dq_state["first"] = False
            P.op("vector", lambda e: e.tensor_tensor(
                dq_a[:, 0:wd], dq_wrep[:, sl, 0:wd], amt_b[:, 0:wd], A.logical_shift_right))
            P.op("vector", lambda e: e.tensor_scalar(dq_a[:, 0:wd], dq_a[:, 0:wd], 15, None, A.bitwise_and))
            P.op("vector", lambda e: e.tensor_scalar(dq_b[:, 0:wd], dq_a[:, 0:wd], 7, None, A.bitwise_and))
            P.op("vector", lambda e: e.tensor_scalar(dq_c[:, 0:wd], dq_b[:, 0:wd], 1, None, A.logical_shift_right))
            P.op("vector", lambda e: e.tensor_scalar(dq_d[:, 0:wd], dq_b[:, 0:wd], 1, None, A.bitwise_and))
            P.op("vector", lambda e: e.tensor_scalar(dq_d[:, 0:wd], dq_d[:, 0:wd], 2, None, A.add))
            P.op("vector", lambda e: e.tensor_tensor(dq_d[:, 0:wd], dq_d[:, 0:wd], dq_c[:, 0:wd], A.logical_shift_left))
            P.op("vector", lambda e: e.tensor_scalar(dq_c[:, 0:wd], dq_b[:, 0:wd], -1, 2, A.mult, A.add))
            P.op("vector", lambda e: e.tensor_scalar(dq_c[:, 0:wd], dq_c[:, 0:wd], 0, None, A.max))
            P.op("vector", lambda e: e.tensor_tensor(dq_d[:, 0:wd], dq_d[:, 0:wd], dq_c[:, 0:wd], A.subtract))
            P.op("vector", lambda e: e.tensor_scalar(dq_c[:, 0:wd], dq_a[:, 0:wd], 8, None, A.bitwise_and))
            P.op("vector", lambda e: e.tensor_scalar(dq_c[:, 0:wd], dq_c[:, 0:wd], 2, None, A.logical_shift_right))
            P.op("vector", lambda e: e.tensor_scalar(dq_c[:, 0:wd], dq_c[:, 0:wd], -1, 1, A.mult, A.add))
            P.op("vector", lambda e: e.tensor_tensor(dq_d[:, 0:wd], dq_d[:, 0:wd], dq_c[:, 0:wd], A.mult))
            P.op("vector", lambda e: e.tensor_copy(dq_v[:, 0:wd], dq_d[:, 0:wd]))
            P.op("vector", lambda e: e.tensor_tensor(ow, dq_v[:, 0:wd], dq_srep[:, sl, 0:wd], A.mult))
            dq_state["mark"][sl] = P.inc("vector", vS)

        for k in range(KD):
            def step(k=k):
                for hh in range(2):
                    sl = dq_state["si"] % 2
                    dq_state["si"] += 1
                    c0 = hh * DQH
                    P.wait("sync", vS, dq_state["mark"][sl])
                    P.dma("sync", dqD, dq_wrep[:, sl, :],
                          gu_w16[k][:, c0:c0 + DQH].unsqueeze(1).broadcast_to([32, 4, DQH]))
                    P.dma("sync", dqD, dq_srep[:, sl, :],
                          sclb_dram[k:k + 1, c0:c0 + DQH].broadcast_to([128, DQH]))
                    P.wait("vector", dqD, dqD.n)
                    decode(sl, DQH, dq_o[:, c0:c0 + DQH])
                P.wait("sync", vS, dq_state["mark"][(dq_state["si"] - 1) % 2])
                for j in range(GU_SH // 128):
                    P.dma("sync", dqD, gu_ag_in[j, :, k, :], dq_o[:, j * 128:(j + 1) * 128])
            yield step
        for k in range(0, KI, 2):
            def step(k=k):
                sl = dq_state["si"] % 2
                dq_state["si"] += 1
                P.wait("sync", vS, dq_state["mark"][sl])
                for q in range(2):
                    P.dma("sync", dqD, dq_wrep[:, sl, q * DN_SH:(q + 1) * DN_SH],
                          dn_w16[k + q].unsqueeze(1).broadcast_to([32, 4, DN_SH]))
                    P.dma("sync", dqD, dq_srep[:, sl, q * DN_SH:(q + 1) * DN_SH],
                          sclb_dram[KD + k + q:KD + k + q + 1, 0:DN_SH].broadcast_to([128, DN_SH]))
                P.wait("vector", dqD, dqD.n)
                decode(sl, 2 * DN_SH, dq_o[:, 0:2 * DN_SH])
                P.wait("sync", vS, dq_state["mark"][sl])
                for q in range(2):
                    for j in range(DN_SH // 128):
                        P.dma("sync", dqD, dn_ag_in[j, :, k + q, :],
                              dq_o[:, q * DN_SH + j * 128: q * DN_SH + (j + 1) * 128])
            yield step
        for j in range(WO_MT):
            def step2(j=j):
                for kq in range(KDIN // 8):
                    P.dma("gpsimd", gD, wo_ag_in[j, :, kq * 8:(kq + 1) * 8, :],
                          w_out_sh[j, kq * 8:(kq + 1) * 8].transpose([1, 0, 2]))
            yield step2

    dq_state["sclb_gate"] = gD.n
    _dq_iter = dq_steps()

    def dq_pump(n=1):
        for _ in range(n):
            s = next(_dq_iter, None)
            if s is not None:
                s()

    def dq_finish():
        while True:
            s = next(_dq_iter, None)
            if s is None:
                return

    if stop <= 0:
        return _finish()
    ubar = _view(poolA, 0, [KD, TH], BF16)
    u_stage = _view(poolD, 0, [2, TH], F32)
    m_bc = _view(poolD, 16408, [NTE * 128], BF16)
    r_bc = _view(poolD, 16408 + 2 * NTE * 128, [NTE * 128], BF16)
    sqd = _view(poolB, 0, [D], BF16)

    P.wait("sync", vF, vF.n)
    P.wait("gpsimd", vF, vF.n)
    P.wait("scalar", gS, CONST_G)
    smark = [0, 0]
    for j in range(NTE):
        rows = 128 if j < NT else HALO
        pb = j % 2
        eng, sem = ("sync", syA) if pb == 0 else ("gpsimd", gD)
        if j >= 2:
            P.wait(eng, sF, smark[pb])
        P.dma(eng, sem, u_stage[0:rows, pb, 0:D], u_tm[j * 128:j * 128 + rows, :])
        P.wait("scalar", sem)
        P.op("scalar", lambda e, j=j, rows=rows, pb=pb: e.activation(
            sqd[0:rows, :], u_stage[0:rows, pb, 0:D], AF.Copy, accum_out=stat_s[0:rows, j, :]))
        P.op("scalar", lambda e, j=j, rows=rows, pb=pb: e.activation(
            sqd[0:rows, :], u_stage[0:rows, pb, 0:D], AF.Square, accum_out=stat_q[0:rows, j, :]))
        smark[pb] = P.inc("scalar", sF)
    STATS_S = P.inc("scalar", sS)
    P.wait("vector", sS, STATS_S)
    P.op("vector", lambda e: e.tensor_scalar(stat_s[:], stat_s[:], 1.0 / D, None, A.mult))
    P.op("vector", lambda e: e.tensor_scalar(stat_q[:], stat_q[:], 1.0 / D, None, A.mult))
    P.op("vector", lambda e: e.tensor_tensor(stat_t[:], stat_s[:], stat_s[:], A.mult))
    P.op("vector", lambda e: e.tensor_tensor(stat_q[:], stat_q[:], stat_t[:], A.subtract))
    P.op("vector", lambda e: e.tensor_scalar(stat_q[:], stat_q[:], 1.0, LN_EPS, A.mult, A.add))
    vm = P.inc("vector", vS)
    P.wait("scalar", vS, vm)
    P.op("scalar", lambda e: e.activation(stat_q[:], stat_q[:], AF.Sqrt))
    sm = P.inc("scalar", sS)
    P.wait("vector", sS, sm)
    P.op("vector", lambda e: e.reciprocal(stat_q[:], stat_q[:]))
    P.op("vector", lambda e: e.tensor_copy(stat_sb[:, :, 0:1], stat_s[:]))
    P.op("vector", lambda e: e.tensor_copy(stat_sb[:, :, 1:2], stat_q[:]))
    STB_V = P.inc("vector", vS)
    P.wait("tensor", vS, STB_V)
    P.op("tensor", lambda e: e.transpose(psT[0:2 * NTE, 0:128],
                                         stat_sb.ap().rearrange("p a b -> p (a b)"), ident[:]), drain=False)
    tm = P.inc("tensor", tS)
    P.wait("vector", tS, tm)
    P.op("vector", lambda e: e.tensor_copy(sttT[:], psT[0:2 * NTE, 0:128]))
    vm = P.inc("vector", vS)
    P.wait("gpsimd", vS, vm)
    P.dma("gpsimd", gD, stats_dram2.ap() if False else stats_dram2, sttT[:])
    P.wait("gpsimd", gD, gD.n)
    msrc = stats_dram2[0:2 * NTE:2, :].unsqueeze(0).broadcast_to([128, NTE, 128])
    rsrc = stats_dram2[1:2 * NTE:2, :].unsqueeze(0).broadcast_to([128, NTE, 128])
    P.dma("gpsimd", gD, m_bc.rearrange("p (a b) -> p a b", a=NTE), msrc)
    P.dma("gpsimd", gD, r_bc.rearrange("p (a b) -> p a b", a=NTE), rsrc)
    STATS_G = gD.n

    vmark = [0, 0]
    P.wait("sync", sF, sF.n)
    P.wait("gpsimd", sF, sF.n)
    for k in range(KD):
        pb = k % 2
        eng, sem = ("sync", syA) if pb == 0 else ("gpsimd", gD)
        if k >= 2:
            P.wait(eng, vF, vmark[pb])
        P.dma(eng, sem, u_stage[:, pb, :], u_fm[k * 128:(k + 1) * 128, :])
        P.wait("vector", sem)
        if k == 0:
            P.wait("vector", gD, STATS_G)
        P.op("vector", lambda e, k=k, pb=pb: e.tensor_tensor(ubar[:, k, :], u_stage[:, pb, :], m_bc[:, 0:TH], A.subtract))
        P.op("vector", lambda e, k=k: e.tensor_tensor(ubar[:, k, :], ubar[:, k, :], r_bc[:, 0:TH], A.mult), drain=False)
        vmark[pb] = P.inc("vector", vF)
    UBAR_V = P.inc("vector", vS)

    if stop <= 1:
        return _finish()
    wzb = _view(poolB, 0, [KD, 2, 512], BF16)
    zev = _view(poolC, 8192, [2, 512], BF16)
    dtraw = _view(poolC, 8192 + 2048, [NCH, NH], F32)

    P.wait("tensor", vS, UBAR_V)
    bank_mark = {0: 0, 1: 0}
    bank_markv = {0: 0, 1: 0}
    wz_sl_tmark = [0, 0]
    zev_mark = [0, 0]
    NCB = ZC // 512 + 1
    for zcb in range(NCB):
        width = 512 if zcb < NCB - 1 else DTC
        sl = zcb % 2
        P.wait("gpsimd", tS, wz_sl_tmark[sl])
        for k in range(KD):
            P.dma("gpsimd", gD, wzb[:, k, sl, 0:width], w_z[k, :, zcb * 512: zcb * 512 + width])
        WZG = gD.n
        dq_pump(2)
        P.wait("tensor", gD, WZG)
        for j in range(NT):
            bi = j % 2
            bank = psA if bi == 0 else psB
            P.wait("tensor", sF, bank_mark[bi])
            P.wait("tensor", vF, bank_markv[bi])
            for k in range(KD):
                P.op("tensor", lambda e, j=j, k=k, sl=sl, width=width, bank=bank: e.matmul(
                    bank[:, 0:width], ubar[:, k, HALO + j * 128: HALO + (j + 1) * 128],
                    wzb[:, k, sl, 0:width], start=(k == 0), stop=(k == KD - 1)), drain=False)
            tm = P.inc("tensor", tS)
            if zcb < NCB - 1:
                pe = j % 2
                P.wait("scalar", tS, tm)
                P.wait("scalar", scD, zev_mark[pe])
                P.op("scalar", lambda e, pe=pe, bank=bank: e.activation(
                    zev[:, pe, 0:512], bank[:, 0:512], AF.Silu))
                bank_mark[bi] = P.inc("scalar", sF)
                P.dma("scalar", scD, z_dram[j * 128:(j + 1) * 128, zcb * 512: zcb * 512 + 512],
                      zev[:, pe, 0:512])
                zev_mark[pe] = scD.n
            else:
                P.wait("vector", tS, tm)
                P.op("vector", lambda e, j=j, bank=bank: e.tensor_copy(dtraw[:, j, :], bank[:, 0:DTC]), drain=False)
                bank_markv[bi] = P.inc("vector", vF)
        wz_sl_tmark[sl] = tS.n
    DT_V = P.inc("vector", vS)

    if stop <= 2:
        return _finish()
    wk_a = _view(poolC, 8192 + 2048 + 4096, [CH], F32)[0:NH, :]
    wk_b = _view(poolC, 8192 + 2048 + 4096 + 512, [CH], F32)[0:NH, :]
    wk_c = _view(poolC, 8192 + 2048 + 4096 + 1024, [CH], F32)[0:NH, :]
    wk_d = _view(poolC, 8192 + 2048 + 4096 + 1536, [CH], F32)[0:NH, :]
    wk_e = _view(poolC, 8192 + 2048 + 4096 + 2048, [CH], F32)[0:NH, :]

    P.wait("tensor", vS, DT_V)
    P.wait("scalar", sS, A_S)
    ps3_mark = {"G": 0, "Y0": 0, "Y1": 0}
    acs2_mark = 0
    acs1_mark = 0
    for c in range(NCH):
        dq_pump(1)
        P.wait("vector", gD, acs1_mark)
        P.wait("scalar", gD, acs1_mark)
        P.wait("tensor", vF, ps3_mark["G"])
        P.op("tensor", lambda e, c=c: e.transpose(psG[0:NH, 0:128], dtraw[:, c, :], ident32[:]), drain=False)
        tm = P.inc("tensor", tS)
        P.wait("vector", tS, tm)
        P.op("vector", lambda e: e.tensor_scalar(wk_a[:], psG[0:NH, 0:128], 1.0, dtb_col[:], A.mult, A.add))
        ps3_mark["G"] = P.inc("vector", vF)
        vm = P.inc("vector", vS)
        P.wait("scalar", vS, vm)
        P.op("scalar", lambda e: e.activation(wk_b[:], wk_a[:], AF.Abs))
        P.op("scalar", lambda e: e.activation(wk_b[:], wk_b[:], AF.Exp, scale=-1.0))
        P.op("scalar", lambda e: e.activation(wk_b[:], wk_b[:], AF.Ln, bias=1.0))
        P.op("scalar", lambda e: e.activation(wk_c[:], wk_a[:], AF.Relu))
        sm = P.inc("scalar", sS)
        P.wait("vector", sS, sm)
        P.op("vector", lambda e: e.tensor_add(wk_b[:], wk_b[:], wk_c[:]))
        P.op("vector", lambda e: e.tensor_scalar(wk_a[:], wk_b[:], A_col[:], None, A.mult))
        P.op("vector", lambda e: e.tensor_tensor_scan(wk_c[:], wk_a[:], wk_a[:], 0.0, A.add, A.bypass))
        P.op("vector", lambda e, c=c: e.tensor_copy(cs_h[:, c:c + 1], wk_c[:, 127:128]))
        P.op("vector", lambda e: e.tensor_copy(acs63[:], wk_c[:, 63:64]))
        P.op("vector", lambda e: e.tensor_scalar(wk_d[:], wk_c[:], -1.0, wk_c[:, 127:128], A.mult, A.add))
        vm = P.inc("vector", vS)
        P.wait("scalar", vS, vm)
        P.op("scalar", lambda e: e.activation(wk_d[:], wk_d[:], AF.Exp))
        sm = P.inc("scalar", sS)
        P.wait("vector", sS, sm)
        P.op("vector", lambda e: e.tensor_tensor(wk_d[:], wk_d[:], wk_b[:], A.mult))
        TDS_V = P.inc("vector", vS)
        P.wait("tensor", vS, TDS_V)
        P.wait("tensor", vF, ps3_mark["Y0"])
        P.op("tensor", lambda e: e.transpose(psY0[:, 0:64], wk_d[:], ident32[0:64, 0:64]), drain=False)
        P.op("tensor", lambda e: e.transpose(psY0[:, 64:128], wk_b[:], ident32[0:64, 0:64]), drain=False)
        tm = P.inc("tensor", tS)
        P.wait("vector", tS, tm)
        P.op("vector", lambda e, c=c: e.tensor_copy(ds_lh[:, c, :], psY0[:, 0:64]))
        P.op("vector", lambda e, c=c: e.tensor_copy(dt_lh[:, c, :], psY0[:, 64:128]))
        ps3_mark["Y0"] = P.inc("vector", vF)
        amark = P.inc("vector", vS)
        P.wait("gpsimd", vS, amark)
        acs1_mark = P.dma("gpsimd", gD, acs_dram[c], wk_c[:])
        P.op("vector", lambda e: e.tensor_copy(wk_e[:], wk_c[:]))
        vm = P.inc("vector", vS)
        P.wait("scalar", vS, vm)
        P.op("scalar", lambda e: e.activation(wk_e[:], wk_e[:], AF.Exp))
        sm = P.inc("scalar", sS)
        P.wait("tensor", sS, sm)
        P.wait("tensor", vF, ps3_mark["Y1"])
        P.op("tensor", lambda e: e.transpose(psY1[:, 0:64], wk_e[:], ident32[0:64, 0:64]), drain=False)
        P.op("tensor", lambda e: e.transpose(psY1[:, 64:128], wk_c[:], ident32[0:64, 0:64]), drain=False)
        tm = P.inc("tensor", tS)
        P.wait("vector", tS, tm)
        P.op("vector", lambda e, c=c: e.tensor_copy(e0_lh[:, c, :], psY1[:, 0:64]))
        P.wait("vector", gD, acs2_mark)
        P.op("vector", lambda e: e.tensor_copy(acs_lc[:], psY1[:, 64:128]))
        ps3_mark["Y1"] = P.inc("vector", vF)
        vm = P.inc("vector", vS)
        P.wait("gpsimd", vS, vm)
        acs2_mark = P.dma("gpsimd", gD, acs2_dram[c], acs_lc[:])
    P.op("vector", lambda e: e.tensor_tensor_scan(pgl_h[:], cs_h[:], cs_h[:], 0.0, A.add, A.bypass))
    P.op("vector", lambda e: e.tensor_tensor(pgl_h[:], pgl_h[:], cs_h[:], A.subtract))
    vm = P.inc("vector", vS)
    P.wait("scalar", vS, vm)
    P.op("scalar", lambda e: e.activation(cs_h[:], cs_h[:], AF.Exp))
    P.op("scalar", lambda e: e.activation(pgl_h[:], pgl_h[:], AF.Exp))
    sm = P.inc("scalar", sS)
    P.wait("tensor", sS, sm)
    P.wait("tensor", vF, ps3_mark["Y0"])
    P.op("tensor", lambda e: e.transpose(psY0[:, 0:64][0:16, :] if False else psY0[0:16, 0:64], cs_h[:], ident32[0:64, 0:64]), drain=False)
    P.op("tensor", lambda e: e.transpose(psY0[0:16, 64:128], pgl_h[:], ident32[0:64, 0:64]), drain=False)
    tm = P.inc("tensor", tS)
    P.wait("vector", tS, tm)
    P.op("vector", lambda e: e.tensor_copy(csT[:], psY0[0:16, 0:64]))
    P.op("vector", lambda e: e.tensor_copy(pgT[:], psY0[0:16, 64:128]))
    vm = P.inc("vector", vS)
    P.wait("gpsimd", vS, vm)
    P.dma("gpsimd", gD, vecs_dram[0], csT[:])
    P.dma("gpsimd", gD, vecs_dram[1], pgT[:])
    P.wait("gpsimd", gD, gD.n)
    P.dma("gpsimd", gD, pg_rep.ap().rearrange("p a b -> p (a b)"),
          vecs_dram[1].rearrange("a b -> (a b)").unsqueeze(0).broadcast_to([128, NCH * NH]))
    PG_G = gD.n
    DVEC_V = P.inc("vector", vS)

    if stop <= 3:
        return _finish()
    wxb = _view(poolC, 0, [2, KD, 128], BF16)
    xbcraw = _view(poolC, 8192, [2, TH], BF16)
    xacc = _view(poolC, 8192 + 8208, [2, TH], BF16)
    XsBUF = _view(poolB, 0, [NCH, 512], BF16)
    xtm_st = _view(poolB, NCH * 1024, [2, 128], BF16)
    st_st = _view(poolB, NCH * 1024 + 512, [2, 512], BF16)

    wxmark = [0, 0]
    xacc_mark = [0, 0]
    xbc_mark = [0, 0]
    psT_mark = [0, 0]
    psT_smark = [0, 0]
    xst_mark = [0, 0]
    stst_mark = [0, 0]
    psSt_mark = 0
    P.wait("sync", vF, vF.n)
    P.wait("scalar", scD, scD.n)
    P.wait("scalar", gD, gD.n)
    order = ([32, 33] + list(range(32)))[:xbc_m]
    pending = []
    xacc_tmark = [0, 0]
    wxb_tmark = [0, 0]
    for mi, m in enumerate(order):
        mb = mi % 2
        P.wait("gpsimd", tS, wxb_tmark[mb])
        WXG = P.dma("gpsimd", gD, wxb[:, mb, :, :], w_xbc[m].transpose([1, 0, 2]))
        P.wait("tensor", gD, WXG)
        if m < 8:
            dq_pump(1)
        ngs = [(0, 512), (512, 512), (1024, 512), (1536, 512), (2048, HALO)]
        for gi, (n0, nn) in enumerate(ngs):
            bi = gi % 2
            bank = psA if bi == 0 else psB
            P.wait("tensor", sF, bank_mark[bi])
            for k in range(KD):
                P.op("tensor", lambda e, k=k, mb=mb, n0=n0, nn=nn, bank=bank: e.matmul(
                    bank[:, 0:nn], wxb[:, mb, k, :], ubar[:, k, n0:n0 + nn],
                    start=(k == 0), stop=(k == KD - 1)), drain=False)
            tm = P.inc("tensor", tS)
            P.wait("scalar", tS, tm)
            if gi == 0:
                P.wait("scalar", vS, xacc_mark[mb])
            P.op("scalar", lambda e, mb=mb, n0=n0, nn=nn, bank=bank: e.copy(
                xbcraw[:, mb, n0:n0 + nn], bank[:, 0:nn]), drain=False)
            bank_mark[bi] = P.inc("scalar", sF)
        wxb_tmark[mb] = tS.n
        XB_S = P.inc("scalar", sS)
        P.wait("vector", sS, XB_S)
        P.wait("vector", tS, xacc_tmark[mb])
        cbb = cb.ap()[:, m:m + 1].broadcast_to([128, T])
        P.op("vector", lambda e, m=m, mb=mb, cbb=cbb: e.scalar_tensor_tensor(
            xacc[:, mb, HALO:TH], xbcraw[:, mb, 0:T], cw.ap()[:, m, 0:1], cbb, A.mult, A.add))
        for tap in range(1, 4):
            P.op("vector", lambda e, m=m, mb=mb, tap=tap: e.scalar_tensor_tensor(
                xacc[:, mb, HALO:TH], xbcraw[:, mb, tap:T + tap], cw.ap()[:, m, tap:tap + 1],
                xacc[:, mb, HALO:TH], A.mult, A.add))
        CV_V = P.inc("vector", vS)
        xacc_mark[mb] = CV_V
        P.wait("scalar", vS, CV_V)
        P.op("scalar", lambda e, mb=mb: e.activation(xacc[:, mb, HALO:TH], xacc[:, mb, HALO:TH], AF.Silu), drain=False)
        SIL_S = P.inc("scalar", sS)
        if m == 32:
            P.wait("vector", sS, SIL_S)
            P.op("vector", lambda e, mb=mb: e.tensor_copy(Bfm[:], xacc[:, mb, HALO:TH]))
            BM_V = P.inc("vector", vS)
            P.wait("tensor", vS, BM_V)
            for c in range(NCH):
                pt = c % 2
                P.wait("tensor", vS, psT_mark[pt])
                P.op("tensor", lambda e, c=c, pt=pt: e.transpose(
                    (psT if pt == 0 else psT2)[:, 0:128], Bfm[:, c * 128:(c + 1) * 128], ident[:]), drain=False)
                tm = P.inc("tensor", tS)
                P.wait("vector", tS, tm)
                P.op("vector", lambda e, c=c, pt=pt: e.tensor_copy(Btm[:, c, :], (psT if pt == 0 else psT2)[:, 0:128]), drain=False)
                psT_mark[pt] = P.inc("vector", vS)
        elif m == 33:
            P.wait("vector", sS, SIL_S)
            P.op("vector", lambda e, mb=mb: e.tensor_copy(Cfm[:], xacc[:, mb, HALO:TH]))
            P.inc("vector", vS)
        else:
            def make_post(m=m, mb=mb, SIL_S=SIL_S):
                def post():
                    nonlocal psSt_mark
                    blk_i = m % 4
                    P.wait("tensor", sS, SIL_S)
                    for c in range(NCH):
                        pt = c % 2
                        P.wait("tensor", vS, psT_mark[pt])
                        P.wait("tensor", sS, psT_smark[pt])
                        P.op("tensor", lambda e, c=c, mb=mb, pt=pt: e.transpose(
                            (psT if pt == 0 else psT2)[:, 0:128],
                            xacc[:, mb, HALO + c * 128: HALO + (c + 1) * 128], ident[:]), drain=False)
                        tm = P.inc("tensor", tS)
                        P.wait("vector", tS, tm)
                        dsap = ds_lh.ap()[:, c, 2 * m: 2 * m + 2].unsqueeze(2).broadcast_to([128, 2, HD])
                        P.op("vector", lambda e, c=c, blk_i=blk_i, pt=pt, dsap=dsap: e.tensor_tensor(
                            XsBUF[:, c, blk_i * 128:(blk_i + 1) * 128].rearrange("p (a b) -> p a b", a=2),
                            (psT if pt == 0 else psT2)[:, 0:128].rearrange("p (a b) -> p a b", a=2), dsap, A.mult), drain=False)
                        psT_mark[pt] = P.inc("vector", vS)
                        P.wait("scalar", tS, tm)
                        P.wait("scalar", vS, psT_mark[pt])
                        P.wait("scalar", gD, xst_mark[pt])
                        P.op("scalar", lambda e, pt=pt: e.copy(xtm_st[:, pt, :], (psT if pt == 0 else psT2)[:, 0:128]), drain=False)
                        psT_smark[pt] = P.inc("scalar", sS)
                        P.wait("gpsimd", sS, psT_smark[pt])
                        P.dma("gpsimd", gD, x_dram[c, :, m * 128:(m + 1) * 128], xtm_st[:, pt, :])
                        xst_mark[pt] = gD.n
                    XS_V = P.inc("vector", vS)
                    xacc_tmark[mb] = tS.n
                    if (m + 1) % 4 == 0:
                        blk0 = (m - 3) * 128
                        P.wait("tensor", vS, XS_V)
                        for c in range(NCH):
                            pt = c % 2
                            P.wait("tensor", vS, psSt_mark)
                            P.op("tensor", lambda e, c=c: e.matmul(
                                psSt[:, :], Btm[:, c, :], XsBUF[:, c, :], start=True, stop=True), drain=False)
                            tm = P.inc("tensor", tS)
                            P.wait("vector", tS, tm)
                            P.wait("vector", gD, stst_mark[pt])
                            P.op("vector", lambda e, pt=pt: e.tensor_copy(st_st[:, pt, :], psSt[:, :]), drain=False)
                            psSt_mark = P.inc("vector", vS)
                            P.wait("gpsimd", vS, psSt_mark)
                            P.dma("gpsimd", gD, st_dram[c, :, blk0:blk0 + 512], st_st[:, pt, :])
                            stst_mark[pt] = gD.n
                return post
            pending.append(make_post())
            if len(pending) > 1:
                pending.pop(0)()
    while pending:
        pending.pop(0)()
    XDONE_V = P.inc("vector", vS)
    if debug:
        P.wait("gpsimd", vS, XDONE_V)
        for i_, tt_ in enumerate([ds_lh, dt_lh, e0_lh]):
            P.dma("gpsimd", gD, dbg_dram[i_], tt_.ap().rearrange("p a b -> p (a b)"))
        P.dma("gpsimd", gD, dbg_dram[5], dtraw.rearrange("p a b -> p (a b)"))

    dq_finish()

    if stop <= 4:
        return _finish()
    S_run = _view(poolA, 0, [DIN], F32)
    st_ld = _view(poolA, DIN * 4, [2, DIN], BF16)
    sin_st = _view(poolA, DIN * 4 + DIN * 4, [2, DIN], BF16)
    etot_rep = _view(poolA, DIN * 4 + DIN * 8, [NCH, NH], BF16)
    etot_flat = _view(poolA, DIN * 4 + DIN * 8, [NCH * NH], BF16)

    P.wait("gpsimd", vS, XDONE_V)
    P.wait("gpsimd", gD, gD.n)
    P.dma("gpsimd", gD, etot_flat,
          vecs_dram[0].rearrange("a b -> (a b)").unsqueeze(0).broadcast_to([128, NCH * NH]))
    ETOT_G = gD.n
    stld_mark = [0, 0]
    P.wait("sync", gD, gD.n)
    P.wait("vector", vS, XDONE_V)
    P.wait("vector", gD, ETOT_G)
    P.op("vector", lambda e: e.memset(S_run, 0.0))
    srun_vmark = P.inc("vector", vF)
    sinst_mark = [0, 0]
    for c in range(NCH):
        pb = c % 2
        if c >= 2:
            P.wait("sync", vF, stld_mark[pb])
        P.dma("sync", syA, st_ld[:, pb, :], st_dram[c])
        P.wait("scalar", vF, srun_vmark)
        P.wait("scalar", scD, sinst_mark[pb])
        P.op("scalar", lambda e, pb=pb: e.copy(sin_st[:, pb, :], S_run))
        sm = P.inc("scalar", sS)
        sinst_mark[pb] = P.dma("scalar", scD, sin_dram[c], sin_st[:, pb, :])
        eap = etot_rep[:, c, :].unsqueeze(2).broadcast_to([128, NH, HD])
        P.wait("vector", sS, sm)
        P.op("vector", lambda e, eap=eap: e.tensor_tensor(
            S_run.rearrange("p (a b) -> p a b", a=NH), S_run.rearrange("p (a b) -> p a b", a=NH), eap, A.mult))
        P.wait("vector", syA)
        P.op("vector", lambda e, pb=pb: e.tensor_tensor(S_run, S_run, st_ld[:, pb, :], A.add))
        stld_mark[pb] = P.inc("vector", vF)
        srun_vmark = stld_mark[pb]
    SRUN_V = P.inc("vector", vS)
    P.wait("gpsimd", vS, SRUN_V)
    P.wait("gpsimd", gD, gD.n)
    P.dma("gpsimd", gD, stateag_in.opt(), S_run)
    P.wait("gpsimd", gD, gD.n)
    P.op("gpsimd", lambda e: e.collective_compute(
        "AllGather", A.bypass, replica_groups=[[0, 1], [2, 3], [4, 5], [6, 7]],
        ins=[stateag_in.opt()], outs=[stateag_out.opt()]), inc=ccS, drain=False)
    CC_ST = ccS.n
    P.wait("gpsimd", dqD, dqD.n)
    P.op("gpsimd", lambda e: e.collective_compute(
        "AllGather", A.bypass, replica_groups=[[0, 1, 2, 3, 4, 5, 6, 7]],
        ins=[gu_ag_in.opt()], outs=[gu_ag_out.opt()]), inc=ccS, drain=False)
    P.op("gpsimd", lambda e: e.collective_compute(
        "AllGather", A.bypass, replica_groups=[[0, 1, 2, 3, 4, 5, 6, 7]],
        ins=[dn_ag_in.opt()], outs=[dn_ag_out.opt()]), inc=ccS, drain=False)
    P.op("gpsimd", lambda e: e.collective_compute(
        "AllGather", A.bypass, replica_groups=[[0, 1, 2, 3, 4, 5, 6, 7]],
        ins=[wo_ag_in.opt()], outs=[wo_ag_out.opt()]), inc=ccS, drain=False)
    CC_W = ccS.n
    P.wait("gpsimd", ccS, CC_ST)
    P.dma("gpsimd", gD, S_ext[:], stateag_out[0])
    P.wait("vector", gD, gD.n)
    P.op("vector", lambda e: e.tensor_scalar(S_ext[:], S_ext[:], isb_col[:], None, A.mult))
    SEXT_V = P.inc("vector", vS)

    if stop <= 5:
        return _finish()
    h_fm = _view(poolA, 0, [KI, GRP], BF16)
    x2_fm = _view(poolA, KI * 1024, [KD, GRP], BF16)
    ygfm = _view(poolB, 0, [KDIN, GRP], BF16)
    y_ffn = _view(poolB, 0, [KD, GRP], BF16)
    zch = _view(poolC, 0, [DIN], BF16)
    xtmch = _view(poolC, DIN * 2, [DIN], BF16)
    outst = _view(poolC, DIN * 4, [2, 1024], F32)
    outst_dbg = _view(poolC, DIN * 4, [D], F32)
    sinch = _view(poolD, 0, [DIN], BF16)
    x1_fm = _view(poolD, 0, [KD, GRP], BF16)
    gext_t = _view(poolD, DIN * 2 * 2, [DIN], BF16)
    u_grp = _view(poolD, DIN * 2 * 2, [2, GRP], F32)
    y_tm = _view(poolF, 0, [DIN], BF16)
    xdblk = _view(poolF, 8192, [GRP], BF16)
    tB_f = _view(poolF, 8192 + 1024, [GRP], F32)
    gm = _view(poolF, 8192 + 1024 + 2048, [128], BF16)
    mh4 = _view(poolF, 8192 + 1024 + 2048 + 256, [2, GRP], BF16)
    ghb = _view(poolF, 8192 + 1024 + 2048 + 256 + 2048, [2, GRP], BF16)
    acs_sv = _view(poolF, 8192 + 1024 + 2048 + 256 + 4096, [NH], F32)
    xdt = _view(poolF, 8192 + 1024 + 2048 + 256 + 4096 + 256, [GRP], BF16)
    acs_b = _view(poolD, 8192, [2, GRP], F32)
    argb = _view(poolD, 12288, [2, GRP], F32)
    wt_s = [_view(poolF, 0, [22, 128], BF16),
            _view(poolF, 5632, [22, 128], BF16)]
    gsil = _view(poolF, KI * 256, [GRP], BF16)
    sq2 = _view(poolF, KI * 256 + 1024, [GRP], BF16)
    srow = _view(poolF, KI * 256 + 2048, [2, GRP], F32)

    wt_slot_mark = [0, 0]
    out_mark = {0: 0, 1: 0}
    sc_g_mark = 0
    P.wait("tensor", vS, SEXT_V)
    P.wait("sync", gD, gD.n)
    P.wait("sync", scD, scD.n)
    P.wait("gpsimd", vS, SEXT_V)
    zch_mark = 0
    xtm_mark = 0
    sin_mark = 0
    acs_s_mark = 0
    accb_mark = [0, 0]
    argb_mark = [0, 0]
    ghb_mark = [0, 0]
    psY_mark = 0
    ytm_mark = 0
    ug_mark = [0, 0]
    wt_mark = 0
    x1_mark = 0
    for g in range(NGRP):
        for ci in range(CPG):
            c = g * CPG + ci
            P.wait("sync", vS, xtm_mark)
            P.dma("sync", syA, xtmch, x_dram[c])
            P.wait("sync", tS, sin_mark)
            P.wait("sync", vS, x1_mark)
            P.dma("sync", syA, sinch, sin_dram[c])
            P.wait("sync", vS, zch_mark)
            P.dma("sync", syA, zch, z_dram[c * 128:(c + 1) * 128, :])
            RELD = syA.n
            P.wait("gpsimd", syA, RELD)
            P.wait("gpsimd", vS, x1_mark)
            pgap = pg_rep.ap()[:, c, :].unsqueeze(2).broadcast_to([128, NH, HD])
            P.op("gpsimd", lambda e, pgap=pgap: e.tensor_tensor(
                gext_t.rearrange("p (a b) -> p a b", a=NH),
                S_ext.ap().rearrange("p (a b) -> p a b", a=NH), pgap, A.mult))
            P.op("gpsimd", lambda e: e.tensor_tensor(sinch, sinch, gext_t, A.add))
            SC_G = P.inc("gpsimd", gS)
            sc_g_mark = SC_G
            P.wait("tensor", gS, SC_G)
            P.wait("tensor", vF, bank_markv[0])
            P.op("tensor", lambda e, c=c: e.matmul(
                psG[:, 0:128], Bfm[:, c * 128:(c + 1) * 128], Cfm[:, c * 128:(c + 1) * 128],
                start=True, stop=True), drain=False)
            tm = P.inc("tensor", tS)
            P.wait("vector", tS, tm)
            P.op("vector", lambda e: e.tensor_tensor(gm, psG[:, 0:128], triu[:], A.mult), drain=False)
            bank_markv[0] = P.inc("vector", vF)
            GM_V = P.inc("vector", vS)
            P.wait("tensor", vS, GM_V)
            P.wait("sync", vS, acs_s_mark)
            P.dma("sync", syA, acs_sv, acs2_dram[c])
            ACSS = syA.n
            P.wait("vector", syA, ACSS)
            for bb in range(DIN // 512):
                h0 = bb * 8
                dDap = D_rep.ap()[:, h0:h0 + 8].unsqueeze(2).broadcast_to([128, 8, HD])
                dtap = dt_lh.ap()[:, c, h0:h0 + 8].unsqueeze(2).broadcast_to([128, 8, HD])
                xsl = xtmch[:, bb * 512:(bb + 1) * 512].rearrange("p (a b) -> p a b", a=8)
                P.op("vector", lambda e, dDap=dDap, xsl=xsl: e.tensor_tensor(
                    xdblk.rearrange("p (a b) -> p a b", a=8), xsl, dDap, A.mult), drain=False)
                P.op("vector", lambda e, dtap=dtap, xsl=xsl: e.tensor_tensor(
                    xdt.rearrange("p (a b) -> p a b", a=8), xsl, dtap, A.mult), drain=False)
                XY_V = P.inc("vector", vS)
                P.wait("tensor", vS, XY_V)
                P.wait("tensor", vS, psY_mark)
                P.op("tensor", lambda e, c=c, bb=bb: e.matmul(
                    psY1[:, :], Cfm[:, c * 128:(c + 1) * 128], sinch[:, bb * 512:(bb + 1) * 512],
                    start=True, stop=True), drain=False)
                P.op("tensor", lambda e: e.matmul(
                    psY0[:, :], ident[:, :], xdblk[:, :], start=True, stop=False), drain=False)
                P.inc("tensor", tS)
                for sub in range(2):
                    hb = 2 * bb + sub
                    slot = hb % 2
                    P.wait("gpsimd", vS, accb_mark[slot])
                    P.dma("gpsimd", gD, acs_b[:, slot, :],
                          acs_dram[c, 4 * hb:4 * hb + 4].rearrange("a b -> (a b)").unsqueeze(0)
                          .broadcast_to([128, GRP]))
                    gdn = gD.n
                    P.wait("vector", gD, gdn)
                    P.wait("vector", sS, argb_mark[slot])
                    a4 = acs_sv[:, 4 * hb:4 * hb + 4].unsqueeze(2).broadcast_to([128, 4, 128])
                    P.op("vector", lambda e, slot=slot, a4=a4: e.tensor_tensor(
                        argb[:, slot, :].rearrange("p (a b) -> p a b", a=4),
                        acs_b[:, slot, :].rearrange("p (a b) -> p a b", a=4), a4, A.subtract))
                    P.op("vector", lambda e, slot=slot: e.tensor_scalar(
                        argb[:, slot, :], argb[:, slot, :], 0.0, None, A.min), drain=False)
                    accb_mark[slot] = P.inc("vector", vF) * 0 + vS.n
                    vm = P.inc("vector", vS)
                    accb_mark[slot] = vm
                    P.wait("scalar", vS, vm)
                    P.op("scalar", lambda e, slot=slot: e.activation(mh4[:, slot, :], argb[:, slot, :], AF.Exp))
                    argb_mark[slot] = P.inc("scalar", sS)
                    P.wait("vector", sS, argb_mark[slot])
                    P.wait("vector", tS, ghb_mark[slot])
                    gmap = gm.unsqueeze(1).broadcast_to([128, 4, 128])
                    P.op("vector", lambda e, slot=slot, gmap=gmap: e.tensor_tensor(
                        ghb[:, slot, :].rearrange("p (a b) -> p a b", a=4),
                        mh4[:, slot, :].rearrange("p (a b) -> p a b", a=4), gmap, A.mult), drain=False)
                    ghv = P.inc("vector", vS)
                    P.wait("tensor", vS, ghv)
                    for hh in range(4):
                        col = (4 * sub + hh) * HD
                        last = (sub == 1 and hh == 3)
                        P.op("tensor", lambda e, slot=slot, hh=hh, col=col, last=last: e.matmul(
                            psY0[:, col:col + HD], ghb[:, slot, hh * 128:(hh + 1) * 128],
                            xdt[:, col:col + HD], start=False, stop=last), drain=False)
                    ghb_mark[slot] = P.inc("tensor", tS)
                tm = tS.n
                P.wait("vector", tS, tm)
                e0ap = e0_lh.ap()[:, c, h0:h0 + 8].unsqueeze(2).broadcast_to([128, 8, HD])
                P.op("vector", lambda e, e0ap=e0ap: e.tensor_tensor(
                    tB_f.rearrange("p (a b) -> p a b", a=8),
                    psY1.ap().rearrange("p (a b) -> p a b", a=8), e0ap, A.mult))
                P.op("vector", lambda e, bb=bb: e.tensor_tensor(
                    y_tm[:, bb * 512:(bb + 1) * 512], psY0.ap()[:, :], tB_f, A.add), drain=False)
                psY_mark = P.inc("vector", vS)
            YV = P.inc("vector", vS)
            xtm_mark = YV
            acs_s_mark = YV
            P.wait("scalar", vS, YV)
            for qq in range(DIN // GRP):
                P.op("scalar", lambda e, qq=qq: e.activation(
                    sq2, y_tm[:, qq * GRP:(qq + 1) * GRP], AF.Square, accum_out=rms8[:, qq, :]))
            SQ_S = P.inc("scalar", sS)
            P.wait("vector", sS, SQ_S)
            P.op("vector", lambda e: e.reduce_sum(rms_c[:], rms8.ap().rearrange("p a b -> p (a b)"),
                                                  axis=mybir.AxisListType.X))
            P.op("vector", lambda e: e.tensor_scalar(rms_c[:], rms_c[:], 1.0 / DIN, RMS_EPS, A.mult, A.add))
            vm = P.inc("vector", vS)
            P.wait("scalar", vS, vm)
            P.op("scalar", lambda e: e.activation(rms_c[:], rms_c[:], AF.Sqrt))
            sm = P.inc("scalar", sS)
            P.wait("vector", sS, sm)
            P.op("vector", lambda e: e.reciprocal(rms_c[:], rms_c[:]))
            P.op("vector", lambda e: e.scalar_tensor_tensor(y_tm[:, :], y_tm[:, :], rms_c[:], nw_rep[:], A.mult, A.mult))
            P.op("vector", lambda e: e.tensor_tensor(y_tm[:, :], y_tm[:, :], zch[:, :], A.mult), drain=False)
            YG_V = P.inc("vector", vS)
            zch_mark = YG_V
            sin_mark = tS.n
            P.wait("tensor", vS, YG_V)
            for m4 in range(KDIN // 4):
                pt = m4 % 2
                bankT = psT if pt == 0 else psT2
                P.wait("tensor", vS, psT_mark[pt])
                P.wait("tensor", sS, psT_smark[pt])
                for q in range(4):
                    mm = m4 * 4 + q
                    P.op("tensor", lambda e, mm=mm, q=q, bankT=bankT: e.transpose(
                        bankT[:, q * 128:(q + 1) * 128], y_tm[:, mm * 128:(mm + 1) * 128], ident[:]),
                        drain=False)
                tm = P.inc("tensor", tS)
                P.wait("scalar", tS, tm)
                P.op("scalar", lambda e, m4=m4, ci=ci, bankT=bankT: e.copy(
                    ygfm[:, m4 * 4:(m4 + 1) * 4, ci * 128:(ci + 1) * 128],
                    bankT[:, 0:512].rearrange("p (a b) -> p a b", a=4)), drain=False)
                psT_smark[pt] = P.inc("scalar", sS)
                psT_mark[pt] = vS.n
        YGT_S = P.inc("scalar", sS)
        ygt_tmark = tS.n
        P.wait("tensor", sS, YGT_S)
        for mm in range(NMO):
            pu = mm % 2
            P.wait("sync", ccS, CC_W)
            P.wait("sync", vF, ug_mark[pu])
            P.wait("sync", gS, sc_g_mark)
            P.dma("sync", syA, u_grp[:, pu, :],
                  u_fm[mm * 128:(mm + 1) * 128, HALO + g * GRP: HALO + (g + 1) * GRP])
            bi = mm % 2
            bank = psA if bi == 0 else psB
            P.wait("tensor", vF, bank_markv[bi])
            for hf in range(2):
                sl = hf
                P.wait("sync", tS, max(wt_slot_mark[sl], ygt_tmark))
                P.wait("sync", vS, x1_mark)
                P.dma("sync", syW, wt_s[sl][:, 0:KDIN // 2, :],
                      wo_ag_out[mm // WO_MT, mm % WO_MT, :, hf * (KDIN // 2):(hf + 1) * (KDIN // 2), :])
                P.wait("tensor", syW)
                for k in range(KDIN // 2):
                    kk = hf * (KDIN // 2) + k
                    P.op("tensor", lambda e, k=k, kk=kk, sl=sl, bank=bank: e.matmul(
                        bank[:, :], wt_s[sl][:, k, :], ygfm[:, kk, :],
                        start=(kk == 0), stop=(kk == KDIN - 1)), drain=False)
                wt_slot_mark[sl] = P.inc("tensor", tS)
            tm = tS.n
            P.wait("vector", tS, tm)
            P.wait("vector", syA)
            P.op("vector", lambda e, mm=mm, pu=pu, bank=bank: e.tensor_tensor(
                x1_fm[:, mm, :], u_grp[:, pu, :], bank[:, :], A.add), drain=False)
            if debug:
                P.wait("vector", gD, gD.n)
                P.op("vector", lambda e, bank=bank: e.tensor_copy(outst_dbg[:, 0:GRP], bank[:, :]))
                vm = P.inc("vector", vS)
                P.wait("gpsimd", vS, vm)
                P.dma("gpsimd", gD, mix_dram[mm * 128:(mm + 1) * 128, g * GRP:(g + 1) * GRP], outst_dbg[:, 0:GRP])
            ug_mark[pu] = P.inc("vector", vF)
            bank_markv[bi] = ug_mark[pu]
        X1_V = P.inc("vector", vS)
        P.wait("tensor", vS, X1_V)
        P.wait("tensor", vS, psY_mark)
        for k in range(KD):
            P.op("tensor", lambda e, k=k: e.matmul(
                psSt[0:1, :], ones_col[:], x1_fm[:, k, :], start=(k == 0), stop=(k == KD - 1)), drain=False)
        tm = P.inc("tensor", tS)
        P.wait("scalar", vS, X1_V)
        for k in range(KD):
            if k > 0:
                P.wait("scalar", tS, sq_tmark)
            P.op("scalar", lambda e, k=k: e.activation(sq2, x1_fm[:, k, :], AF.Square))
            sm = P.inc("scalar", sS)
            P.wait("tensor", sS, sm)
            P.op("tensor", lambda e, k=k: e.matmul(
                psG[0:1, :], ones_col[:], sq2, start=(k == 0), stop=(k == KD - 1)), drain=False)
            sq_tmark = P.inc("tensor", tS)
        tm = sq_tmark
        P.wait("vector", tS, tm)
        P.op("vector", lambda e: e.tensor_scalar(srow[0:1, 0, :], psSt[0:1, :], 1.0 / D, None, A.mult))
        P.op("vector", lambda e: e.tensor_scalar(srow[0:1, 1, :], psG[0:1, :], 1.0 / D, None, A.mult))
        P.op("vector", lambda e: e.tensor_tensor(psY0.ap()[0:1, :], srow[0:1, 0, :], srow[0:1, 0, :], A.mult))
        P.op("vector", lambda e: e.tensor_tensor(srow[0:1, 1, :], srow[0:1, 1, :], psY0.ap()[0:1, :], A.subtract))
        P.op("vector", lambda e: e.tensor_scalar(srow[0:1, 1, :], srow[0:1, 1, :], 1.0, LN_EPS, A.mult, A.add))
        vm = P.inc("vector", vS)
        P.wait("scalar", vS, vm)
        P.op("scalar", lambda e: e.activation(srow[0:1, 1, :], srow[0:1, 1, :], AF.Sqrt))
        sm = P.inc("scalar", sS)
        P.wait("vector", sS, sm)
        P.op("vector", lambda e: e.reciprocal(srow[0:1, 1, :], srow[0:1, 1, :]))
        vm = P.inc("vector", vS)
        P.wait("tensor", vS, vm)
        P.op("tensor", lambda e: e.matmul(psY0[:, :], ones_row32[:], srow[0:1, 0, :], start=True, stop=True), drain=False)
        P.op("tensor", lambda e: e.matmul(psY1[:, :], ones_row32[:], srow[0:1, 1, :], start=True, stop=True), drain=False)
        tm = P.inc("tensor", tS)
        P.wait("vector", tS, tm)
        for k in range(KD):
            P.op("vector", lambda e, k=k: e.tensor_tensor(x2_fm[:, k, :], x1_fm[:, k, :], psY0[:, :], A.subtract))
            P.op("vector", lambda e, k=k: e.tensor_tensor(x2_fm[:, k, :], x2_fm[:, k, :], psY1[:, :], A.mult), drain=False)
        X2_V = P.inc("vector", vS)
        psY_mark = X2_V
        P.wait("tensor", vS, X2_V)
        for j in range(MGU // 2):
            P.wait("tensor", sF, bank_mark[0])
            P.wait("tensor", vF, bank_markv[1])
            for half, jj, bank in ((0, j, psA), (1, j + MGU // 2, psB)):
                sl = half
                P.wait("sync", tS, wt_slot_mark[sl])
                P.dma("sync", syW, wt_s[sl][:, 0:KD, :], gu_ag_out[jj // GU_MT, jj % GU_MT])
                P.wait("tensor", syW)
                for k in range(KD):
                    P.op("tensor", lambda e, k=k, sl=sl, bank=bank: e.matmul(
                        bank[:, :], wt_s[sl][:, k, :], x2_fm[:, k, :],
                        start=(k == 0), stop=(k == KD - 1)), drain=False)
                wt_slot_mark[sl] = P.inc("tensor", tS)
            tm = tS.n
            P.wait("scalar", tS, tm)
            P.op("scalar", lambda e: e.activation(gsil, psA[:, :], AF.Silu))
            bank_mark[0] = P.inc("scalar", sF)
            sm = P.inc("scalar", sS)
            P.wait("vector", sS, sm)
            P.op("vector", lambda e, j=j: e.tensor_tensor(h_fm[:, j, :], gsil, psB[:, :], A.mult), drain=False)
            bank_markv[1] = P.inc("vector", vF)
        H_V = P.inc("vector", vS)
        P.wait("tensor", vS, H_V)
        for mm in range(NMO):
            bi = mm % 2
            bank = psA if bi == 0 else psB
            P.wait("tensor", vF, bank_markv[bi])
            for hf in range(2):
                sl = hf
                k0 = hf * (KI // 2)
                kn = (KI // 2) if hf == 0 else (KI - KI // 2)
                P.wait("sync", tS, wt_slot_mark[sl])
                P.dma("sync", syW, wt_s[sl][:, 0:kn, :],
                      dn_ag_out[mm // DN_MT, mm % DN_MT, :, k0:k0 + kn, :])
                P.wait("tensor", syW)
                for k in range(kn):
                    kk = k0 + k
                    P.op("tensor", lambda e, k=k, kk=kk, sl=sl, bank=bank: e.matmul(
                        bank[:, :], wt_s[sl][:, k, :], h_fm[:, kk, :],
                        start=(kk == 0), stop=(kk == KI - 1)), drain=False)
                wt_slot_mark[sl] = P.inc("tensor", tS)
            tm = tS.n
            P.wait("vector", tS, tm)
            P.op("vector", lambda e, mm=mm, bank=bank: e.tensor_copy(y_ffn[:, mm, :], bank[:, :]), drain=False)
            bank_markv[bi] = P.inc("vector", vF)
        YF_V = P.inc("vector", vS)
        P.wait("scalar", vS, YF_V)
        for k in range(KD):
            if k > 0:
                P.wait("scalar", tS, sq_tmark)
            P.op("scalar", lambda e, k=k: e.activation(sq2, y_ffn[:, k, :], AF.Square))
            sm = P.inc("scalar", sS)
            P.wait("tensor", sS, sm)
            P.op("tensor", lambda e, k=k: e.matmul(
                psSt[0:1, :], ones_col[:], sq2, start=(k == 0), stop=(k == KD - 1)), drain=False)
            sq_tmark = P.inc("tensor", tS)
        tm = sq_tmark
        P.wait("vector", tS, tm)
        P.op("vector", lambda e: e.tensor_scalar(srow[0:1, 0, :], psSt[0:1, :], 1.0 / D, FFN_EPS, A.mult, A.add))
        vm = P.inc("vector", vS)
        P.wait("scalar", vS, vm)
        P.op("scalar", lambda e: e.activation(srow[0:1, 0, :], srow[0:1, 0, :], AF.Sqrt))
        sm = P.inc("scalar", sS)
        P.wait("vector", sS, sm)
        P.op("vector", lambda e: e.reciprocal(srow[0:1, 0, :], srow[0:1, 0, :]))
        vm = P.inc("vector", vS)
        P.wait("tensor", vS, vm)
        P.op("tensor", lambda e: e.matmul(psY0[:, :], ones_row32[:], srow[0:1, 0, :], start=True, stop=True), drain=False)
        tm = P.inc("tensor", tS)
        P.wait("vector", tS, tm)
        for k in range(KD):
            P.op("vector", lambda e, k=k: e.tensor_tensor(y_ffn[:, k, :], y_ffn[:, k, :], psY0[:, :], A.mult))
            P.op("vector", lambda e, k=k: e.tensor_tensor(y_ffn[:, k, :], y_ffn[:, k, :], x1_fm[:, k, :], A.add), drain=False)
        S_V = P.inc("vector", vS)
        psY_mark = S_V
        x1_mark = S_V
        sin_mark = S_V
        if debug:
            P.wait("vector", gD, gD.n)
        P.wait("tensor", vS, S_V)
        for tt in range(CPG):
            for half in range(2):
                o_eng, o_sem = ("scalar", scD) if half == 0 else ("sync", syA)
                P.wait("vector", o_sem, out_mark[half])
                for q in range(8):
                    mm = half * 8 + q
                    pt = mm % 2
                    P.wait("tensor", vS, psT_mark[pt])
                    P.op("tensor", lambda e, tt=tt, mm=mm, pt=pt: e.transpose(
                        (psT if pt == 0 else psT2)[:, 0:128], y_ffn[:, mm, tt * 128:(tt + 1) * 128], ident[:]),
                        drain=False)
                    tm = P.inc("tensor", tS)
                    P.wait("vector", tS, tm)
                    P.op("vector", lambda e, q=q, half=half, pt=pt: e.tensor_copy(
                        outst[:, half, q * 128:(q + 1) * 128], (psT if pt == 0 else psT2)[:, 0:128]), drain=False)
                    psT_mark[pt] = P.inc("vector", vS)
                OW_V = P.inc("vector", vS)
                P.wait(o_eng, vS, OW_V)
                out_mark[half] = P.dma(o_eng, o_sem,
                                       out_t[g * GRP + tt * 128: g * GRP + (tt + 1) * 128,
                                             half * 1024:(half + 1) * 1024],
                                       outst[:, half, :])

    return _finish()


def host_prepare(inputs):
    f32 = np.float32
    u = np.asarray(inputs["L0_attn_u"], f32)
    W_in = np.asarray(inputs["L0_attn_W_in"], f32)
    conv_w = np.asarray(inputs["L0_attn_conv_w"], f32)
    conv_b = np.asarray(inputs["L0_attn_conv_b"], f32)
    A_log = np.asarray(inputs["L0_attn_A_log"], f32)
    Dv = np.asarray(inputs["L0_attn_D"], f32)
    dt_bias = np.asarray(inputs["L0_attn_dt_bias"], f32)
    norm_w = np.asarray(inputs["L0_attn_norm_w"], f32)
    W_out = np.asarray(inputs["L0_attn_W_out"], f32)
    gup = np.asarray(inputs["L0_ffn_gate_up_packed"], np.int32)
    gus = np.asarray(inputs["L0_ffn_gate_up_scales"], np.float16)
    dnp = np.asarray(inputs["L0_ffn_down_packed"], np.int32)
    dns = np.asarray(inputs["L0_ffn_down_scales"], np.float16)

    WT = W_in.T
    w_z = np.ascontiguousarray(
        np.concatenate([WT[:, 0:ZC], WT[:, ZC + CONVD:]], axis=1)).reshape(KD, 128, ZC + DTC)
    w_xbc = np.ascontiguousarray(
        WT[:, ZC:ZC + CONVD].reshape(KD, 128, MXBC, 128).transpose(2, 0, 1, 3))
    wo_t = np.ascontiguousarray(W_out.T.reshape(KDIN, 128, NMO, 128).transpose(2, 0, 1, 3))
    gu16 = np.ascontiguousarray(gup.view(np.uint16).reshape(2 * I5, D // 4).T)
    gu16 = gu16.reshape(KD, 32, 2 * I5)
    dn16 = np.ascontiguousarray(dnp.view(np.uint16).reshape(D, I5 // 4).T)
    dn16 = dn16.reshape(KI, 32, D)
    cwp = np.ascontiguousarray(conv_w.reshape(MXBC, 128, 4).transpose(1, 0, 2))
    cbp = np.ascontiguousarray(conv_b.reshape(MXBC, 128).T)
    nwrep = np.ascontiguousarray(np.broadcast_to(norm_w[None, :], (128, DIN)))

    in_maps = []
    for c in range(8):
        b, h = c // 2, c % 2
        t0 = h * T
        if h == 0:
            u_ext = np.concatenate([np.zeros((HALO, D), f32), u[b, 0:T]], axis=0)
        else:
            u_ext = u[b, t0 - HALO:t0 + T]
        u_ext = np.ascontiguousarray(u_ext)
        in_maps.append({
            "u_fm": np.ascontiguousarray(u_ext.T),
            "u_tm": u_ext,
            "w_z": w_z,
            "w_xbc": w_xbc,
            "w_out_sh": np.ascontiguousarray(wo_t[2 * c:2 * c + 2]),
            "gu_w16": np.ascontiguousarray(gu16[:, :, c * GU_SH:(c + 1) * GU_SH]).view(np.int16),
            "gu_scl": np.ascontiguousarray(np.asarray(gus)[:, c * GU_SH:(c + 1) * GU_SH]),
            "dn_w16": np.ascontiguousarray(dn16[:, :, c * DN_SH:(c + 1) * DN_SH]).view(np.int16),
            "dn_scl": np.ascontiguousarray(np.asarray(dns)[:, c * DN_SH:(c + 1) * DN_SH]),
            "conv_w": cwp,
            "conv_b": cbp,
            "a_log": np.ascontiguousarray(A_log[:, None]),
            "dt_bias": np.ascontiguousarray(dt_bias[:, None]),
            "d_col_i": np.ascontiguousarray(Dv[:, None]),
            "d_row_i": np.ascontiguousarray(Dv[None, :]),
            "nw_rep_i": nwrep,
            "is_b": np.array([[float(h)]], f32),
        })
    return in_maps


_NC_CACHE = {}


def kernel(**inputs):
    in_maps = host_prepare(inputs)
    if "nc" not in _NC_CACHE:
        _NC_CACHE["nc"] = build(debug=0)
    nc = _NC_CACHE["nc"]
    res = run_bass_kernel_spmd(nc, in_maps, core_ids=list(range(8)))
    out = np.empty((4, 2 * T, D), np.float32)
    for c in range(8):
        b, h = c // 2, c % 2
        out[b, h * T:(h + 1) * T, :] = res.results[c]["out"]
    return out


# version 18 (final)
# speedup vs baseline: 1.0270x; 1.0217x over previous
import numpy as np
from contextlib import ExitStack

import concourse.bass as bass
import concourse.mybir as mybir
from concourse.bass_utils import run_bass_kernel_spmd

F32 = mybir.dt.float32
BF16 = mybir.dt.bfloat16
F16 = mybir.dt.float16
I16 = mybir.dt.int16
U16 = mybir.dt.uint16
I32 = mybir.dt.int32
U8 = mybir.dt.uint8
A = mybir.AluOpType
AF = mybir.ActivationFunctionType

D = 2048
T = 2048
HALO = 3
TH = T + HALO
DIN = 4096
NH = 64
HD = 64
DS = 128
CONVD = 4352
ZC = 4096
DTC = 64
KD = 16
NT = 16
NTE = 17
CH = 128
NCH = 16
GRP = 512
NGRP = 4
CPG = 4
MXBC = 34
KDIN = 32
I5 = 5632
KI = 44
MGU = 88
NMO = 16
ZBLK = 1024
NZB = 4
LN_EPS = 1e-5
RMS_EPS = 1.1920929e-07
FFN_EPS = 1e-6
GU_SH = 1408
GU_MT = 11
DN_SH = 256
DN_MT = 2
WO_MT = 2


class Sem:
    def __init__(self, h):
        self.h = h
        self.n = 0


class Prog:
    ENGS = ("sync", "tensor", "vector", "scalar", "gpsimd")

    def __init__(self):
        self.q = {e: [] for e in self.ENGS}

    def dma(self, eng, sem, out, in_, **kw):
        if sem.n >= 16:
            self.q[eng].append(lambda e, s=sem, vv=sem.n: e.wait_ge(s.h, vv))
        sem.n += 16
        self.q[eng].append(lambda e, o=out, i=in_, s=sem, kw=kw: e.dma_start(out=o, in_=i, **kw).then_inc(s.h, 16))
        return sem.n

    def op(self, eng, fn, inc=None, drain=True):
        if inc is not None:
            inc.n += 1
            g = lambda e, f=fn, s=inc: f(e).then_inc(s.h, 1)
            g._has_inc = True
            g._is_op = True
            self.q[eng].append(g)
        else:
            fn._is_op = True
            self.q[eng].append(fn)
        if drain and eng in ("vector", "scalar", "gpsimd"):
            d = lambda e: e.drain()
            d._is_drain = True
            self.q[eng].append(d)
        return inc.n if inc is not None else None

    def wait(self, eng, sem, val=None):
        v = sem.n if val is None else val
        if v > 0:
            self.q[eng].append(lambda e, s=sem, vv=v: e.wait_ge(s.h, vv))
        return v

    def inc(self, eng, sem, n=1):
        sem.n += n
        last = self.q[eng][-1] if self.q[eng] else None
        if (last is not None and getattr(last, "_is_op", False)
                and not getattr(last, "_has_inc", False)):
            prev = self.q[eng].pop()
            fn = lambda e, p=prev, s=sem, nn=n: p(e).then_inc(s.h, nn)
            fn._has_inc = True
            fn._is_op = True
            self.q[eng].append(fn)
        else:
            if eng in ("vector", "scalar", "gpsimd") and last is not None \
                    and not getattr(last, "_is_drain", False):
                d = lambda e: e.drain()
                d._is_drain = True
                self.q[eng].append(d)
            self.q[eng].append(lambda e, s=sem, nn=n: e.nop().then_inc(s.h, nn))
        return sem.n


def _view(pool, off, shape, dtype):
    esz = mybir.dt.size(dtype)
    n = int(np.prod(shape)) * esz
    assert off % 4 == 0
    ap = pool.ap()[:, off:off + n].bitcast(dtype)
    if len(shape) == 1:
        return ap
    spec = " ".join(f"a{i}" for i in range(len(shape)))
    kw = {f"a{i}": int(shape[i]) for i in range(len(shape))}
    return ap.rearrange(f"p ({spec}) -> p {spec}", **kw)


def build(debug=0, stop=99, xbc_m=36, dis=()):
    nc = bass.Bass("TRN2", target_bir_lowering=False)
    P = Prog()
    es = ExitStack()

    def dr(name, shape, dt, kind=None):
        if kind:
            return nc.dram_tensor(name, shape, dt, kind=kind).ap()
        return nc.dram_tensor(name, shape, dt).ap()

    u_fm = dr("u_fm", [D, TH], F32, "ExternalInput")
    u_tm = dr("u_tm", [TH, D], F32, "ExternalInput")
    w_z = dr("w_z", [KD, 128, ZC + DTC], F32, "ExternalInput")
    w_xbc = dr("w_xbc", [MXBC, KD, 128, 128], F32, "ExternalInput")
    w_out_sh = dr("w_out_sh", [WO_MT, KDIN, 128, 128], F32, "ExternalInput")
    gu_w16 = dr("gu_w16", [KD, 32, GU_SH], I16, "ExternalInput")
    gu_scl = dr("gu_scl", [KD, GU_SH], F16, "ExternalInput")
    dn_w16 = dr("dn_w16", [KI, 32, DN_SH], I16, "ExternalInput")
    dn_scl = dr("dn_scl", [KI, DN_SH], F16, "ExternalInput")
    conv_w_i = dr("conv_w", [128, MXBC, 4], F32, "ExternalInput")
    conv_b_i = dr("conv_b", [128, MXBC], F32, "ExternalInput")
    a_log_i = dr("a_log", [NH, 1], F32, "ExternalInput")
    dt_bias_i = dr("dt_bias", [NH, 1], F32, "ExternalInput")
    d_col_i = dr("d_col_i", [NH, 1], F32, "ExternalInput")
    nw_rep_i = dr("nw_rep_i", [128, DIN], F32, "ExternalInput")
    d_row_i = dr("d_row_i", [1, NH], F32, "ExternalInput")
    is_b_i = dr("is_b", [1, 1], F32, "ExternalInput")
    out_t = dr("out", [T, D], F32, "ExternalOutput")

    dbg = "ExternalOutput" if debug else None
    z_dram = dr("z_dram", [T, ZC], BF16, dbg)
    x_dram = dr("x_dram", [NCH, CH, DIN], BF16, dbg)
    st_dram = dr("st_dram", [NCH, DS, DIN], BF16, dbg)
    sin_dram = dr("sin_dram", [NCH, DS, DIN], BF16, dbg)
    stats_dram2 = dr("stats_dram2", [2 * NTE, 128], BF16)
    vecs_dram = dr("vecs_dram", [2, NCH, NH], F32)
    acs_dram = dr("acs_dram", [NCH, NH, CH], F32)
    acs2_dram = dr("acs2_dram", [NCH, CH, NH], F32)
    sclb_dram = dr("sclb_dram", [KD + KI, GU_SH], BF16)
    stateag_in = dr("stateag_in", [DS, DIN], F32)
    stateag_out = dr("stateag_out", [2, DS, DIN], F32)
    gu_ag_in = dr("gu_ag_in", [GU_MT, 128, KD, 128], BF16)
    gu_ag_out = dr("gu_ag_out", [8, GU_MT, 128, KD, 128], BF16)
    dn_ag_in = dr("dn_ag_in", [DN_MT, 128, KI, 128], BF16)
    dn_ag_out = dr("dn_ag_out", [8, DN_MT, 128, KI, 128], BF16)
    wo_ag_in = dr("wo_ag_in", [WO_MT, 128, KDIN, 128], BF16)
    wo_ag_out = dr("wo_ag_out", [8, WO_MT, 128, KDIN, 128], BF16)
    if debug:
        mix_dram = dr("mix_dram", [D, T], F32, "ExternalOutput")
        dbg_dram = dr("dbg_dram", [6, 128, NCH * NH], F32, "ExternalOutput")

    sb = lambda n, s, d: es.enter_context(nc.sbuf_tensor(n, s, d))
    poolA = sb("poolA", [128, 65664], U8)
    poolB = sb("poolB", [128, 33536], U8)
    poolC = sb("poolC", [128, 26624], U8)
    poolD = sb("poolD", [128, 25600], U8)
    poolF = sb("poolF", [128, 17536], U8)

    Bfm = sb("Bfm", [128, T], BF16)
    Cfm = sb("Cfm", [128, T], BF16)
    ds_lh = sb("ds_lh", [128, NCH, NH], BF16)
    e0_lh = sb("e0_lh", [128, NCH, NH], BF16)
    dt_lh = sb("dt_lh", [128, NCH, NH], BF16)
    D_rep = sb("D_rep", [128, NH], F32)
    acs_lc = sb("acs_lc", [128, NH], F32)
    pg_rep = sb("pg_rep", [128, NCH, NH], BF16)
    S_ext = sb("S_ext", [DS, DIN], BF16)
    nw_rep = sb("nw_rep", [128, DIN], BF16)
    cw = sb("cw", [128, MXBC, 4], F32)
    cb = sb("cb", [128, MXBC], F32)
    ident = sb("ident", [128, 128], BF16)
    ident32 = sb("ident32", [128, 128], F32)
    triu = sb("triu", [128, 128], BF16)
    ones_col = sb("ones_col", [128, 1], BF16)
    ones_row32 = sb("ones_row32", [1, 128], F32)
    A_col = sb("A_col", [NH, 1], F32)
    dtb_col = sb("dtb_col", [NH, 1], F32)
    d_col = sb("d_col", [NH, 1], F32)
    isb_col = sb("isb_col", [128, 1], F32)
    amt16 = sb("amt16", [128, 1], I16)
    amt32 = sb("amt32", [128, 1], I32)
    cs_h = sb("cs_h", [NH, NCH], F32)
    pgl_h = sb("pgl_h", [NH, NCH], F32)
    acs63 = sb("acs63", [NH, 1], F32)
    csT = sb("csT", [NCH, NH], F32)
    pgT = sb("pgT", [NCH, NH], F32)
    sttT = sb("sttT", [2 * NTE, 128], BF16)
    stat_s = sb("stat_s", [128, NTE, 1], F32)
    stat_q = sb("stat_q", [128, NTE, 1], F32)
    stat_t = sb("stat_t", [128, NTE, 1], F32)
    stat_sb = sb("stat_sb", [128, NTE, 2], BF16)
    rms_c = sb("rms_c", [128, 1], F32)
    rms8 = sb("rms8", [128, 8, 1], F32)
    gsc_f = _view(poolF, 0, [GU_SH], F16)[0:KD, :]
    gsc_b = _view(poolF, 2816, [GU_SH], BF16)[0:KD, :]
    dsc_f = _view(poolF, 5632, [DN_SH], F16)[0:KI, :]
    dsc_b = _view(poolF, 6144, [DN_SH], BF16)[0:KI, :]

    ps = lambda n, s, d: es.enter_context(nc.psum_tensor(n, s, d))
    Btm = _view(poolB, 18944, [NCH, 128], BF16)
    psA = ps("psA", [128, 512], F32)
    psB = ps("psB", [128, 512], F32)
    psT = ps("psT", [128, 1024], BF16)
    psT2 = ps("psT2", [128, 1024], BF16)
    psG = ps("psG", [128, 512], F32)
    psY0 = ps("psY0", [128, 512], F32)
    psY1 = ps("psY1", [128, 512], F32)
    psSt = ps("psSt", [128, 512], F32)

    mk = lambda n: Sem(es.enter_context(nc.semaphore(n)))
    syW = mk("syW"); syA = mk("syA"); scD = mk("scD"); gD = mk("gD"); dqD = mk("dqD")
    tS = mk("tS"); vS = mk("vS"); sS = mk("sS"); gS = mk("gS"); ccS = mk("ccS")
    vF = mk("vF"); sF = mk("sF"); tF = mk("tF"); gF = mk("gF")

    blk = es.enter_context(nc.Block())

    def _finish():
        P.wait("scalar", scD, scD.n)
        P.wait("gpsimd", gD, gD.n)
        P.wait("sync", syA, syA.n)
        P.wait("sync", syW, syW.n)
        P.wait("vector", vS, 0)

        @blk.sync
        def _(e):
            for f in P.q["sync"]:
                f(e)

        @blk.tensor
        def _(e):
            for f in P.q["tensor"]:
                f(e)

        @blk.vector
        def _(e):
            for f in P.q["vector"]:
                f(e)

        @blk.scalar
        def _(e):
            for f in P.q["scalar"]:
                f(e)

        @blk.gpsimd
        def _(e):
            for f in P.q["gpsimd"]:
                f(e)

        es.close()
        return nc

    P.op("gpsimd", lambda e: e.memset(ident[:], 0.0))
    P.op("gpsimd", lambda e: e.affine_select(out=ident[:], in_=ident[:], compare_op=A.not_equal,
                                             fill=1.0, base=0, pattern=[[-1, 128]], channel_multiplier=1))
    P.op("gpsimd", lambda e: e.memset(ident32[:], 0.0))
    P.op("gpsimd", lambda e: e.affine_select(out=ident32[:], in_=ident32[:], compare_op=A.not_equal,
                                             fill=1.0, base=0, pattern=[[-1, 128]], channel_multiplier=1))
    P.op("gpsimd", lambda e: e.memset(triu[:], 0.0))
    P.op("gpsimd", lambda e: e.affine_select(out=triu[:], in_=triu[:], compare_op=A.is_gt,
                                             fill=1.0, base=0, pattern=[[-1, 128]], channel_multiplier=1))
    P.op("gpsimd", lambda e: e.memset(ones_col[:], 1.0))
    P.op("gpsimd", lambda e: e.memset(ones_row32[:], 1.0))
    P.op("gpsimd", lambda e: e.memset(stat_s[:], 0.0))
    P.op("gpsimd", lambda e: e.memset(stat_q[:], 0.0))
    P.op("gpsimd", lambda e: e.iota(amt32[:], pattern=[[0, 1]], base=0, channel_multiplier=1))
    CONST_G = P.inc("gpsimd", gS)
    P.wait("vector", gS, CONST_G)
    P.op("vector", lambda e: e.tensor_scalar(amt32[:], amt32[:], 3, None, A.bitwise_and))
    P.op("vector", lambda e: e.tensor_scalar(amt32[:], amt32[:], 4, None, A.mult))
    P.op("vector", lambda e: e.tensor_copy(amt16[:], amt32[:]))
    AMT_V = P.inc("vector", vS)

    P.dma("sync", syA, cw[:], conv_w_i)
    P.dma("sync", syA, cb[:], conv_b_i)
    P.dma("sync", syA, A_col[:], a_log_i)
    P.dma("sync", syA, dtb_col[:], dt_bias_i)
    P.dma("sync", syA, d_col[:], d_col_i)
    P.dma("sync", syA, isb_col[:], is_b_i.broadcast_to([128, 1]))
    P.dma("sync", syA, D_rep[:], d_row_i.broadcast_to([128, NH]))
    P.dma("sync", syA, gsc_f, gu_scl)
    P.dma("sync", syA, dsc_f, dn_scl)
    TINY_LD = syA.n
    nwst = _view(poolD, 0, [2, ZBLK], F32)
    P.wait("sync", syA, TINY_LD)
    nwmark = [0, 0]
    for i in range(4):
        if i >= 2:
            P.wait("sync", vF, nwmark[i % 2])
        P.dma("sync", syA, nwst[:, i % 2, :], nw_rep_i[:, i * ZBLK:(i + 1) * ZBLK])
        P.wait("vector", syA)
        P.op("vector", lambda e, i=i: e.tensor_copy(nw_rep[:, i * ZBLK:(i + 1) * ZBLK], nwst[:, i % 2, :]))
        nwmark[i % 2] = P.inc("vector", vF)
    P.wait("scalar", syA, TINY_LD)
    P.op("scalar", lambda e: e.activation(A_col[:], A_col[:], AF.Exp))
    P.op("scalar", lambda e: e.mul(A_col[:], A_col[:], -1.0))
    A_S = P.inc("scalar", sS)
    P.wait("vector", syA, TINY_LD)
    P.op("vector", lambda e: e.tensor_scalar(gsc_b, gsc_f, 0.25, None, A.mult))
    P.op("vector", lambda e: e.tensor_scalar(dsc_b, dsc_f, 0.25, None, A.mult))
    SCL_V = P.inc("vector", vS)
    P.wait("gpsimd", vS, SCL_V)
    P.dma("gpsimd", gD, sclb_dram[0:KD, :], gsc_b)
    P.dma("gpsimd", gD, sclb_dram[KD:KD + KI, 0:DN_SH], dsc_b)
    P.wait("gpsimd", gD, gD.n)


    dq_wrep = _view(poolF, 0, [2, 704], I16)
    dq_srep = _view(poolF, 2816, [2, 704], BF16)
    DQH = 704
    dq_a = _view(poolF, 5632, [DQH], I16)
    dq_b = _view(poolF, 5632 + 1408, [DQH], I16)
    dq_c = _view(poolF, 5632 + 2816, [DQH], I16)
    dq_d = _view(poolF, 5632 + 4224, [DQH], I16)
    dq_v = _view(poolF, 5632 + 5632, [DQH], BF16)
    dq_o = _view(poolF, 5632 + 7040, [GU_SH], BF16)
    dq_state = {"mark": [0, 0], "si": 0, "first": True, "sclb_gate": 0}

    def dq_steps():
        amt_b = amt16.ap().broadcast_to([128, DQH])

        def decode(sl, wd, ow):
            if dq_state["first"]:
                P.wait("vector", gD, dq_state["sclb_gate"])
                dq_state["first"] = False
            P.op("vector", lambda e: e.tensor_tensor(
                dq_a[:, 0:wd], dq_wrep[:, sl, 0:wd], amt_b[:, 0:wd], A.logical_shift_right))
            P.op("vector", lambda e: e.tensor_scalar(dq_a[:, 0:wd], dq_a[:, 0:wd], 15, None, A.bitwise_and))
            P.op("vector", lambda e: e.tensor_scalar(dq_b[:, 0:wd], dq_a[:, 0:wd], 7, None, A.bitwise_and))
            P.op("vector", lambda e: e.tensor_scalar(dq_c[:, 0:wd], dq_b[:, 0:wd], 1, None, A.logical_shift_right))
            P.op("vector", lambda e: e.tensor_scalar(dq_d[:, 0:wd], dq_b[:, 0:wd], 1, None, A.bitwise_and))
            P.op("vector", lambda e: e.tensor_scalar(dq_d[:, 0:wd], dq_d[:, 0:wd], 2, None, A.add))
            P.op("vector", lambda e: e.tensor_tensor(dq_d[:, 0:wd], dq_d[:, 0:wd], dq_c[:, 0:wd], A.logical_shift_left))
            P.op("vector", lambda e: e.tensor_scalar(dq_c[:, 0:wd], dq_b[:, 0:wd], -1, 2, A.mult, A.add))
            P.op("vector", lambda e: e.tensor_scalar(dq_c[:, 0:wd], dq_c[:, 0:wd], 0, None, A.max))
            P.op("vector", lambda e: e.tensor_tensor(dq_d[:, 0:wd], dq_d[:, 0:wd], dq_c[:, 0:wd], A.subtract))
            P.op("vector", lambda e: e.tensor_scalar(dq_c[:, 0:wd], dq_a[:, 0:wd], 8, None, A.bitwise_and))
            P.op("vector", lambda e: e.tensor_scalar(dq_c[:, 0:wd], dq_c[:, 0:wd], 2, None, A.logical_shift_right))
            P.op("vector", lambda e: e.tensor_scalar(dq_c[:, 0:wd], dq_c[:, 0:wd], -1, 1, A.mult, A.add))
            P.op("vector", lambda e: e.tensor_tensor(dq_d[:, 0:wd], dq_d[:, 0:wd], dq_c[:, 0:wd], A.mult))
            P.op("vector", lambda e: e.tensor_copy(dq_v[:, 0:wd], dq_d[:, 0:wd]))
            P.op("vector", lambda e: e.tensor_tensor(ow, dq_v[:, 0:wd], dq_srep[:, sl, 0:wd], A.mult))
            dq_state["mark"][sl] = P.inc("vector", vS)

        for k in range(KD):
            def step(k=k):
                for hh in range(2):
                    sl = dq_state["si"] % 2
                    dq_state["si"] += 1
                    c0 = hh * DQH
                    P.wait("sync", vS, dq_state["mark"][sl])
                    P.dma("sync", dqD, dq_wrep[:, sl, :],
                          gu_w16[k][:, c0:c0 + DQH].unsqueeze(1).broadcast_to([32, 4, DQH]))
                    P.dma("sync", dqD, dq_srep[:, sl, :],
                          sclb_dram[k:k + 1, c0:c0 + DQH].broadcast_to([128, DQH]))
                    P.wait("vector", dqD, dqD.n)
                    decode(sl, DQH, dq_o[:, c0:c0 + DQH])
                P.wait("sync", vS, dq_state["mark"][(dq_state["si"] - 1) % 2])
                for j in range(GU_SH // 128):
                    P.dma("sync", dqD, gu_ag_in[j, :, k, :], dq_o[:, j * 128:(j + 1) * 128])
            yield step
        for k in range(0, KI, 2):
            def step(k=k):
                sl = dq_state["si"] % 2
                dq_state["si"] += 1
                P.wait("sync", vS, dq_state["mark"][sl])
                for q in range(2):
                    P.dma("sync", dqD, dq_wrep[:, sl, q * DN_SH:(q + 1) * DN_SH],
                          dn_w16[k + q].unsqueeze(1).broadcast_to([32, 4, DN_SH]))
                    P.dma("sync", dqD, dq_srep[:, sl, q * DN_SH:(q + 1) * DN_SH],
                          sclb_dram[KD + k + q:KD + k + q + 1, 0:DN_SH].broadcast_to([128, DN_SH]))
                P.wait("vector", dqD, dqD.n)
                decode(sl, 2 * DN_SH, dq_o[:, 0:2 * DN_SH])
                P.wait("sync", vS, dq_state["mark"][sl])
                for q in range(2):
                    for j in range(DN_SH // 128):
                        P.dma("sync", dqD, dn_ag_in[j, :, k + q, :],
                              dq_o[:, q * DN_SH + j * 128: q * DN_SH + (j + 1) * 128])
            yield step
        for j in range(WO_MT):
            def step2(j=j):
                for kq in range(KDIN // 8):
                    P.dma("gpsimd", gD, wo_ag_in[j, :, kq * 8:(kq + 1) * 8, :],
                          w_out_sh[j, kq * 8:(kq + 1) * 8].transpose([1, 0, 2]))
            yield step2

    dq_state["sclb_gate"] = gD.n
    _dq_iter = dq_steps()

    def dq_pump(n=1):
        for _ in range(n):
            s = next(_dq_iter, None)
            if s is not None:
                s()

    def dq_finish():
        while True:
            s = next(_dq_iter, None)
            if s is None:
                return

    if stop <= 0:
        return _finish()
    ubar = _view(poolA, 0, [KD, TH], BF16)
    u_stage = _view(poolD, 0, [2, TH], F32)
    m_bc = _view(poolD, 16408, [NTE * 128], BF16)
    r_bc = _view(poolD, 16408 + 2 * NTE * 128, [NTE * 128], BF16)
    sqd = _view(poolB, 0, [D], BF16)

    P.wait("sync", vF, vF.n)
    P.wait("scalar", gS, CONST_G)
    smark = [0, 0]
    for j in range(NTE):
        rows = 128 if j < NT else HALO
        pb = j % 2
        if j >= 2:
            P.wait("sync", sF, smark[pb])
        P.dma("sync", syA, u_stage[0:rows, pb, 0:D], u_tm[j * 128:j * 128 + rows, :])
        P.wait("scalar", syA)
        P.op("scalar", lambda e, j=j, rows=rows, pb=pb: e.activation(
            sqd[0:rows, :], u_stage[0:rows, pb, 0:D], AF.Copy, accum_out=stat_s[0:rows, j, :]))
        P.op("scalar", lambda e, j=j, rows=rows, pb=pb: e.activation(
            sqd[0:rows, :], u_stage[0:rows, pb, 0:D], AF.Square, accum_out=stat_q[0:rows, j, :]))
        smark[pb] = P.inc("scalar", sF)
    STATS_S = P.inc("scalar", sS)
    P.wait("vector", sS, STATS_S)
    P.op("vector", lambda e: e.tensor_scalar(stat_s[:], stat_s[:], 1.0 / D, None, A.mult))
    P.op("vector", lambda e: e.tensor_scalar(stat_q[:], stat_q[:], 1.0 / D, None, A.mult))
    P.op("vector", lambda e: e.tensor_tensor(stat_t[:], stat_s[:], stat_s[:], A.mult))
    P.op("vector", lambda e: e.tensor_tensor(stat_q[:], stat_q[:], stat_t[:], A.subtract))
    P.op("vector", lambda e: e.tensor_scalar(stat_q[:], stat_q[:], 1.0, LN_EPS, A.mult, A.add))
    vm = P.inc("vector", vS)
    P.wait("scalar", vS, vm)
    P.op("scalar", lambda e: e.activation(stat_q[:], stat_q[:], AF.Sqrt))
    sm = P.inc("scalar", sS)
    P.wait("vector", sS, sm)
    P.op("vector", lambda e: e.reciprocal(stat_q[:], stat_q[:]))
    P.op("vector", lambda e: e.tensor_copy(stat_sb[:, :, 0:1], stat_s[:]))
    P.op("vector", lambda e: e.tensor_copy(stat_sb[:, :, 1:2], stat_q[:]))
    STB_V = P.inc("vector", vS)
    P.wait("tensor", vS, STB_V)
    P.op("tensor", lambda e: e.transpose(psT[0:2 * NTE, 0:128],
                                         stat_sb.ap().rearrange("p a b -> p (a b)"), ident[:]), drain=False)
    tm = P.inc("tensor", tS)
    P.wait("vector", tS, tm)
    P.op("vector", lambda e: e.tensor_copy(sttT[:], psT[0:2 * NTE, 0:128]))
    vm = P.inc("vector", vS)
    P.wait("gpsimd", vS, vm)
    P.dma("gpsimd", gD, stats_dram2.ap() if False else stats_dram2, sttT[:])
    P.wait("gpsimd", gD, gD.n)
    msrc = stats_dram2[0:2 * NTE:2, :].unsqueeze(0).broadcast_to([128, NTE, 128])
    rsrc = stats_dram2[1:2 * NTE:2, :].unsqueeze(0).broadcast_to([128, NTE, 128])
    P.dma("gpsimd", gD, m_bc.rearrange("p (a b) -> p a b", a=NTE), msrc)
    P.dma("gpsimd", gD, r_bc.rearrange("p (a b) -> p a b", a=NTE), rsrc)
    STATS_G = gD.n

    vmark = [0, 0]
    P.wait("sync", sF, sF.n)
    for k in range(KD):
        pb = k % 2
        if k >= 2:
            P.wait("sync", vF, vmark[pb])
        P.dma("sync", syA, u_stage[:, pb, :], u_fm[k * 128:(k + 1) * 128, :])
        P.wait("vector", syA)
        if k == 0:
            P.wait("vector", gD, STATS_G)
        P.op("vector", lambda e, k=k, pb=pb: e.tensor_tensor(ubar[:, k, :], u_stage[:, pb, :], m_bc[:, 0:TH], A.subtract))
        P.op("vector", lambda e, k=k: e.tensor_tensor(ubar[:, k, :], ubar[:, k, :], r_bc[:, 0:TH], A.mult), drain=False)
        vmark[pb] = P.inc("vector", vF)
    UBAR_V = P.inc("vector", vS)

    if stop <= 1:
        return _finish()
    wzb = _view(poolB, 0, [KD, 2, 512], BF16)
    zev = _view(poolC, 8192, [2, 512], BF16)
    dtraw = _view(poolC, 8192 + 2048, [NCH, NH], F32)

    P.wait("tensor", vS, UBAR_V)
    bank_mark = {0: 0, 1: 0}
    bank_markv = {0: 0, 1: 0}
    wz_sl_tmark = [0, 0]
    zev_mark = [0, 0]
    NCB = ZC // 512 + 1
    for zcb in range(NCB):
        width = 512 if zcb < NCB - 1 else DTC
        sl = zcb % 2
        P.wait("gpsimd", tS, wz_sl_tmark[sl])
        for k in range(KD):
            P.dma("gpsimd", gD, wzb[:, k, sl, 0:width], w_z[k, :, zcb * 512: zcb * 512 + width])
        WZG = gD.n
        dq_pump(2)
        P.wait("tensor", gD, WZG)
        for j in range(NT):
            bi = j % 2
            bank = psA if bi == 0 else psB
            P.wait("tensor", sF, bank_mark[bi])
            P.wait("tensor", vF, bank_markv[bi])
            for k in range(KD):
                P.op("tensor", lambda e, j=j, k=k, sl=sl, width=width, bank=bank: e.matmul(
                    bank[:, 0:width], ubar[:, k, HALO + j * 128: HALO + (j + 1) * 128],
                    wzb[:, k, sl, 0:width], start=(k == 0), stop=(k == KD - 1)), drain=False)
            tm = P.inc("tensor", tS)
            if zcb < NCB - 1:
                pe = j % 2
                P.wait("scalar", tS, tm)
                P.wait("scalar", scD, zev_mark[pe])
                P.op("scalar", lambda e, pe=pe, bank=bank: e.activation(
                    zev[:, pe, 0:512], bank[:, 0:512], AF.Silu))
                bank_mark[bi] = P.inc("scalar", sF)
                P.dma("scalar", scD, z_dram[j * 128:(j + 1) * 128, zcb * 512: zcb * 512 + 512],
                      zev[:, pe, 0:512])
                zev_mark[pe] = scD.n
            else:
                P.wait("vector", tS, tm)
                P.op("vector", lambda e, j=j, bank=bank: e.tensor_copy(dtraw[:, j, :], bank[:, 0:DTC]), drain=False)
                bank_markv[bi] = P.inc("vector", vF)
        wz_sl_tmark[sl] = tS.n
    DT_V = P.inc("vector", vS)

    if stop <= 2:
        return _finish()
    wk_a = _view(poolC, 8192 + 2048 + 4096, [CH], F32)[0:NH, :]
    wk_b = _view(poolC, 8192 + 2048 + 4096 + 512, [CH], F32)[0:NH, :]
    wk_c = _view(poolC, 8192 + 2048 + 4096 + 1024, [CH], F32)[0:NH, :]
    wk_d = _view(poolC, 8192 + 2048 + 4096 + 1536, [CH], F32)[0:NH, :]
    wk_e = _view(poolC, 8192 + 2048 + 4096 + 2048, [CH], F32)[0:NH, :]

    P.wait("tensor", vS, DT_V)
    P.wait("scalar", sS, A_S)
    ps3_mark = {"G": 0, "Y0": 0, "Y1": 0}
    acs2_mark = 0
    acs1_mark = 0
    for c in range(NCH):
        P.wait("vector", gD, acs1_mark)
        P.wait("scalar", gD, acs1_mark)
        P.wait("tensor", vF, ps3_mark["G"])
        P.op("tensor", lambda e, c=c: e.transpose(psG[0:NH, 0:128], dtraw[:, c, :], ident32[:]), drain=False)
        tm = P.inc("tensor", tS)
        P.wait("vector", tS, tm)
        P.op("vector", lambda e: e.tensor_scalar(wk_a[:], psG[0:NH, 0:128], 1.0, dtb_col[:], A.mult, A.add))
        ps3_mark["G"] = P.inc("vector", vF)
        vm = P.inc("vector", vS)
        P.wait("scalar", vS, vm)
        P.op("scalar", lambda e: e.activation(wk_b[:], wk_a[:], AF.Abs))
        P.op("scalar", lambda e: e.activation(wk_b[:], wk_b[:], AF.Exp, scale=-1.0))
        P.op("scalar", lambda e: e.activation(wk_b[:], wk_b[:], AF.Ln, bias=1.0))
        P.op("scalar", lambda e: e.activation(wk_c[:], wk_a[:], AF.Relu))
        sm = P.inc("scalar", sS)
        P.wait("vector", sS, sm)
        P.op("vector", lambda e: e.tensor_add(wk_b[:], wk_b[:], wk_c[:]))
        P.op("vector", lambda e: e.tensor_scalar(wk_a[:], wk_b[:], A_col[:], None, A.mult))
        P.op("vector", lambda e: e.tensor_tensor_scan(wk_c[:], wk_a[:], wk_a[:], 0.0, A.add, A.bypass))
        P.op("vector", lambda e, c=c: e.tensor_copy(cs_h[:, c:c + 1], wk_c[:, 127:128]))
        P.op("vector", lambda e: e.tensor_copy(acs63[:], wk_c[:, 63:64]))
        P.op("vector", lambda e: e.tensor_scalar(wk_d[:], wk_c[:], -1.0, wk_c[:, 127:128], A.mult, A.add))
        vm = P.inc("vector", vS)
        P.wait("scalar", vS, vm)
        P.op("scalar", lambda e: e.activation(wk_d[:], wk_d[:], AF.Exp))
        sm = P.inc("scalar", sS)
        P.wait("vector", sS, sm)
        P.op("vector", lambda e: e.tensor_tensor(wk_d[:], wk_d[:], wk_b[:], A.mult))
        TDS_V = P.inc("vector", vS)
        P.wait("tensor", vS, TDS_V)
        P.wait("tensor", vF, ps3_mark["Y0"])
        P.op("tensor", lambda e: e.transpose(psY0[:, 0:64], wk_d[:], ident32[0:64, 0:64]), drain=False)
        P.op("tensor", lambda e: e.transpose(psY0[:, 64:128], wk_b[:], ident32[0:64, 0:64]), drain=False)
        tm = P.inc("tensor", tS)
        P.wait("vector", tS, tm)
        P.op("vector", lambda e, c=c: e.tensor_copy(ds_lh[:, c, :], psY0[:, 0:64]))
        P.op("vector", lambda e, c=c: e.tensor_copy(dt_lh[:, c, :], psY0[:, 64:128]))
        ps3_mark["Y0"] = P.inc("vector", vF)
        amark = P.inc("vector", vS)
        P.wait("gpsimd", vS, amark)
        acs1_mark = P.dma("gpsimd", gD, acs_dram[c], wk_c[:])
        P.op("vector", lambda e: e.tensor_copy(wk_e[:], wk_c[:]))
        vm = P.inc("vector", vS)
        P.wait("scalar", vS, vm)
        P.op("scalar", lambda e: e.activation(wk_e[:], wk_e[:], AF.Exp))
        sm = P.inc("scalar", sS)
        P.wait("tensor", sS, sm)
        P.wait("tensor", vF, ps3_mark["Y1"])
        P.op("tensor", lambda e: e.transpose(psY1[:, 0:64], wk_e[:], ident32[0:64, 0:64]), drain=False)
        P.op("tensor", lambda e: e.transpose(psY1[:, 64:128], wk_c[:], ident32[0:64, 0:64]), drain=False)
        tm = P.inc("tensor", tS)
        P.wait("vector", tS, tm)
        P.op("vector", lambda e, c=c: e.tensor_copy(e0_lh[:, c, :], psY1[:, 0:64]))
        P.wait("vector", gD, acs2_mark)
        P.op("vector", lambda e: e.tensor_copy(acs_lc[:], psY1[:, 64:128]))
        ps3_mark["Y1"] = P.inc("vector", vF)
        vm = P.inc("vector", vS)
        P.wait("gpsimd", vS, vm)
        acs2_mark = P.dma("gpsimd", gD, acs2_dram[c], acs_lc[:])
    P.op("vector", lambda e: e.tensor_tensor_scan(pgl_h[:], cs_h[:], cs_h[:], 0.0, A.add, A.bypass))
    P.op("vector", lambda e: e.tensor_tensor(pgl_h[:], pgl_h[:], cs_h[:], A.subtract))
    vm = P.inc("vector", vS)
    P.wait("scalar", vS, vm)
    P.op("scalar", lambda e: e.activation(cs_h[:], cs_h[:], AF.Exp))
    P.op("scalar", lambda e: e.activation(pgl_h[:], pgl_h[:], AF.Exp))
    sm = P.inc("scalar", sS)
    P.wait("tensor", sS, sm)
    P.wait("tensor", vF, ps3_mark["Y0"])
    P.op("tensor", lambda e: e.transpose(psY0[:, 0:64][0:16, :] if False else psY0[0:16, 0:64], cs_h[:], ident32[0:64, 0:64]), drain=False)
    P.op("tensor", lambda e: e.transpose(psY0[0:16, 64:128], pgl_h[:], ident32[0:64, 0:64]), drain=False)
    tm = P.inc("tensor", tS)
    P.wait("vector", tS, tm)
    P.op("vector", lambda e: e.tensor_copy(csT[:], psY0[0:16, 0:64]))
    P.op("vector", lambda e: e.tensor_copy(pgT[:], psY0[0:16, 64:128]))
    vm = P.inc("vector", vS)
    P.wait("gpsimd", vS, vm)
    P.dma("gpsimd", gD, vecs_dram[0], csT[:])
    P.dma("gpsimd", gD, vecs_dram[1], pgT[:])
    P.wait("gpsimd", gD, gD.n)
    P.dma("gpsimd", gD, pg_rep.ap().rearrange("p a b -> p (a b)"),
          vecs_dram[1].rearrange("a b -> (a b)").unsqueeze(0).broadcast_to([128, NCH * NH]))
    PG_G = gD.n
    DVEC_V = P.inc("vector", vS)

    if stop <= 3:
        return _finish()
    wxb = _view(poolC, 0, [2, KD, 128], BF16)
    xbcraw = _view(poolC, 8192, [2, TH], BF16)
    xacc = _view(poolC, 8192 + 8208, [2, TH], BF16)
    XsBUF = _view(poolB, 0, [NCH, 512], BF16)
    xtm_st = _view(poolB, NCH * 1024, [2, 128], BF16)
    st_st = _view(poolB, NCH * 1024 + 512, [2, 512], BF16)

    wxmark = [0, 0]
    xacc_mark = [0, 0]
    xbc_mark = [0, 0]
    psT_mark = [0, 0]
    psT_smark = [0, 0]
    xst_mark = [0, 0]
    stst_mark = [0, 0]
    psSt_mark = 0
    P.wait("sync", vF, vF.n)
    P.wait("scalar", scD, scD.n)
    P.wait("scalar", gD, gD.n)
    order = ([32, 33] + list(range(32)))[:xbc_m]
    pending = []
    xacc_tmark = [0, 0]
    wxb_tmark = [0, 0]
    for mi, m in enumerate(order):
        mb = mi % 2
        P.wait("gpsimd", tS, wxb_tmark[mb])
        WXG = P.dma("gpsimd", gD, wxb[:, mb, :, :], w_xbc[m].transpose([1, 0, 2]))
        P.wait("tensor", gD, WXG)
        dq_pump(1)
        ngs = [(0, 512), (512, 512), (1024, 512), (1536, 512), (2048, HALO)]
        for gi, (n0, nn) in enumerate(ngs):
            bi = gi % 2
            bank = psA if bi == 0 else psB
            P.wait("tensor", sF, bank_mark[bi])
            for k in range(KD):
                P.op("tensor", lambda e, k=k, mb=mb, n0=n0, nn=nn, bank=bank: e.matmul(
                    bank[:, 0:nn], wxb[:, mb, k, :], ubar[:, k, n0:n0 + nn],
                    start=(k == 0), stop=(k == KD - 1)), drain=False)
            tm = P.inc("tensor", tS)
            P.wait("scalar", tS, tm)
            if gi == 0:
                P.wait("scalar", vS, xacc_mark[mb])
            P.op("scalar", lambda e, mb=mb, n0=n0, nn=nn, bank=bank: e.copy(
                xbcraw[:, mb, n0:n0 + nn], bank[:, 0:nn]), drain=False)
            bank_mark[bi] = P.inc("scalar", sF)
        wxb_tmark[mb] = tS.n
        XB_S = P.inc("scalar", sS)
        P.wait("vector", sS, XB_S)
        P.wait("vector", tS, xacc_tmark[mb])
        cbb = cb.ap()[:, m:m + 1].broadcast_to([128, T])
        P.op("vector", lambda e, m=m, mb=mb, cbb=cbb: e.scalar_tensor_tensor(
            xacc[:, mb, HALO:TH], xbcraw[:, mb, 0:T], cw.ap()[:, m, 0:1], cbb, A.mult, A.add))
        for tap in range(1, 4):
            P.op("vector", lambda e, m=m, mb=mb, tap=tap: e.scalar_tensor_tensor(
                xacc[:, mb, HALO:TH], xbcraw[:, mb, tap:T + tap], cw.ap()[:, m, tap:tap + 1],
                xacc[:, mb, HALO:TH], A.mult, A.add))
        CV_V = P.inc("vector", vS)
        xacc_mark[mb] = CV_V
        P.wait("scalar", vS, CV_V)
        P.op("scalar", lambda e, mb=mb: e.activation(xacc[:, mb, HALO:TH], xacc[:, mb, HALO:TH], AF.Silu), drain=False)
        SIL_S = P.inc("scalar", sS)
        if m == 32:
            P.wait("vector", sS, SIL_S)
            P.op("vector", lambda e, mb=mb: e.tensor_copy(Bfm[:], xacc[:, mb, HALO:TH]))
            BM_V = P.inc("vector", vS)
            P.wait("tensor", vS, BM_V)
            for c in range(NCH):
                pt = c % 2
                P.wait("tensor", vS, psT_mark[pt])
                P.op("tensor", lambda e, c=c, pt=pt: e.transpose(
                    (psT if pt == 0 else psT2)[:, 0:128], Bfm[:, c * 128:(c + 1) * 128], ident[:]), drain=False)
                tm = P.inc("tensor", tS)
                P.wait("vector", tS, tm)
                P.op("vector", lambda e, c=c, pt=pt: e.tensor_copy(Btm[:, c, :], (psT if pt == 0 else psT2)[:, 0:128]), drain=False)
                psT_mark[pt] = P.inc("vector", vS)
        elif m == 33:
            P.wait("vector", sS, SIL_S)
            P.op("vector", lambda e, mb=mb: e.tensor_copy(Cfm[:], xacc[:, mb, HALO:TH]))
            P.inc("vector", vS)
        else:
            def make_post(m=m, mb=mb, SIL_S=SIL_S):
                def post():
                    nonlocal psSt_mark
                    blk_i = m % 4
                    P.wait("tensor", sS, SIL_S)
                    for c in range(NCH):
                        pt = c % 2
                        P.wait("tensor", vS, psT_mark[pt])
                        P.wait("tensor", sS, psT_smark[pt])
                        P.op("tensor", lambda e, c=c, mb=mb, pt=pt: e.transpose(
                            (psT if pt == 0 else psT2)[:, 0:128],
                            xacc[:, mb, HALO + c * 128: HALO + (c + 1) * 128], ident[:]), drain=False)
                        tm = P.inc("tensor", tS)
                        P.wait("vector", tS, tm)
                        dsap = ds_lh.ap()[:, c, 2 * m: 2 * m + 2].unsqueeze(2).broadcast_to([128, 2, HD])
                        P.op("vector", lambda e, c=c, blk_i=blk_i, pt=pt, dsap=dsap: e.tensor_tensor(
                            XsBUF[:, c, blk_i * 128:(blk_i + 1) * 128].rearrange("p (a b) -> p a b", a=2),
                            (psT if pt == 0 else psT2)[:, 0:128].rearrange("p (a b) -> p a b", a=2), dsap, A.mult), drain=False)
                        psT_mark[pt] = P.inc("vector", vS)
                        P.wait("scalar", tS, tm)
                        P.wait("scalar", vS, psT_mark[pt])
                        P.wait("scalar", gD, xst_mark[pt])
                        P.op("scalar", lambda e, pt=pt: e.copy(xtm_st[:, pt, :], (psT if pt == 0 else psT2)[:, 0:128]), drain=False)
                        psT_smark[pt] = P.inc("scalar", sS)
                        P.wait("gpsimd", sS, psT_smark[pt])
                        P.dma("gpsimd", gD, x_dram[c, :, m * 128:(m + 1) * 128], xtm_st[:, pt, :])
                        xst_mark[pt] = gD.n
                    XS_V = P.inc("vector", vS)
                    xacc_tmark[mb] = tS.n
                    if (m + 1) % 4 == 0:
                        blk0 = (m - 3) * 128
                        P.wait("tensor", vS, XS_V)
                        for c in range(NCH):
                            pt = c % 2
                            P.wait("tensor", vS, psSt_mark)
                            P.op("tensor", lambda e, c=c: e.matmul(
                                psSt[:, :], Btm[:, c, :], XsBUF[:, c, :], start=True, stop=True), drain=False)
                            tm = P.inc("tensor", tS)
                            P.wait("vector", tS, tm)
                            P.wait("vector", gD, stst_mark[pt])
                            P.op("vector", lambda e, pt=pt: e.tensor_copy(st_st[:, pt, :], psSt[:, :]), drain=False)
                            psSt_mark = P.inc("vector", vS)
                            P.wait("gpsimd", vS, psSt_mark)
                            P.dma("gpsimd", gD, st_dram[c, :, blk0:blk0 + 512], st_st[:, pt, :])
                            stst_mark[pt] = gD.n
                return post
            pending.append(make_post())
            if len(pending) > 1:
                pending.pop(0)()
    while pending:
        pending.pop(0)()
    XDONE_V = P.inc("vector", vS)
    if debug:
        P.wait("gpsimd", vS, XDONE_V)
        for i_, tt_ in enumerate([ds_lh, dt_lh, e0_lh]):
            P.dma("gpsimd", gD, dbg_dram[i_], tt_.ap().rearrange("p a b -> p (a b)"))
        P.dma("gpsimd", gD, dbg_dram[5], dtraw.rearrange("p a b -> p (a b)"))

    dq_finish()

    if stop <= 4:
        return _finish()
    S_run = _view(poolA, 0, [DIN], F32)
    st_ld = _view(poolA, DIN * 4, [2, DIN], BF16)
    sin_st = _view(poolA, DIN * 4 + DIN * 4, [2, DIN], BF16)
    etot_rep = _view(poolA, DIN * 4 + DIN * 8, [NCH, NH], BF16)
    etot_flat = _view(poolA, DIN * 4 + DIN * 8, [NCH * NH], BF16)

    P.wait("gpsimd", vS, XDONE_V)
    P.wait("gpsimd", gD, gD.n)
    P.dma("gpsimd", gD, etot_flat,
          vecs_dram[0].rearrange("a b -> (a b)").unsqueeze(0).broadcast_to([128, NCH * NH]))
    ETOT_G = gD.n
    stld_mark = [0, 0]
    P.wait("sync", gD, gD.n)
    P.wait("vector", vS, XDONE_V)
    P.wait("vector", gD, ETOT_G)
    P.op("vector", lambda e: e.memset(S_run, 0.0))
    srun_vmark = P.inc("vector", vF)
    sinst_mark = [0, 0]
    for c in range(NCH):
        pb = c % 2
        if c >= 2:
            P.wait("sync", vF, stld_mark[pb])
        P.dma("sync", syA, st_ld[:, pb, :], st_dram[c])
        P.wait("scalar", vF, srun_vmark)
        P.wait("scalar", scD, sinst_mark[pb])
        P.op("scalar", lambda e, pb=pb: e.copy(sin_st[:, pb, :], S_run))
        sm = P.inc("scalar", sS)
        sinst_mark[pb] = P.dma("scalar", scD, sin_dram[c], sin_st[:, pb, :])
        eap = etot_rep[:, c, :].unsqueeze(2).broadcast_to([128, NH, HD])
        P.wait("vector", sS, sm)
        P.op("vector", lambda e, eap=eap: e.tensor_tensor(
            S_run.rearrange("p (a b) -> p a b", a=NH), S_run.rearrange("p (a b) -> p a b", a=NH), eap, A.mult))
        P.wait("vector", syA)
        P.op("vector", lambda e, pb=pb: e.tensor_tensor(S_run, S_run, st_ld[:, pb, :], A.add))
        stld_mark[pb] = P.inc("vector", vF)
        srun_vmark = stld_mark[pb]
    SRUN_V = P.inc("vector", vS)
    P.wait("gpsimd", vS, SRUN_V)
    P.wait("gpsimd", gD, gD.n)
    P.dma("gpsimd", gD, stateag_in.opt(), S_run)
    P.wait("gpsimd", gD, gD.n)
    P.op("gpsimd", lambda e: e.collective_compute(
        "AllGather", A.bypass, replica_groups=[[0, 1], [2, 3], [4, 5], [6, 7]],
        ins=[stateag_in.opt()], outs=[stateag_out.opt()]), inc=ccS, drain=False)
    CC_ST = ccS.n
    P.wait("gpsimd", dqD, dqD.n)
    P.op("gpsimd", lambda e: e.collective_compute(
        "AllGather", A.bypass, replica_groups=[[0, 1, 2, 3, 4, 5, 6, 7]],
        ins=[gu_ag_in.opt()], outs=[gu_ag_out.opt()]), inc=ccS, drain=False)
    P.op("gpsimd", lambda e: e.collective_compute(
        "AllGather", A.bypass, replica_groups=[[0, 1, 2, 3, 4, 5, 6, 7]],
        ins=[dn_ag_in.opt()], outs=[dn_ag_out.opt()]), inc=ccS, drain=False)
    P.op("gpsimd", lambda e: e.collective_compute(
        "AllGather", A.bypass, replica_groups=[[0, 1, 2, 3, 4, 5, 6, 7]],
        ins=[wo_ag_in.opt()], outs=[wo_ag_out.opt()]), inc=ccS, drain=False)
    CC_W = ccS.n
    P.wait("gpsimd", ccS, CC_ST)
    P.dma("gpsimd", gD, S_ext[:], stateag_out[0])
    P.wait("vector", gD, gD.n)
    P.op("vector", lambda e: e.tensor_scalar(S_ext[:], S_ext[:], isb_col[:], None, A.mult))
    SEXT_V = P.inc("vector", vS)

    if stop <= 5:
        return _finish()
    h_fm = _view(poolA, 0, [KI, GRP], BF16)
    x2_fm = _view(poolA, KI * 1024, [KD, GRP], BF16)
    ygfm = _view(poolB, 0, [KDIN, GRP], BF16)
    y_ffn = _view(poolB, 0, [KD, GRP], BF16)
    zch = _view(poolC, 0, [DIN], BF16)
    xtmch = _view(poolC, DIN * 2, [DIN], BF16)
    outst = _view(poolC, DIN * 4, [2, 1024], F32)
    outst_dbg = _view(poolC, DIN * 4, [D], F32)
    sinch = _view(poolD, 0, [DIN], BF16)
    x1_fm = _view(poolD, 0, [KD, GRP], BF16)
    gext_t = _view(poolD, DIN * 2 * 2, [DIN], BF16)
    u_grp = _view(poolD, DIN * 2 * 2, [2, GRP], F32)
    y_tm = _view(poolF, 0, [DIN], BF16)
    xdblk = _view(poolF, 8192, [GRP], BF16)
    tB_f = _view(poolF, 8192 + 1024, [GRP], F32)
    gm = _view(poolF, 8192 + 1024 + 2048, [128], BF16)
    mh4 = _view(poolF, 8192 + 1024 + 2048 + 256, [2, GRP], BF16)
    ghb = _view(poolF, 8192 + 1024 + 2048 + 256 + 2048, [2, GRP], BF16)
    acs_sv = _view(poolF, 8192 + 1024 + 2048 + 256 + 4096, [NH], F32)
    xdt = _view(poolF, 8192 + 1024 + 2048 + 256 + 4096 + 256, [GRP], BF16)
    acs_b = _view(poolD, 8192, [2, GRP], F32)
    argb = _view(poolD, 12288, [2, GRP], F32)
    wt_s = [_view(poolF, 0, [22, 128], BF16),
            _view(poolF, 5632, [22, 128], BF16)]
    gsil = _view(poolF, KI * 256, [GRP], BF16)
    sq2 = _view(poolF, KI * 256 + 1024, [GRP], BF16)
    srow = _view(poolF, KI * 256 + 2048, [2, GRP], F32)

    wt_slot_mark = [0, 0]
    out_mark = {0: 0, 1: 0}
    sc_g_mark = 0
    P.wait("tensor", vS, SEXT_V)
    P.wait("sync", gD, gD.n)
    P.wait("sync", scD, scD.n)
    P.wait("gpsimd", vS, SEXT_V)
    zch_mark = 0
    xtm_mark = 0
    sin_mark = 0
    acs_s_mark = 0
    accb_mark = [0, 0]
    argb_mark = [0, 0]
    ghb_mark = [0, 0]
    psY_mark = 0
    ytm_mark = 0
    ug_mark = [0, 0]
    wt_mark = 0
    x1_mark = 0
    for g in range(NGRP):
        for ci in range(CPG):
            c = g * CPG + ci
            P.wait("sync", vS, xtm_mark)
            P.dma("sync", syA, xtmch, x_dram[c])
            P.wait("sync", tS, sin_mark)
            P.wait("sync", vS, x1_mark)
            P.dma("sync", syA, sinch, sin_dram[c])
            P.wait("sync", vS, zch_mark)
            P.dma("sync", syA, zch, z_dram[c * 128:(c + 1) * 128, :])
            RELD = syA.n
            P.wait("gpsimd", syA, RELD)
            P.wait("gpsimd", vS, x1_mark)
            pgap = pg_rep.ap()[:, c, :].unsqueeze(2).broadcast_to([128, NH, HD])
            P.op("gpsimd", lambda e, pgap=pgap: e.tensor_tensor(
                gext_t.rearrange("p (a b) -> p a b", a=NH),
                S_ext.ap().rearrange("p (a b) -> p a b", a=NH), pgap, A.mult))
            P.op("gpsimd", lambda e: e.tensor_tensor(sinch, sinch, gext_t, A.add))
            SC_G = P.inc("gpsimd", gS)
            sc_g_mark = SC_G
            P.wait("tensor", gS, SC_G)
            P.wait("tensor", vF, bank_markv[0])
            P.op("tensor", lambda e, c=c: e.matmul(
                psG[:, 0:128], Bfm[:, c * 128:(c + 1) * 128], Cfm[:, c * 128:(c + 1) * 128],
                start=True, stop=True), drain=False)
            tm = P.inc("tensor", tS)
            P.wait("vector", tS, tm)
            P.op("vector", lambda e: e.tensor_tensor(gm, psG[:, 0:128], triu[:], A.mult), drain=False)
            bank_markv[0] = P.inc("vector", vF)
            GM_V = P.inc("vector", vS)
            P.wait("tensor", vS, GM_V)
            P.wait("sync", vS, acs_s_mark)
            P.dma("sync", syA, acs_sv, acs2_dram[c])
            ACSS = syA.n
            P.wait("vector", syA, ACSS)
            for bb in range(DIN // 512):
                h0 = bb * 8
                dDap = D_rep.ap()[:, h0:h0 + 8].unsqueeze(2).broadcast_to([128, 8, HD])
                dtap = dt_lh.ap()[:, c, h0:h0 + 8].unsqueeze(2).broadcast_to([128, 8, HD])
                xsl = xtmch[:, bb * 512:(bb + 1) * 512].rearrange("p (a b) -> p a b", a=8)
                P.op("vector", lambda e, dDap=dDap, xsl=xsl: e.tensor_tensor(
                    xdblk.rearrange("p (a b) -> p a b", a=8), xsl, dDap, A.mult), drain=False)
                P.op("vector", lambda e, dtap=dtap, xsl=xsl: e.tensor_tensor(
                    xdt.rearrange("p (a b) -> p a b", a=8), xsl, dtap, A.mult), drain=False)
                XY_V = P.inc("vector", vS)
                P.wait("tensor", vS, XY_V)
                P.wait("tensor", vS, psY_mark)
                P.op("tensor", lambda e, c=c, bb=bb: e.matmul(
                    psY1[:, :], Cfm[:, c * 128:(c + 1) * 128], sinch[:, bb * 512:(bb + 1) * 512],
                    start=True, stop=True), drain=False)
                P.op("tensor", lambda e: e.matmul(
                    psY0[:, :], ident[:, :], xdblk[:, :], start=True, stop=False), drain=False)
                P.inc("tensor", tS)
                for sub in range(2):
                    hb = 2 * bb + sub
                    slot = hb % 2
                    P.wait("gpsimd", vS, accb_mark[slot])
                    P.dma("gpsimd", gD, acs_b[:, slot, :],
                          acs_dram[c, 4 * hb:4 * hb + 4].rearrange("a b -> (a b)").unsqueeze(0)
                          .broadcast_to([128, GRP]))
                    gdn = gD.n
                    P.wait("vector", gD, gdn)
                    P.wait("vector", sS, argb_mark[slot])
                    a4 = acs_sv[:, 4 * hb:4 * hb + 4].unsqueeze(2).broadcast_to([128, 4, 128])
                    P.op("vector", lambda e, slot=slot, a4=a4: e.tensor_tensor(
                        argb[:, slot, :].rearrange("p (a b) -> p a b", a=4),
                        acs_b[:, slot, :].rearrange("p (a b) -> p a b", a=4), a4, A.subtract))
                    P.op("vector", lambda e, slot=slot: e.tensor_scalar(
                        argb[:, slot, :], argb[:, slot, :], 0.0, None, A.min), drain=False)
                    accb_mark[slot] = P.inc("vector", vF) * 0 + vS.n
                    vm = P.inc("vector", vS)
                    accb_mark[slot] = vm
                    P.wait("scalar", vS, vm)
                    P.op("scalar", lambda e, slot=slot: e.activation(mh4[:, slot, :], argb[:, slot, :], AF.Exp))
                    argb_mark[slot] = P.inc("scalar", sS)
                    P.wait("vector", sS, argb_mark[slot])
                    P.wait("vector", tS, ghb_mark[slot])
                    gmap = gm.unsqueeze(1).broadcast_to([128, 4, 128])
                    P.op("vector", lambda e, slot=slot, gmap=gmap: e.tensor_tensor(
                        ghb[:, slot, :].rearrange("p (a b) -> p a b", a=4),
                        mh4[:, slot, :].rearrange("p (a b) -> p a b", a=4), gmap, A.mult), drain=False)
                    ghv = P.inc("vector", vS)
                    P.wait("tensor", vS, ghv)
                    for hh in range(4):
                        col = (4 * sub + hh) * HD
                        last = (sub == 1 and hh == 3)
                        P.op("tensor", lambda e, slot=slot, hh=hh, col=col, last=last: e.matmul(
                            psY0[:, col:col + HD], ghb[:, slot, hh * 128:(hh + 1) * 128],
                            xdt[:, col:col + HD], start=False, stop=last), drain=False)
                    ghb_mark[slot] = P.inc("tensor", tS)
                tm = tS.n
                P.wait("vector", tS, tm)
                e0ap = e0_lh.ap()[:, c, h0:h0 + 8].unsqueeze(2).broadcast_to([128, 8, HD])
                P.op("vector", lambda e, e0ap=e0ap: e.tensor_tensor(
                    tB_f.rearrange("p (a b) -> p a b", a=8),
                    psY1.ap().rearrange("p (a b) -> p a b", a=8), e0ap, A.mult))
                P.op("vector", lambda e, bb=bb: e.tensor_tensor(
                    y_tm[:, bb * 512:(bb + 1) * 512], psY0.ap()[:, :], tB_f, A.add), drain=False)
                psY_mark = P.inc("vector", vS)
            YV = P.inc("vector", vS)
            xtm_mark = YV
            acs_s_mark = YV
            P.wait("scalar", vS, YV)
            for qq in range(DIN // GRP):
                P.op("scalar", lambda e, qq=qq: e.activation(
                    sq2, y_tm[:, qq * GRP:(qq + 1) * GRP], AF.Square, accum_out=rms8[:, qq, :]))
            SQ_S = P.inc("scalar", sS)
            P.wait("vector", sS, SQ_S)
            P.op("vector", lambda e: e.reduce_sum(rms_c[:], rms8.ap().rearrange("p a b -> p (a b)"),
                                                  axis=mybir.AxisListType.X))
            P.op("vector", lambda e: e.tensor_scalar(rms_c[:], rms_c[:], 1.0 / DIN, RMS_EPS, A.mult, A.add))
            vm = P.inc("vector", vS)
            P.wait("scalar", vS, vm)
            P.op("scalar", lambda e: e.activation(rms_c[:], rms_c[:], AF.Sqrt))
            sm = P.inc("scalar", sS)
            P.wait("vector", sS, sm)
            P.op("vector", lambda e: e.reciprocal(rms_c[:], rms_c[:]))
            P.op("vector", lambda e: e.scalar_tensor_tensor(y_tm[:, :], y_tm[:, :], rms_c[:], nw_rep[:], A.mult, A.mult))
            P.op("vector", lambda e: e.tensor_tensor(y_tm[:, :], y_tm[:, :], zch[:, :], A.mult), drain=False)
            YG_V = P.inc("vector", vS)
            zch_mark = YG_V
            sin_mark = tS.n
            P.wait("tensor", vS, YG_V)
            for m4 in range(KDIN // 4):
                pt = m4 % 2
                bankT = psT if pt == 0 else psT2
                P.wait("tensor", vS, psT_mark[pt])
                P.wait("tensor", sS, psT_smark[pt])
                for q in range(4):
                    mm = m4 * 4 + q
                    P.op("tensor", lambda e, mm=mm, q=q, bankT=bankT: e.transpose(
                        bankT[:, q * 128:(q + 1) * 128], y_tm[:, mm * 128:(mm + 1) * 128], ident[:]),
                        drain=False)
                tm = P.inc("tensor", tS)
                P.wait("scalar", tS, tm)
                P.op("scalar", lambda e, m4=m4, ci=ci, bankT=bankT: e.copy(
                    ygfm[:, m4 * 4:(m4 + 1) * 4, ci * 128:(ci + 1) * 128],
                    bankT[:, 0:512].rearrange("p (a b) -> p a b", a=4)), drain=False)
                psT_smark[pt] = P.inc("scalar", sS)
                psT_mark[pt] = vS.n
        YGT_S = P.inc("scalar", sS)
        ygt_tmark = tS.n
        P.wait("tensor", sS, YGT_S)
        for mm in range(NMO):
            pu = mm % 2
            P.wait("sync", ccS, CC_W)
            P.wait("sync", vF, ug_mark[pu])
            P.wait("sync", gS, sc_g_mark)
            P.dma("sync", syA, u_grp[:, pu, :],
                  u_fm[mm * 128:(mm + 1) * 128, HALO + g * GRP: HALO + (g + 1) * GRP])
            bi = mm % 2
            bank = psA if bi == 0 else psB
            P.wait("tensor", vF, bank_markv[bi])
            for hf in range(2):
                sl = hf
                P.wait("sync", tS, max(wt_slot_mark[sl], ygt_tmark))
                P.wait("sync", vS, x1_mark)
                P.dma("sync", syW, wt_s[sl][:, 0:KDIN // 2, :],
                      wo_ag_out[mm // WO_MT, mm % WO_MT, :, hf * (KDIN // 2):(hf + 1) * (KDIN // 2), :])
                P.wait("tensor", syW)
                for k in range(KDIN // 2):
                    kk = hf * (KDIN // 2) + k
                    P.op("tensor", lambda e, k=k, kk=kk, sl=sl, bank=bank: e.matmul(
                        bank[:, :], wt_s[sl][:, k, :], ygfm[:, kk, :],
                        start=(kk == 0), stop=(kk == KDIN - 1)), drain=False)
                wt_slot_mark[sl] = P.inc("tensor", tS)
            tm = tS.n
            P.wait("vector", tS, tm)
            P.wait("vector", syA)
            P.op("vector", lambda e, mm=mm, pu=pu, bank=bank: e.tensor_tensor(
                x1_fm[:, mm, :], u_grp[:, pu, :], bank[:, :], A.add), drain=False)
            if debug:
                P.wait("vector", gD, gD.n)
                P.op("vector", lambda e, bank=bank: e.tensor_copy(outst_dbg[:, 0:GRP], bank[:, :]))
                vm = P.inc("vector", vS)
                P.wait("gpsimd", vS, vm)
                P.dma("gpsimd", gD, mix_dram[mm * 128:(mm + 1) * 128, g * GRP:(g + 1) * GRP], outst_dbg[:, 0:GRP])
            ug_mark[pu] = P.inc("vector", vF)
            bank_markv[bi] = ug_mark[pu]
        X1_V = P.inc("vector", vS)
        P.wait("tensor", vS, X1_V)
        P.wait("tensor", vS, psY_mark)
        for k in range(KD):
            P.op("tensor", lambda e, k=k: e.matmul(
                psSt[0:1, :], ones_col[:], x1_fm[:, k, :], start=(k == 0), stop=(k == KD - 1)), drain=False)
        tm = P.inc("tensor", tS)
        P.wait("scalar", vS, X1_V)
        for k in range(KD):
            if k > 0:
                P.wait("scalar", tS, sq_tmark)
            P.op("scalar", lambda e, k=k: e.activation(sq2, x1_fm[:, k, :], AF.Square))
            sm = P.inc("scalar", sS)
            P.wait("tensor", sS, sm)
            P.op("tensor", lambda e, k=k: e.matmul(
                psG[0:1, :], ones_col[:], sq2, start=(k == 0), stop=(k == KD - 1)), drain=False)
            sq_tmark = P.inc("tensor", tS)
        tm = sq_tmark
        P.wait("vector", tS, tm)
        P.op("vector", lambda e: e.tensor_scalar(srow[0:1, 0, :], psSt[0:1, :], 1.0 / D, None, A.mult))
        P.op("vector", lambda e: e.tensor_scalar(srow[0:1, 1, :], psG[0:1, :], 1.0 / D, None, A.mult))
        P.op("vector", lambda e: e.tensor_tensor(psY0.ap()[0:1, :], srow[0:1, 0, :], srow[0:1, 0, :], A.mult))
        P.op("vector", lambda e: e.tensor_tensor(srow[0:1, 1, :], srow[0:1, 1, :], psY0.ap()[0:1, :], A.subtract))
        P.op("vector", lambda e: e.tensor_scalar(srow[0:1, 1, :], srow[0:1, 1, :], 1.0, LN_EPS, A.mult, A.add))
        vm = P.inc("vector", vS)
        P.wait("scalar", vS, vm)
        P.op("scalar", lambda e: e.activation(srow[0:1, 1, :], srow[0:1, 1, :], AF.Sqrt))
        sm = P.inc("scalar", sS)
        P.wait("vector", sS, sm)
        P.op("vector", lambda e: e.reciprocal(srow[0:1, 1, :], srow[0:1, 1, :]))
        vm = P.inc("vector", vS)
        P.wait("tensor", vS, vm)
        P.op("tensor", lambda e: e.matmul(psY0[:, :], ones_row32[:], srow[0:1, 0, :], start=True, stop=True), drain=False)
        P.op("tensor", lambda e: e.matmul(psY1[:, :], ones_row32[:], srow[0:1, 1, :], start=True, stop=True), drain=False)
        tm = P.inc("tensor", tS)
        P.wait("vector", tS, tm)
        for k in range(KD):
            P.op("vector", lambda e, k=k: e.tensor_tensor(x2_fm[:, k, :], x1_fm[:, k, :], psY0[:, :], A.subtract))
            P.op("vector", lambda e, k=k: e.tensor_tensor(x2_fm[:, k, :], x2_fm[:, k, :], psY1[:, :], A.mult), drain=False)
        X2_V = P.inc("vector", vS)
        psY_mark = X2_V
        P.wait("tensor", vS, X2_V)
        for j in range(MGU // 2):
            P.wait("tensor", sF, bank_mark[0])
            P.wait("tensor", vF, bank_markv[1])
            for half, jj, bank in ((0, j, psA), (1, j + MGU // 2, psB)):
                sl = half
                P.wait("sync", tS, wt_slot_mark[sl])
                P.dma("sync", syW, wt_s[sl][:, 0:KD, :], gu_ag_out[jj // GU_MT, jj % GU_MT])
                P.wait("tensor", syW)
                for k in range(KD):
                    P.op("tensor", lambda e, k=k, sl=sl, bank=bank: e.matmul(
                        bank[:, :], wt_s[sl][:, k, :], x2_fm[:, k, :],
                        start=(k == 0), stop=(k == KD - 1)), drain=False)
                wt_slot_mark[sl] = P.inc("tensor", tS)
            tm = tS.n
            P.wait("scalar", tS, tm)
            P.op("scalar", lambda e: e.activation(gsil, psA[:, :], AF.Silu))
            bank_mark[0] = P.inc("scalar", sF)
            sm = P.inc("scalar", sS)
            P.wait("vector", sS, sm)
            P.op("vector", lambda e, j=j: e.tensor_tensor(h_fm[:, j, :], gsil, psB[:, :], A.mult), drain=False)
            bank_markv[1] = P.inc("vector", vF)
        H_V = P.inc("vector", vS)
        P.wait("tensor", vS, H_V)
        for mm in range(NMO):
            bi = mm % 2
            bank = psA if bi == 0 else psB
            P.wait("tensor", vF, bank_markv[bi])
            for hf in range(2):
                sl = hf
                k0 = hf * (KI // 2)
                kn = (KI // 2) if hf == 0 else (KI - KI // 2)
                P.wait("sync", tS, wt_slot_mark[sl])
                P.dma("sync", syW, wt_s[sl][:, 0:kn, :],
                      dn_ag_out[mm // DN_MT, mm % DN_MT, :, k0:k0 + kn, :])
                P.wait("tensor", syW)
                for k in range(kn):
                    kk = k0 + k
                    P.op("tensor", lambda e, k=k, kk=kk, sl=sl, bank=bank: e.matmul(
                        bank[:, :], wt_s[sl][:, k, :], h_fm[:, kk, :],
                        start=(kk == 0), stop=(kk == KI - 1)), drain=False)
                wt_slot_mark[sl] = P.inc("tensor", tS)
            tm = tS.n
            P.wait("vector", tS, tm)
            P.op("vector", lambda e, mm=mm, bank=bank: e.tensor_copy(y_ffn[:, mm, :], bank[:, :]), drain=False)
            bank_markv[bi] = P.inc("vector", vF)
        YF_V = P.inc("vector", vS)
        P.wait("scalar", vS, YF_V)
        for k in range(KD):
            if k > 0:
                P.wait("scalar", tS, sq_tmark)
            P.op("scalar", lambda e, k=k: e.activation(sq2, y_ffn[:, k, :], AF.Square))
            sm = P.inc("scalar", sS)
            P.wait("tensor", sS, sm)
            P.op("tensor", lambda e, k=k: e.matmul(
                psSt[0:1, :], ones_col[:], sq2, start=(k == 0), stop=(k == KD - 1)), drain=False)
            sq_tmark = P.inc("tensor", tS)
        tm = sq_tmark
        P.wait("vector", tS, tm)
        P.op("vector", lambda e: e.tensor_scalar(srow[0:1, 0, :], psSt[0:1, :], 1.0 / D, FFN_EPS, A.mult, A.add))
        vm = P.inc("vector", vS)
        P.wait("scalar", vS, vm)
        P.op("scalar", lambda e: e.activation(srow[0:1, 0, :], srow[0:1, 0, :], AF.Sqrt))
        sm = P.inc("scalar", sS)
        P.wait("vector", sS, sm)
        P.op("vector", lambda e: e.reciprocal(srow[0:1, 0, :], srow[0:1, 0, :]))
        vm = P.inc("vector", vS)
        P.wait("tensor", vS, vm)
        P.op("tensor", lambda e: e.matmul(psY0[:, :], ones_row32[:], srow[0:1, 0, :], start=True, stop=True), drain=False)
        tm = P.inc("tensor", tS)
        P.wait("vector", tS, tm)
        for k in range(KD):
            P.op("vector", lambda e, k=k: e.tensor_tensor(y_ffn[:, k, :], y_ffn[:, k, :], psY0[:, :], A.mult))
            P.op("vector", lambda e, k=k: e.tensor_tensor(y_ffn[:, k, :], y_ffn[:, k, :], x1_fm[:, k, :], A.add), drain=False)
        S_V = P.inc("vector", vS)
        psY_mark = S_V
        x1_mark = S_V
        sin_mark = S_V
        if debug:
            P.wait("vector", gD, gD.n)
        P.wait("tensor", vS, S_V)
        for tt in range(CPG):
            for half in range(2):
                o_eng, o_sem = ("scalar", scD) if half == 0 else ("sync", syA)
                P.wait("vector", o_sem, out_mark[half])
                for q in range(8):
                    mm = half * 8 + q
                    pt = mm % 2
                    P.wait("tensor", vS, psT_mark[pt])
                    P.op("tensor", lambda e, tt=tt, mm=mm, pt=pt: e.transpose(
                        (psT if pt == 0 else psT2)[:, 0:128], y_ffn[:, mm, tt * 128:(tt + 1) * 128], ident[:]),
                        drain=False)
                    tm = P.inc("tensor", tS)
                    P.wait("vector", tS, tm)
                    P.op("vector", lambda e, q=q, half=half, pt=pt: e.tensor_copy(
                        outst[:, half, q * 128:(q + 1) * 128], (psT if pt == 0 else psT2)[:, 0:128]), drain=False)
                    psT_mark[pt] = P.inc("vector", vS)
                OW_V = P.inc("vector", vS)
                P.wait(o_eng, vS, OW_V)
                out_mark[half] = P.dma(o_eng, o_sem,
                                       out_t[g * GRP + tt * 128: g * GRP + (tt + 1) * 128,
                                             half * 1024:(half + 1) * 1024],
                                       outst[:, half, :])

    return _finish()


def host_prepare(inputs):
    f32 = np.float32
    u = np.asarray(inputs["L0_attn_u"], f32)
    W_in = np.asarray(inputs["L0_attn_W_in"], f32)
    conv_w = np.asarray(inputs["L0_attn_conv_w"], f32)
    conv_b = np.asarray(inputs["L0_attn_conv_b"], f32)
    A_log = np.asarray(inputs["L0_attn_A_log"], f32)
    Dv = np.asarray(inputs["L0_attn_D"], f32)
    dt_bias = np.asarray(inputs["L0_attn_dt_bias"], f32)
    norm_w = np.asarray(inputs["L0_attn_norm_w"], f32)
    W_out = np.asarray(inputs["L0_attn_W_out"], f32)
    gup = np.asarray(inputs["L0_ffn_gate_up_packed"], np.int32)
    gus = np.asarray(inputs["L0_ffn_gate_up_scales"], np.float16)
    dnp = np.asarray(inputs["L0_ffn_down_packed"], np.int32)
    dns = np.asarray(inputs["L0_ffn_down_scales"], np.float16)

    WT = W_in.T
    w_z = np.ascontiguousarray(
        np.concatenate([WT[:, 0:ZC], WT[:, ZC + CONVD:]], axis=1)).reshape(KD, 128, ZC + DTC)
    w_xbc = np.ascontiguousarray(
        WT[:, ZC:ZC + CONVD].reshape(KD, 128, MXBC, 128).transpose(2, 0, 1, 3))
    wo_t = np.ascontiguousarray(W_out.T.reshape(KDIN, 128, NMO, 128).transpose(2, 0, 1, 3))
    gu16 = np.ascontiguousarray(gup.view(np.uint16).reshape(2 * I5, D // 4).T)
    gu16 = gu16.reshape(KD, 32, 2 * I5)
    dn16 = np.ascontiguousarray(dnp.view(np.uint16).reshape(D, I5 // 4).T)
    dn16 = dn16.reshape(KI, 32, D)
    cwp = np.ascontiguousarray(conv_w.reshape(MXBC, 128, 4).transpose(1, 0, 2))
    cbp = np.ascontiguousarray(conv_b.reshape(MXBC, 128).T)
    nwrep = np.ascontiguousarray(np.broadcast_to(norm_w[None, :], (128, DIN)))

    in_maps = []
    for c in range(8):
        b, h = c // 2, c % 2
        t0 = h * T
        if h == 0:
            u_ext = np.concatenate([np.zeros((HALO, D), f32), u[b, 0:T]], axis=0)
        else:
            u_ext = u[b, t0 - HALO:t0 + T]
        u_ext = np.ascontiguousarray(u_ext)
        in_maps.append({
            "u_fm": np.ascontiguousarray(u_ext.T),
            "u_tm": u_ext,
            "w_z": w_z,
            "w_xbc": w_xbc,
            "w_out_sh": np.ascontiguousarray(wo_t[2 * c:2 * c + 2]),
            "gu_w16": np.ascontiguousarray(gu16[:, :, c * GU_SH:(c + 1) * GU_SH]).view(np.int16),
            "gu_scl": np.ascontiguousarray(np.asarray(gus)[:, c * GU_SH:(c + 1) * GU_SH]),
            "dn_w16": np.ascontiguousarray(dn16[:, :, c * DN_SH:(c + 1) * DN_SH]).view(np.int16),
            "dn_scl": np.ascontiguousarray(np.asarray(dns)[:, c * DN_SH:(c + 1) * DN_SH]),
            "conv_w": cwp,
            "conv_b": cbp,
            "a_log": np.ascontiguousarray(A_log[:, None]),
            "dt_bias": np.ascontiguousarray(dt_bias[:, None]),
            "d_col_i": np.ascontiguousarray(Dv[:, None]),
            "d_row_i": np.ascontiguousarray(Dv[None, :]),
            "nw_rep_i": nwrep,
            "is_b": np.array([[float(h)]], f32),
        })
    return in_maps


_NC_CACHE = {}


def kernel(**inputs):
    in_maps = host_prepare(inputs)
    if "nc" not in _NC_CACHE:
        _NC_CACHE["nc"] = build(debug=0)
    nc = _NC_CACHE["nc"]
    res = run_bass_kernel_spmd(nc, in_maps, core_ids=list(range(8)))
    out = np.empty((4, 2 * T, D), np.float32)
    for c in range(8):
        b, h = c // 2, c % 2
        out[b, h * T:(h + 1) * T, :] = res.results[c]["out"]
    return out
